# version 12
# baseline (speedup 1.0000x reference)
import numpy as np
import concourse.bass as bass
import concourse.mybir as mybir
from concourse.bass_utils import run_bass_kernel_spmd

F32 = mybir.dt.float32
ALU = mybir.AluOpType
ACT = mybir.ActivationFunctionType
AX = mybir.AxisListType

NCORE = 8
T = 16384
TC = 2048
C = 1024
NCC = 8
NT = 16
H4 = 4096
NEC = 8
E2 = 2048
EPS = 1e-5
CLAMP = 80.0
NEG = -1e38


class Bld:
    def __init__(self, nc, block):
        self.nc = nc
        self.block = block
        self.cnt = {}

    def sem(self, name):
        self.cnt[name] = [self.nc.alloc_semaphore(name), 0]
        return name

    def inc(self, name, ins, k=1):
        ins.then_inc(self.cnt[name][0], k)
        self.cnt[name][1] += k
        return ins

    def n(self, name):
        return self.cnt[name][1]

    def wait(self, eng, name, val=None):
        h, v = self.cnt[name]
        eng.wait_ge(h, v if val is None else val)


class AutoDrain:
    _PASS = {"wait_ge", "drain", "engine_nop", "sem_inc", "nop", "wait_op"}

    def __init__(self, eng, state):
        self._e = eng
        self._st = state

    @staticmethod
    def _span(ap):
        try:
            name = ap.name
            name = name() if callable(name) else name
        except Exception:
            return None
        try:
            off = int(ap.offset)
            reach = 1
            for st, ct in ap.ap:
                reach += abs(int(st)) * (int(ct) - 1)
            return (str(name), off, off + reach)
        except Exception:
            return (str(name), 0, 1 << 60)

    @staticmethod
    def _hits(spans, lst):
        return any(s[0] == wn and s[1] < whi and wlo < s[2]
                   for s in spans for (wn, wlo, whi) in lst)

    def __getattr__(self, nm):
        fn = getattr(self._e, nm)
        if nm in self._PASS or not callable(fn):
            return fn

        def wrap(*a, **k):
            aps = [x for x in a if isinstance(x, bass.AP)]
            aps += [x for x in k.values() if isinstance(x, bass.AP)]
            outs, ins = [], []
            if aps and nm != "dma_start":
                outs = [self._span(aps[0])]
                ac = k.get("accum_out")
                if isinstance(ac, bass.AP):
                    outs.append(self._span(ac))
                ins = [self._span(x) for x in aps[1:] if not isinstance(x, type(None))]
            else:
                ins = [self._span(x) for x in aps]
            outs = [s for s in outs if s is not None]
            ins = [s for s in ins if s is not None]
            need = (self._hits(outs, self._st["r"]) or
                    self._hits(outs, self._st["w"]) or
                    self._hits(ins, self._st["w"]))
            if need:
                self._e.drain()
                self._st["w"] = []
                self._st["r"] = []
            res = fn(*a, **k)
            self._st["w"].extend(outs)
            self._st["r"].extend(ins)
            return res
        return wrap


def build_nc(dbg=()):
    dbg = set(dbg)
    nc = bass.Bass("TRN2", target_bir_lowering=False,
                   dynamic_dma_scratch_size=4096)

    def di(name, shape):
        return nc.dram_tensor(name, list(shape), F32, kind="ExternalInput").ap()

    x_sl = di("x_sl", (TC + 1, C))
    wvT = di("wvT", (C, C))
    wkT = di("wkT", (C, C))
    wrT = di("wrT", (C, C))
    woT = di("woT", (C, C))
    fwkT = di("fwkT", (C, H4))
    fwvT = di("fwvT", (H4, C))
    fwrT = di("fwrT", (C, C))
    phiT_d = di("phiT", (C, E2))
    phib_d = di("phib", (128, E2))
    expw = di("expw", (NEC, C, C))
    biasq_d = di("biasq", (128, 2 * C))
    md_d = di("md", (64, E2))
    mc_d = di("mc", (32, E2))
    iden = di("iden", (128, 128))
    pn = ["ln1g", "ln1b", "ln2g", "ln2b", "tmk", "tmv", "tmr", "cmk", "cmr",
          "decay", "u8"]
    p_in = {nm: di(nm, (128, NCC)) for nm in pn}
    mk_nm = ["maskS", "maskP", "m2S", "m2P", "m3S", "m3P"]
    mk_in = {nm: di(nm, (128, NCORE)) for nm in mk_nm}
    c0m = di("c0m", (128, 1))

    o_out = nc.dram_tensor("o_out", [TC, C], F32, kind="ExternalOutput").ap()

    z0sp = nc.dram_tensor("z0sp", [128, NCC * TC], F32).ap()
    ksp = nc.dram_tensor("ksp", [128, NCC * TC], F32).ap()
    vsp = nc.dram_tensor("vsp", [128, NCC * TC], F32).ap()
    rsp = nc.dram_tensor("rsp", [128, NCC * TC], F32).ap()
    xasp = nc.dram_tensor("xasp", [128, NCC * TC], F32).ap()
    sgsp = nc.dram_tensor("sgsp", [128, NCC * TC], F32).ap()
    x2sp = nc.dram_tensor("x2sp", [128, NT * C], F32).ap()
    dsp = nc.dram_tensor("dsp", [TC, E2], F32).ap()
    csp = nc.dram_tensor("csp", [E2, TC], F32).ap()
    ag1i = nc.dram_tensor("ag1i", [128, 72], F32).ap()
    ag1o = nc.dram_tensor("ag1o", [128 * NCORE, 72], F32).ap()
    rsi = nc.dram_tensor("rsi", [E2, C], F32).ap()
    rso = nc.dram_tensor("rso", [E2 // NCORE, C], F32).ap()
    ag2i = nc.dram_tensor("ag2i", [E2 // NCORE, C], F32).ap()
    ag2o = nc.dram_tensor("ag2o", [E2, C], F32).ap()

    B1 = nc.alloc_sbuf_tensor("B1", [128, 16384], F32).ap()
    B2 = nc.alloc_sbuf_tensor("B2", [128, 16384], F32).ap()
    B3 = nc.alloc_sbuf_tensor("B3", [128, 16384], F32).ap()
    S0 = nc.alloc_sbuf_tensor("S0", [128, 2048], F32).ap()
    S1 = nc.alloc_sbuf_tensor("S1", [128, 2048], F32).ap()
    W0 = nc.alloc_sbuf_tensor("W0", [128, 1024], F32).ap()
    W1 = nc.alloc_sbuf_tensor("W1", [128, 1024], F32).ap()

    sm = {}

    def asm(name, shape):
        sm[name] = nc.alloc_sbuf_tensor("s_" + name, list(shape), F32).ap()
        return sm[name]

    IDN = asm("IDN", (128, 128))
    prm = {nm: asm("p_" + nm, (128, NCC)) for nm in pn}
    w8 = asm("w8", (128, NCC))
    tw8 = asm("tw8", (128, NCC))
    tw47 = asm("tw47", (128, NCC))
    MK = {nm: asm("m_" + nm, (128, NCORE)) for nm in mk_nm}
    C0M = asm("C0M", (128, 1))
    ONESC = asm("ONESC", (128, 1))
    z0b = asm("z0b", (128, NCC))
    xab = asm("xab", (128, NCC))
    xcb = asm("xcb", (128, NCC))
    stat = asm("stat", (128, 96))
    stat1 = asm("stat1", (1, 24))
    sS = {nm: asm(nm, (128, NCC)) for nm in
          ["PLe", "ALe", "BLe", "PLp", "ALp", "BLp", "KL", "VL", "RL"]}
    ppin = asm("ppin", (128, NCC))
    aain = asm("aain", (128, NCC))
    bbin = asm("bbin", (128, NCC))
    f = {nm: asm(nm, (128, NCC)) for nm in
         ["f1", "f2", "f3", "f4", "f5", "f6", "f7", "fA", "fB", "fP",
          "bk8", "bv8", "br8", "bm2x"]}

    PA = nc.alloc_psum_tensor("PA", [128, 1024], F32).ap()
    PB = nc.alloc_psum_tensor("PB", [128, 1024], F32).ap()
    PC = nc.alloc_psum_tensor("PC", [128, 512], F32).ap()
    PD = nc.alloc_psum_tensor("PD", [128, 512], F32).ap()
    PE_ = nc.alloc_psum_tensor("PE_", [128, 512], F32).ap()
    PF = nc.alloc_psum_tensor("PF", [128, 512], F32).ap()

    def sl(buf, i, w=2048):
        return buf[:, i * w:(i + 1) * w]

    def tmt(buf, tt):
        return buf[:, tt * 1024:(tt + 1) * 1024]

    VBUF = B1[:, 0:2048]
    R_W = B1[:, 2048:4096]
    R_PL = B1[:, 4096:6145]
    R_AL = B1[:, 6145:8194]
    R_BL = B1[:, 8194:10243]
    R_T2 = B1[:, 10243:12291]
    R_T3 = B1[:, 12291:14339]
    ROWB = W1[0:1, 0:1024]
    PHIB = B3[:, 0:2048]
    MDs = B3[0:64, 2048:4096]
    MCs = B3[0:32, 4096:6144]
    CSTG = B3[:, 6144:8192]
    XT = B3[:, 8192:10240]
    XTt = B3[:, 10240:12288]
    YT = B3[:, 12288:14336]
    BQ = B3[:, 14336:16384]
    agbuf = B3[:, 8192:8768].rearrange("p (r f) -> p r f", f=72)
    rDT = asm("rDT", (64, 128))
    rCT = asm("rCT", (32, 128))
    sD = stat[:, 0:64]
    sCt = stat[:, 64:96]
    ONESR = S0[0:1, 1024:1152]

    d_dbg = {}

    def mktap(name, shape):
        d_dbg[name] = nc.dram_tensor("dbg_" + name, list(shape), F32,
                                     kind="ExternalOutput").ap()
        return d_dbg[name]

    with nc.Block() as block:
        b = Bld(nc, block)

        _wve = {"w": [], "r": []}
        _wsc = {"w": [], "r": []}

        def bvec(fn):
            block.vector(lambda e: fn(AutoDrain(e, _wve)))

        def bscal(fn):
            block.scalar(lambda e: fn(AutoDrain(e, _wsc)))
        sLD = b.sem("sLD")
        sSP = b.sem("sSP")
        sTE = b.sem("sTE")
        sVE = b.sem("sVE")
        sSC = b.sem("sSC")
        sGP = b.sem("sGP")
        sCC = b.sem("sCC")
        sW0d = b.sem("sW0d")
        sW1d = b.sem("sW1d")
        sRG0 = b.sem("sRG0")
        sRG1 = b.sem("sRG1")
        sZR0 = b.sem("sZR0")
        sZR1 = b.sem("sZR1")
        sZR2 = b.sem("sZR2")
        sZR3 = b.sem("sZR3")
        sCW = b.sem("sCW")
        sSPb = b.sem("sSPb")
        sSP0 = b.sem("sSP0")
        sSP1 = b.sem("sSP1")
        sSPa = b.sem("sSPa")
        sSPc3 = b.sem("sSPc3")
        sSPw0 = b.sem("sSPw0")
        sSPw1 = b.sem("sSPw1")
        sSPo = b.sem("sSPo")
        sVs0 = b.sem("sVs0"); sVs1 = b.sem("sVs1")
        sKs0 = b.sem("sKs0"); sKs1 = b.sem("sKs1")
        sRs0 = b.sem("sRs0"); sRs1 = b.sem("sRs1")
        sDsp = b.sem("sDsp")
        sCs0 = b.sem("sCs0"); sCs1 = b.sem("sCs1")
        sSPt = b.sem("sSPt")
        marks = {}

        def gate(eng, key):
            if key in marks:
                nm, v = marks[key]
                b.wait(eng, nm, v)

        def setg(key, nm):
            marks[key] = (nm, b.n(nm))

        def tapwait(eng):
            if b.n(sSPt):
                b.wait(eng, sSPt, b.n(sSPt))

        def emit_ln(get_tile, ntile, out_tile=None, wait_list=()):
            P0 = get_tile(0).shape[0]
            if P0 != 128:
                _emit_ln_one(get_tile(0), get_tile(0) if out_tile is None
                             else out_tile(0), wait_list)
                return
            for i in range(ntile):
                tile = get_tile(i)
                F = tile.shape[-1]
                c = i * 6

                def v1(v, tile=tile, F=F, c=c, i=i, wl=tuple(wait_list)):
                    if i == 0:
                        for (nm, val) in wl:
                            b.wait(v, nm, val)
                        gate(v, ("w", id(W0)))
                    v.reduce_sum(stat[:, c:c + 1], tile, AX.X)
                    v.scalar_tensor_tensor(W0[:, 0:F], tile, 1.0, tile,
                                           ALU.mult, ALU.mult,
                                           accum_out=stat[:, c + 1:c + 2])
                    v.tensor_scalar_mul(stat[:, c + 2:c + 3], stat[:, c:c + 1], 1.0 / F)
                    v.tensor_scalar_mul(stat[:, c + 3:c + 4], stat[:, c + 1:c + 2], 1.0 / F)
                    v.scalar_tensor_tensor(stat[:, c + 4:c + 5], stat[:, c + 2:c + 3],
                                           stat[:, c + 2:c + 3], stat[:, c + 3:c + 4],
                                           ALU.mult, ALU.subtract)
                    ins = v.tensor_scalar(stat[:, c + 4:c + 5], stat[:, c + 4:c + 5],
                                          -1.0, EPS, ALU.mult, ALU.add)
                    if i == ntile - 1:
                        b.inc(sVE, ins)
                bvec(v1)
            setg(("w", id(W0)), sVE)
            nve = b.n(sVE)

            def s2(s, nve=nve, ntile=ntile):
                b.wait(s, sVE, nve)
                for i in range(ntile):
                    c = i * 6
                    ins = s.activation(stat[:, c + 5:c + 6], stat[:, c + 4:c + 5],
                                       ACT.Sqrt)
                b.inc(sSC, ins)
            bscal(s2)
            nsc = b.n(sSC)

            for i in range(ntile):
                tile = get_tile(i)
                dst = tile if out_tile is None else out_tile(i)
                c = i * 6

                def v3(v, tile=tile, dst=dst, c=c, i=i, nsc=nsc):
                    if i == 0:
                        b.wait(v, sSC, nsc)
                    v.reciprocal(stat[:, c:c + 1], stat[:, c + 5:c + 6])
                    ins = v.tensor_scalar(dst, tile, stat[:, c + 2:c + 3],
                                          stat[:, c:c + 1], ALU.subtract, ALU.mult)
                    if i == ntile - 1:
                        b.inc(sVE, ins)
                bvec(v3)

        def _emit_ln_one(tile, dst, wait_list):
            F = tile.shape[-1]
            P = tile.shape[0]
            st = stat1

            def v1(v, wl=tuple(wait_list)):
                for (nm, val) in wl:
                    b.wait(v, nm, val)
                gate(v, ("w", id(W0)))
                v.reduce_sum(st[:, 0:1], tile, AX.X)
                v.scalar_tensor_tensor(W0[0:P, 0:F], tile, 1.0, tile,
                                       ALU.mult, ALU.mult, accum_out=st[:, 1:2])
                v.tensor_scalar_mul(st[:, 2:3], st[:, 0:1], 1.0 / F)
                v.tensor_scalar_mul(st[:, 3:4], st[:, 1:2], 1.0 / F)
                v.scalar_tensor_tensor(st[:, 4:5], st[:, 2:3], st[:, 2:3],
                                       st[:, 3:4], ALU.mult, ALU.subtract)
                b.inc(sVE, v.tensor_scalar(st[:, 4:5], st[:, 4:5], -1.0, EPS,
                                           ALU.mult, ALU.add))
            bvec(v1)
            setg(("w", id(W0)), sVE)
            nve1 = b.n(sVE)

            def s2(s, nve1=nve1):
                b.wait(s, sVE, nve1)
                b.inc(sSC, s.activation(st[:, 5:6], st[:, 4:5], ACT.Sqrt))
            bscal(s2)
            nsc2 = b.n(sSC)

            def v3(v, nsc2=nsc2):
                b.wait(v, sSC, nsc2)
                v.reciprocal(st[:, 6:7], st[:, 5:6])
                b.inc(sVE, v.tensor_scalar(dst, tile, st[:, 2:3],
                                           st[:, 6:7], ALU.subtract, ALU.mult))
            bvec(v3)

        tps = [0]

        def emit_tp(srcs, dsts512, wait_list=(), banks=None):
            bset = banks if banks is not None else (PE_, PF)
            nb = (len(srcs) + 3) // 4
            assert len(dsts512) == nb
            for bb_ in range(nb):
                bank = bset[tps[0] % len(bset)]
                tps[0] += 1
                lo, hi = bb_ * 4, min(bb_ * 4 + 4, len(srcs))

                def t1(t, lo=lo, hi=hi, bank=bank, wl=tuple(wait_list)):
                    for (nm, v) in wl:
                        b.wait(t, nm, v)
                    gate(t, ("bank", id(bank)))
                    for j in range(lo, hi):
                        ins = t.transpose(bank[:, (j - lo) * 128:(j - lo + 1) * 128],
                                          srcs[j], IDN)
                    b.inc(sTE, ins)
                block.tensor(t1)
                nte = b.n(sTE)

                def s1(s, bank=bank, bb_=bb_, nte=nte, lo=lo, hi=hi):
                    b.wait(s, sTE, nte)
                    b.inc(sSC, s.copy(dsts512[bb_][:, 0:(hi - lo) * 128],
                                      bank[:, 0:(hi - lo) * 128]))
                bscal(s1)
                setg(("bank", id(bank)), sSC)

        def p0_loads(sync):
            for nm in pn:
                b.inc(sLD, sync.dma_start(out=prm[nm], in_=p_in[nm]), 16)
            for nm in mk_nm:
                b.inc(sLD, sync.dma_start(out=MK[nm], in_=mk_in[nm]), 16)
            b.inc(sLD, sync.dma_start(out=IDN, in_=iden), 16)
            b.inc(sLD, sync.dma_start(out=C0M, in_=c0m), 16)
            b.inc(sLD, sync.dma_start(
                out=B1.rearrange("p (a f) -> p a f", f=1024),
                in_=x_sl[1:TC + 1, :].rearrange("(a p) f -> p a f", p=128)), 16)
            b.inc(sLD, sync.dma_start(out=ROWB, in_=x_sl[0:1, :]), 16)
        block.sync(p0_loads)
        n_ld0 = b.n(sLD)

        def p0_ones(v):
            b.wait(v, sLD, n_ld0)
            b.inc(sVE, v.memset(ONESC, 1.0))
        bvec(p0_ones)

        def p0_w(s):
            b.wait(s, sLD, n_ld0)
            b.inc(sSC, s.activation(w8, prm["decay"], ACT.Exp))
        bscal(p0_w)
        n_w = b.n(sSC)

        def p0_w2(v):
            b.wait(v, sSC, n_w)
            v.tensor_scalar_mul(w8, w8, -1.0)
            v.tensor_scalar_mul(tw8, w8, float(TC))
            b.inc(sVE, v.tensor_scalar_mul(tw47, w8, float(TC - 1)))
        bvec(p0_w2)

        emit_ln(lambda i: tmt(B1, i), NT)
        emit_ln(lambda i: ROWB, 1)
        setg(("w", id(W1)), sVE)
        n_z0 = b.n(sVE)

        for i in range(NCC):
            srcs = [tmt(B1, tt)[:, i * 128:(i + 1) * 128] for tt in range(NT)]
            dsts = [B2[:, i * 2048 + q * 512: i * 2048 + (q + 1) * 512]
                    for q in range(4)]
            emit_tp(srcs, dsts, wait_list=[(sVE, n_z0)], banks=(PC, PD, PE_, PF))
        n_z0fm = b.n(sSC)

        def p0_bt(t):
            b.wait(t, sVE, n_z0)
            gate(t, ("bank", id(PE_)))
            for i in range(NCC):
                ins = t.transpose(PE_[:, i:i + 1], ROWB[:, i * 128:(i + 1) * 128],
                                  IDN[0:1, 0:1])
            b.inc(sTE, ins)
        block.tensor(p0_bt)
        n_bt = b.n(sTE)

        def p0_bt2(s):
            b.wait(s, sTE, n_bt)
            b.inc(sSC, s.copy(z0b, PE_[:, 0:NCC]))
        bscal(p0_bt2)
        setg(("bank", id(PE_)), sSC)
        setg(("w", id(W1)), sTE)
        n_z0b = b.n(sSC)

        def p0_spill(s):
            b.wait(s, sSC, n_z0fm)
            b.inc(sSPb, s.dma_start(out=z0sp, in_=B2), 16)
        bscal(p0_spill)
        n_z0sp = b.n(sSPb)

        if "z0fm" in dbg:
            tap = mktap("z0fm", (128, 16384))

            def dt0(s, tap=tap):
                b.wait(s, sSC, n_z0fm)
                b.inc(sSPt, s.dma_start(out=tap, in_=B2), 16)
            bscal(dt0)
            setg(("tap", id(B2)), sSPt)

        def p0_xa(v):
            b.wait(v, sSC, n_z0fm)
            for i in range(NCC):
                v.tensor_scalar(sl(B3, i), sl(B2, i), prm["ln1g"][:, i:i + 1],
                                prm["ln1b"][:, i:i + 1], ALU.mult, ALU.add)
            b.wait(v, sSC, n_z0b)
            v.tensor_mul(xab, z0b, prm["ln1g"])
            v.tensor_add(xab, xab, prm["ln1b"])
            b.inc(sVE, v.tensor_scalar_mul(xab, xab, C0M[:, 0:1]))
        bvec(p0_xa)
        n_xa = b.n(sVE)

        def pa_mix(v):
            b.wait(v, sVE, n_xa)
            b.wait(v, sSPb, n_z0sp)
            tapwait(v)
            for i in range(NCC):
                xa_i = sl(B3, i)
                v.tensor_sub(S0[:, 1:2048], xa_i[:, 1:2048], xa_i[:, 0:2047])
                v.tensor_sub(S0[:, 0:1], xa_i[:, 0:1], xab[:, i:i + 1])
                for (buf, tnm) in ((B1, "tmv"), (B2, "tmr"), (B3, "tmk")):
                    dst = sl(buf, i)
                    v.scalar_tensor_tensor(dst[:, 1:2048], S0[:, 1:2048],
                                           prm[tnm][:, i:i + 1], xa_i[:, 0:2047],
                                           ALU.mult, ALU.add)
                    ins = v.scalar_tensor_tensor(dst[:, 0:1], S0[:, 0:1],
                                                 prm[tnm][:, i:i + 1],
                                                 xab[:, i:i + 1], ALU.mult, ALU.add)
                b.inc(sVE, ins)
        bvec(pa_mix)
        n_mix = b.n(sVE)

        def proj(pnm, wT, mixbuf, spdst, lastc, ssems):
            for j in range(NCC):
                wslot = W0 if (j % 2 == 0) else W1

                wsem = sW0d if (j % 2 == 0) else sW1d

                def ld(sync, wT=wT, j=j, wslot=wslot, wsem=wsem):
                    gate(sync, ("w", id(wslot)))
                    b.inc(wsem, sync.dma_start(
                        out=wslot.rearrange("p (a f) -> p a f", f=128),
                        in_=wT[:, j * 128:(j + 1) * 128]
                            .rearrange("(a p) f -> p a f", p=128)), 16)
                block.sync(ld)
                nld = b.n(wsem)
                sslot = S0 if (j % 2 == 0) else S1
                for tb in range(4):
                    bank = PC if (tb % 2 == 0) else PD

                    wsem2 = sW0d if (j % 2 == 0) else sW1d

                    def mm(t, j=j, tb=tb, wslot=wslot, mixbuf=mixbuf, bank=bank,
                           nld=nld, sslot=sslot, wsem2=wsem2):
                        b.wait(t, wsem2, nld)
                        b.wait(t, sVE, n_mix)
                        gate(t, ("bank", id(bank)))
                        if tb == 0:
                            gate(t, ("spill", id(sslot)))
                            gate(t, ("lc", id(sslot)))
                        wv = wslot.rearrange("p (a f) -> p a f", f=128)
                        for ci in range(NCC):
                            ins = t.matmul(bank[:, 0:512], wv[:, ci, :],
                                           sl(mixbuf, ci)[:, tb * 512:(tb + 1) * 512],
                                           start=(ci == 0), stop=(ci == NCC - 1))
                        b.inc(sTE, ins)
                    block.tensor(mm)
                    setg(("w", id(wslot)), sTE)
                    nte = b.n(sTE)

                    def ev(s, tb=tb, bank=bank, nte=nte, sslot=sslot):
                        b.wait(s, sTE, nte)
                        b.inc(sSC, s.copy(sslot[:, tb * 512:(tb + 1) * 512],
                                          bank[:, 0:512]))
                    bscal(ev)
                    setg(("bank", id(bank)), sSC)
                nev = b.n(sSC)

                def lc(v, j=j, sslot=sslot, lastc=lastc, nev=nev):
                    b.wait(v, sSC, nev)
                    b.inc(sVE, v.tensor_copy(lastc[:, j:j + 1], sslot[:, 2047:2048]))
                bvec(lc)
                setg(("lc", id(sslot)), sVE)

                ssem = ssems[j % 2]

                def spill(s, j=j, sslot=sslot, spdst=spdst, nev=nev, ssem=ssem):
                    b.wait(s, sSC, nev)
                    b.inc(ssem, s.dma_start(out=spdst[:, j * 2048:(j + 1) * 2048],
                                            in_=sslot), 16)
                bscal(spill)
                setg(("spill", id(sslot)), ssem)

        proj("v", wvT, B1, vsp, sS["VL"], (sVs0, sVs1))
        n_v_sp = (b.n(sVs0), b.n(sVs1))

        proj("r", wrT, B2, rsp, sS["RL"], (sRs0, sRs1))
        n_r_sp = (b.n(sRs0), b.n(sRs1))
        n_r_lc = b.n(sVE)

        ZT = B2[:, 12288:14336]

        def zts(s):
            b.wait(s, sRs0, n_r_sp[0])
            b.wait(s, sRs1, n_r_sp[1])
            b.inc(sSC, s.memzero(ZT))
        bscal(zts)
        nprev1 = n_mix
        for j in range(NCC):
            wslot = W0 if (j % 2 == 0) else W1
            wsem = sW0d if (j % 2 == 0) else sW1d
            sslot = S0 if (j % 2 == 0) else S1
            ksem = sKs0 if (j % 2 == 0) else sKs1

            def ldw(sync, j=j, wslot=wslot, wsem=wsem):
                gate(sync, ("w", id(wslot)))
                b.inc(wsem, sync.dma_start(
                    out=wslot.rearrange("p (a f) -> p a f", f=128),
                    in_=wkT[:, j * 128:(j + 1) * 128]
                        .rearrange("(a p) f -> p a f", p=128)), 16)
            block.sync(ldw)
            nld = b.n(wsem)

            def ldv(sync, j=j):
                b.wait(sync, sVs0, n_v_sp[0])
                b.wait(sync, sVs1, n_v_sp[1])
                gate(sync, ("vbuf",))
                b.inc(sRG0, sync.dma_start(out=VBUF,
                                           in_=vsp[:, j * 2048:(j + 1) * 2048]), 16)
            block.sync(ldv)
            nldv = b.n(sRG0)

            for tb in range(4):
                bank = PC if (tb % 2 == 0) else PD

                def mm(t, j=j, tb=tb, wslot=wslot, bank=bank, nld=nld, wsem=wsem,
                       sslot=sslot):
                    b.wait(t, wsem, nld)
                    b.wait(t, sVE, n_mix)
                    gate(t, ("bank", id(bank)))
                    if tb == 0:
                        gate(t, ("spill", id(sslot)))
                        gate(t, ("lc", id(sslot)))
                        gate(t, ("pscan", id(sslot)))
                    wv = wslot.rearrange("p (a f) -> p a f", f=128)
                    for ci in range(NCC):
                        ins = t.matmul(bank[:, 0:512], wv[:, ci, :],
                                       sl(B3, ci)[:, tb * 512:(tb + 1) * 512],
                                       start=(ci == 0), stop=(ci == NCC - 1))
                    b.inc(sTE, ins)
                block.tensor(mm)
                setg(("w", id(wslot)), sTE)
                nte = b.n(sTE)

                def ev(s, tb=tb, bank=bank, nte=nte, sslot=sslot):
                    b.wait(s, sTE, nte)
                    b.inc(sSC, s.copy(sslot[:, tb * 512:(tb + 1) * 512],
                                      bank[:, 0:512]))
                bscal(ev)
                setg(("bank", id(bank)), sSC)
            nev = b.n(sSC)

            def lc(v, j=j, sslot=sslot, nev=nev):
                b.wait(v, sSC, nev)
                b.inc(sVE, v.tensor_copy(sS["KL"][:, j:j + 1], sslot[:, 2047:2048]))
            bvec(lc)
            setg(("lc", id(sslot)), sVE)

            def spill(s, j=j, sslot=sslot, nev=nev, ksem=ksem):
                b.wait(s, sSC, nev)
                b.inc(ksem, s.dma_start(out=ksp[:, j * 2048:(j + 1) * 2048],
                                        in_=sslot), 16)
            bscal(spill)
            setg(("spill", id(sslot)), ksem)

            def s0p(s, j=j, nprev1=nprev1):
                b.wait(s, sVE, nprev1)
                b.inc(sSC, s.add(R_W, ZT, w8[:, j:j + 1]))
            bscal(s0p)
            ns0p = b.n(sSC)

            def v1(v, j=j, sslot=sslot, ns0p=ns0p, nldv=nldv):
                b.wait(v, sSC, ns0p)
                b.wait(v, sRG0, nldv)
                v.memset(R_PL[:, 0:1], NEG)
                v.tensor_tensor_scan(R_PL[:, 1:2049], R_W, sslot, NEG, ALU.add, ALU.max)
                b.inc(sVE, v.tensor_sub(R_T2, R_PL[:, 0:2048], R_PL[:, 1:2049]))
            bvec(v1)
            nv1 = b.n(sVE)
            nprev1 = nv1

            def s1(s, j=j, nv1=nv1):
                b.wait(s, sVE, nv1)
                b.inc(sSC, s.activation(R_T2, R_T2, ACT.Exp, bias=w8[:, j:j + 1]))
            bscal(s1)
            ns1 = b.n(sSC)

            def v2(v, sslot=sslot, ns1=ns1):
                b.wait(v, sSC, ns1)
                b.inc(sVE, v.tensor_sub(R_T3, sslot, R_PL[:, 1:2049]))
            bvec(v2)
            nv2 = b.n(sVE)

            def s2(s, nv2=nv2):
                b.wait(s, sVE, nv2)
                b.inc(sSC, s.activation(R_T3, R_T3, ACT.Exp))
            bscal(s2)
            ns2 = b.n(sSC)

            def v3(v, j=j, ns2=ns2):
                b.wait(v, sSC, ns2)
                v.tensor_tensor_scan(R_BL[:, 1:2049], R_T2, R_T3, 0.0, ALU.mult, ALU.add)
                v.tensor_mul(R_T3, R_T3, VBUF)
                v.tensor_tensor_scan(R_AL[:, 1:2049], R_T2, R_T3, 0.0, ALU.mult, ALU.add)
                v.tensor_copy(sS["PLe"][:, j:j + 1], R_PL[:, 2048:2049])
                v.tensor_copy(sS["ALe"][:, j:j + 1], R_AL[:, 2048:2049])
                v.tensor_copy(sS["BLe"][:, j:j + 1], R_BL[:, 2048:2049])
                v.tensor_copy(sS["PLp"][:, j:j + 1], R_PL[:, 2047:2048])
                v.tensor_copy(sS["ALp"][:, j:j + 1], R_AL[:, 2047:2048])
                b.inc(sVE, v.tensor_copy(sS["BLp"][:, j:j + 1], R_BL[:, 2047:2048]))
            bvec(v3)
            setg(("pscan", id(sslot)), sVE)
            setg(("vbuf",), sVE)
        n_p1 = b.n(sVE)
        n_k_sp = (b.n(sKs0), b.n(sKs1))

        def ag1st(sync):
            b.wait(sync, sVE, max(n_p1, n_r_lc))
            cols = [("PLe", 0), ("ALe", 8), ("BLe", 16), ("PLp", 24), ("ALp", 32),
                    ("BLp", 40), ("KL", 48), ("VL", 56), ("RL", 64)]
            for nm, lo in cols:
                b.inc(sLD, sync.dma_start(out=ag1i[:, lo:lo + 8], in_=sS[nm]), 16)
        block.sync(ag1st)
        n_ag1st = b.n(sLD)

        def ag1(g):
            b.wait(g, sLD, n_ag1st)
            b.inc(sCC, g.collective_compute(
                "AllGather", ALU.bypass, replica_groups=[list(range(NCORE))],
                ins=[ag1i.opt()], outs=[ag1o.opt()]))
        block.gpsimd(ag1)

        def ag1rd(sync):
            sync.wait_ge(b.cnt[sCC][0], 1)
            b.inc(sLD, sync.dma_start(out=agbuf,
                                      in_=ag1o.rearrange("(r p) f -> p r f", p=128)), 16)
        block.sync(ag1rd)
        n_ag1 = b.n(sLD)

        def emit_fold(dstP, dstA, dstB, chain2):
            def fv0(v):
                b.wait(v, sLD, n_ag1)
                v.memset(f["fP"], NEG)
                v.memset(f["fA"], 0.0)
                b.inc(sVE, v.memset(f["fB"], 0.0))
            bvec(fv0)
            for j in range(NCORE - 1):
                def fv1(v, j=j):
                    g = lambda lo: agbuf[:, j, lo:lo + 8]
                    if not chain2:
                        v.tensor_scalar_mul(f["f1"], tw8, MK["maskS"][:, j:j + 1])
                        v.tensor_scalar_add(f["f3"], g(0), MK["maskP"][:, j:j + 1])
                    else:
                        v.tensor_scalar_mul(f["f1"], tw8, MK["m2S"][:, j:j + 1])
                        v.tensor_scalar_mul(f["f2"], tw47, MK["m3S"][:, j:j + 1])
                        v.tensor_add(f["f1"], f["f1"], f["f2"])
                        v.tensor_scalar_add(f["f3"], g(24), MK["m3P"][:, j:j + 1])
                        v.tensor_scalar_add(f["f4"], g(0), MK["m2P"][:, j:j + 1])
                        v.tensor_max(f["f3"], f["f3"], f["f4"])
                    v.tensor_add(f["f1"], f["f1"], f["fP"])
                    v.tensor_max(f["f4"], f["f1"], f["f3"])
                    v.tensor_sub(f["f5"], f["f1"], f["f4"])
                    b.inc(sVE, v.tensor_sub(f["f6"], f["f3"], f["f4"]))
                bvec(fv1)
                nv = b.n(sVE)

                def fs1(s, nv=nv):
                    b.wait(s, sVE, nv)
                    s.activation(f["f5"], f["f5"], ACT.Exp)
                    b.inc(sSC, s.activation(f["f6"], f["f6"], ACT.Exp))
                bscal(fs1)
                ns = b.n(sSC)

                def fv2(v, j=j, ns=ns):
                    g = lambda lo: agbuf[:, j, lo:lo + 8]
                    b.wait(v, sSC, ns)
                    for (acc, loF, loP) in ((f["fA"], 8, 32), (f["fB"], 16, 40)):
                        if not chain2:
                            v.tensor_scalar_mul(f["f7"], g(loF), MK["maskS"][:, j:j + 1])
                        else:
                            v.tensor_scalar_mul(f["f7"], g(loP), MK["m3S"][:, j:j + 1])
                            v.tensor_scalar_mul(f["f2"], g(loF), MK["m2S"][:, j:j + 1])
                            v.tensor_add(f["f7"], f["f7"], f["f2"])
                        v.tensor_mul(acc, acc, f["f5"])
                        v.tensor_mul(f["f7"], f["f7"], f["f6"])
                        v.tensor_add(acc, acc, f["f7"])
                    b.inc(sVE, v.tensor_copy(f["fP"], f["f4"]))
                bvec(fv2)

            def fv3(v):
                v.tensor_copy(dstP, f["fP"])
                v.tensor_copy(dstA, f["fA"])
                b.inc(sVE, v.tensor_copy(dstB, f["fB"]))
            bvec(fv3)

        emit_fold(ppin, aain, bbin, False)
        emit_fold(sS["PLp"], sS["ALp"], sS["BLp"], True)
        n_folds = b.n(sVE)

        def bnd1(v):
            b.wait(v, sVE, n_folds)
            v.memset(f["bk8"], 0.0)
            v.memset(f["bv8"], 0.0)
            v.memset(f["br8"], 0.0)
            for j in range(NCORE - 1):
                for (acc, lo) in ((f["bk8"], 48), (f["bv8"], 56), (f["br8"], 64)):
                    v.tensor_scalar_mul(f["f4"], agbuf[:, j, lo:lo + 8],
                                        MK["m3S"][:, j:j + 1])
                    v.tensor_add(acc, acc, f["f4"])
            v.tensor_sub(f["f4"], f["bk8"], sS["PLp"])
            v.tensor_add(f["f4"], f["f4"], prm["u8"])
            b.inc(sVE, v.tensor_scalar_min(f["f4"], f["f4"], CLAMP))
        bvec(bnd1)
        nbv = b.n(sVE)

        def bnd2(s):
            b.wait(s, sVE, nbv)
            s.activation(f["f4"], f["f4"], ACT.Exp)
            b.inc(sSC, s.activation(f["br8"], f["br8"], ACT.Exp, scale=-1.0))
        bscal(bnd2)
        nbs = b.n(sSC)

        def bnd3(v):
            b.wait(v, sSC, nbs)
            v.tensor_mul(f["f5"], f["f4"], f["bv8"])
            v.tensor_add(f["f5"], sS["ALp"], f["f5"])
            v.tensor_add(f["f6"], sS["BLp"], f["f4"])
            v.reciprocal(f["f6"], f["f6"])
            v.tensor_mul(f["f5"], f["f5"], f["f6"])
            v.tensor_scalar_add(f["br8"], f["br8"], 1.0)
            v.reciprocal(f["br8"], f["br8"])
            b.inc(sVE, v.tensor_mul(f["f5"], f["f5"], f["br8"]))
        bvec(bnd3)
        n_rwkvl = b.n(sVE)

        nprev = n_folds
        for i in range(NCC):
            base = (i % 2) * 3
            rk = sl(B2, base)
            rv = sl(B2, base + 1)
            rr = sl(B2, base + 2)

            rsem = sRG0 if (i % 2 == 0) else sRG1

            def s0(s, i=i, nprev=nprev):
                b.wait(s, sVE, nprev)
                b.inc(sSC, s.add(R_W, ZT, w8[:, i:i + 1]))
            bscal(s0)
            ns0 = b.n(sSC)

            def ld(sync, i=i, rk=rk, rv=rv, rr=rr, rsem=rsem):
                b.wait(sync, sRs0, n_r_sp[0])
                b.wait(sync, sRs1, n_r_sp[1])
                b.wait(sync, sKs0, n_k_sp[0])
                b.wait(sync, sKs1, n_k_sp[1])
                b.wait(sync, sLD, n_ag1)
                gate(sync, ("ring2", i % 2))
                b.inc(rsem, sync.dma_start(out=rk, in_=ksp[:, i * 2048:(i + 1) * 2048]), 16)
                b.inc(rsem, sync.dma_start(out=rv, in_=vsp[:, i * 2048:(i + 1) * 2048]), 16)
                b.inc(rsem, sync.dma_start(out=rr, in_=rsp[:, i * 2048:(i + 1) * 2048]), 16)
            block.sync(ld)
            nld = b.n(rsem)

            def v1(v, i=i, rk=rk, nld=nld, rsem=rsem, ns0=ns0):
                b.wait(v, rsem, nld)
                b.wait(v, sVE, n_folds)
                b.wait(v, sSC, ns0)
                v.tensor_copy(R_PL[:, 0:1], ppin[:, i:i + 1])
                v.tensor_tensor_scan(R_PL[:, 1:2049], R_W, rk, ppin[:, i:i + 1],
                                     ALU.add, ALU.max)
                b.inc(sVE, v.tensor_sub(R_T2, R_PL[:, 0:2048], R_PL[:, 1:2049]))
            bvec(v1)
            nv1 = b.n(sVE)
            nprev = nv1

            def s1(s, i=i, nv1=nv1):
                b.wait(s, sVE, nv1)
                b.inc(sSC, s.activation(R_T2, R_T2, ACT.Exp, bias=w8[:, i:i + 1]))
            bscal(s1)
            ns1 = b.n(sSC)

            def v2(v, rk=rk, ns1=ns1):
                b.wait(v, sSC, ns1)
                b.inc(sVE, v.tensor_sub(R_T3, rk, R_PL[:, 1:2049]))
            bvec(v2)
            nv2 = b.n(sVE)

            def s2(s, nv2=nv2):
                b.wait(s, sVE, nv2)
                b.inc(sSC, s.activation(R_T3, R_T3, ACT.Exp))
            bscal(s2)
            ns2 = b.n(sSC)

            def v3(v, i=i, rv=rv, rk=rk, ns2=ns2):
                b.wait(v, sSC, ns2)
                v.tensor_copy(R_BL[:, 0:1], bbin[:, i:i + 1])
                v.tensor_tensor_scan(R_BL[:, 1:2049], R_T2, R_T3, bbin[:, i:i + 1],
                                     ALU.mult, ALU.add)
                v.tensor_mul(R_T3, R_T3, rv)
                v.tensor_copy(R_AL[:, 0:1], aain[:, i:i + 1])
                v.tensor_tensor_scan(R_AL[:, 1:2049], R_T2, R_T3, aain[:, i:i + 1],
                                     ALU.mult, ALU.add)
                v.tensor_sub(R_T2, rk, R_PL[:, 0:2048])
                b.inc(sVE, v.tensor_scalar(R_T2, R_T2, prm["u8"][:, i:i + 1],
                                           CLAMP, ALU.add, ALU.min))
            bvec(v3)
            nv3 = b.n(sVE)

            def s3(s, rr=rr, nv3=nv3):
                b.wait(s, sVE, nv3)
                s.activation(R_T2, R_T2, ACT.Exp)
                b.inc(sSC, s.activation(rr, rr, ACT.Exp, scale=-1.0))
            bscal(s3)
            ns3 = b.n(sSC)

            def v4(v, i=i, rv=rv, rr=rr, ns3=ns3):
                b.wait(v, sSC, ns3)
                v.tensor_mul(R_T3, R_T2, rv)
                v.tensor_add(R_T3, R_AL[:, 0:2048], R_T3)
                v.tensor_add(R_T2, R_BL[:, 0:2048], R_T2)
                v.reciprocal(R_T2, R_T2)
                v.tensor_mul(R_T3, R_T3, R_T2)
                v.tensor_scalar_add(rr, rr, 1.0)
                v.reciprocal(rr, rr)
                b.inc(sVE, v.tensor_mul(sl(B3, i), R_T3, rr))
            bvec(v4)
            marks[("ring2", i % 2)] = (sVE, b.n(sVE))
        n_p2 = b.n(sVE)

        if "rwkv" in dbg:
            tap = mktap("rwkv", (128, 16384))

            def dt1(s, tap=tap):
                b.wait(s, sVE, n_p2)
                b.inc(sSPt, s.dma_start(out=tap, in_=B3), 16)
            bscal(dt1)
            setg(("tap", id(B3)), sSPt)

        PAlo = PA[:, 0:512]
        PAhi = PA[:, 512:1024]
        dbanks = [PC, PD, PAlo, PAhi]
        zs4 = [S0[:, 0:512], S0[:, 512:1024], S1[:, 0:512], S1[:, 512:1024]]
        for cb in range(NCC):
            wslot = W0 if (cb % 2 == 0) else W1
            wsem = sW0d if (cb % 2 == 0) else sW1d

            def ld(sync, cb=cb, wslot=wslot, wsem=wsem):
                b.wait(sync, sVE, n_p2)
                gate(sync, ("w", id(wslot)))
                b.inc(wsem, sync.dma_start(
                    out=wslot.rearrange("p (a f) -> p a f", f=128),
                    in_=woT[:, cb * 128:(cb + 1) * 128]
                        .rearrange("(a p) f -> p a f", p=128)), 16)
            block.sync(ld)
            nld = b.n(wsem)

            for tb in range(4):
                bank = dbanks[tb]
                zslot = zs4[tb]

                zsem = [sZR0, sZR1, sZR2, sZR3][tb]

                def zld(sync, cb=cb, tb=tb, zslot=zslot, zsem=zsem):
                    b.wait(sync, sRs0, n_r_sp[0])
                    b.wait(sync, sRs1, n_r_sp[1])
                    b.wait(sync, sSPb, n_z0sp)
                    gate(sync, ("zres", id(zslot)))
                    b.inc(zsem, sync.dma_start(
                        out=zslot,
                        in_=z0sp[:, cb * 2048 + tb * 512: cb * 2048 + (tb + 1) * 512]), 16)
                block.sync(zld)
                nzld = b.n(zsem)

                def mm(t, tb=tb, wslot=wslot, bank=bank, nld=nld, wsem=wsem):
                    b.wait(t, wsem, nld)
                    gate(t, ("bank", id(bank)))
                    wv = wslot.rearrange("p (a f) -> p a f", f=128)
                    for a in range(NCC):
                        ins = t.matmul(bank, wv[:, a, :],
                                       sl(B3, a)[:, tb * 512:(tb + 1) * 512],
                                       start=(a == 0), stop=(a == NCC - 1))
                    b.inc(sTE, ins)
                block.tensor(mm)
                nte = b.n(sTE)

                def ev(v, cb=cb, tb=tb, bank=bank, nte=nte, nzld=nzld, zslot=zslot,
                       zsem=zsem):
                    b.wait(v, sTE, nte)
                    b.wait(v, zsem, nzld)
                    b.inc(sVE, v.tensor_add(
                        sl(B1, cb)[:, tb * 512:(tb + 1) * 512],
                        bank, zslot))
                bvec(ev)
                setg(("bank", id(bank)), sVE)
                setg(("zres", id(zslot)), sVE)

            def bmm(t, cb=cb, wslot=wslot):
                b.wait(t, sVE, n_rwkvl)
                if cb == 0:
                    gate(t, ("bank", id(PE_)))
                wv = wslot.rearrange("p (a f) -> p a f", f=128)
                for a in range(NCC):
                    ins = t.matmul(PE_[:, cb:cb + 1], wv[:, a, :],
                                   f["f5"][:, a:a + 1],
                                   start=(a == 0), stop=(a == NCC - 1))
                b.inc(sTE, ins)
            block.tensor(bmm)
            setg(("w", id(wslot)), sTE)
        n_delta = b.n(sVE)
        n_dl = b.n(sTE)

        def dl1(s):
            b.wait(s, sTE, n_dl)
            b.inc(sSC, s.copy(f["f7"], PE_[:, 0:NCC]))
        bscal(dl1)
        setg(("bank", id(PE_)), sSC)
        ndl1 = b.n(sSC)

        def dl2(v):
            b.wait(v, sSC, ndl1)
            b.inc(sVE, v.tensor_add(f["f7"], f["f7"], z0b))
        bvec(dl2)
        ndl2 = b.n(sVE)

        def dl3(s):
            b.wait(s, sVE, ndl2)
            b.inc(sSC, s.activation(f["f6"], f["f7"], ACT.Square))
        bscal(dl3)
        ndl3 = b.n(sSC)

        def dl4(t):
            b.wait(t, sSC, ndl3)
            gate(t, ("bank", id(PF)))
            t.matmul(PF[0:1, 0:8], ONESC, f["f7"], start=True, stop=True)
            b.inc(sTE, t.matmul(PF[0:1, 8:16], ONESC, f["f6"], start=True, stop=True))
        block.tensor(dl4)
        ndl4 = b.n(sTE)

        def dl5(v):
            b.wait(v, sTE, ndl4)
            v.reduce_sum(stat1[:, 0:1], PF[0:1, 0:8], AX.X)
            v.reduce_sum(stat1[:, 1:2], PF[0:1, 8:16], AX.X)
            v.tensor_scalar_mul(stat1[:, 2:3], stat1[:, 0:1], 1.0 / C)
            v.tensor_scalar_mul(stat1[:, 3:4], stat1[:, 1:2], 1.0 / C)
            v.scalar_tensor_tensor(stat1[:, 4:5], stat1[:, 2:3], stat1[:, 2:3],
                                   stat1[:, 3:4], ALU.mult, ALU.subtract)
            b.inc(sVE, v.tensor_scalar(stat1[:, 4:5], stat1[:, 4:5], -1.0, EPS,
                                       ALU.mult, ALU.add))
        bvec(dl5)
        ndl5 = b.n(sVE)

        def dl6(s):
            b.wait(s, sVE, ndl5)
            b.inc(sSC, s.activation(stat1[:, 5:6], stat1[:, 4:5], ACT.Sqrt))
        bscal(dl6)
        ndl6 = b.n(sSC)

        def dl7(v):
            b.wait(v, sSC, ndl6)
            v.memset(ONESR, 1.0)
            v.reciprocal(stat1[:, 6:7], stat1[:, 5:6])
            b.inc(sVE, v.tensor_copy(stat1[:, 7:8], stat1[:, 2:3]))
        bvec(dl7)
        ndl7 = b.n(sVE)

        def dl8(t):
            b.wait(t, sVE, ndl7)
            b.inc(sTE, t.matmul(PF[:, 16:18], ONESR, stat1[:, 6:8],
                                start=True, stop=True))
        block.tensor(dl8)
        ndl8 = b.n(sTE)

        def dl9(v):
            b.wait(v, sTE, ndl8)
            v.tensor_copy(f["bm2x"][:, 0:1], PF[:, 17:18])
            v.tensor_copy(f["bm2x"][:, 1:2], PF[:, 16:17])
            v.tensor_scalar(xcb, f["f7"], f["bm2x"][:, 0:1], f["bm2x"][:, 1:2],
                            ALU.subtract, ALU.mult)
            v.tensor_mul(xcb, xcb, prm["ln2g"])
            v.tensor_add(xcb, xcb, prm["ln2b"])
            b.inc(sVE, v.tensor_scalar_mul(xcb, xcb, C0M[:, 0:1]))
        bvec(dl9)
        setg(("bank", id(PF)), sVE)
        n_xcb = b.n(sVE)

        if "xatt" in dbg:
            tap = mktap("xatt", (128, 16384))

            def dt2(s, tap=tap):
                b.wait(s, sVE, n_delta)
                b.inc(sSPt, s.dma_start(out=tap, in_=B1), 16)
            bscal(dt2)
            setg(("tap", id(B1)), sSPt)

        for tt in range(NT):
            for bg in range(2):
                srcs = [B1[:, cb2 * 2048 + tt * 128: cb2 * 2048 + (tt + 1) * 128]
                        for cb2 in range(bg * 4, bg * 4 + 4)]
                dsts = [B2[:, tt * 1024 + bg * 512: tt * 1024 + (bg + 1) * 512]]
                emit_tp(srcs, dsts, wait_list=[(sVE, n_delta)], banks=(PC, PD, PE_, PF))
        n_xatm = b.n(sSC)

        emit_ln(lambda i: tmt(B2, i), NT, wait_list=[(sSC, n_xatm)])
        n_ln2 = b.n(sVE)

        for i in range(NCC):
            srcs = [tmt(B2, tt)[:, i * 128:(i + 1) * 128] for tt in range(NT)]
            dsts = [B3[:, i * 2048 + q * 512: i * 2048 + (q + 1) * 512]
                    for q in range(4)]
            wl = [(sVE, max(n_ln2, n_p2))]
            if b.n(sSPt):
                wl.append((sSPt, b.n(sSPt)))
            emit_tp(srcs, dsts, wait_list=wl, banks=(PC, PD, PE_, PF))
        n_z2 = b.n(sSC)

        def cm_xc(v):
            b.wait(v, sSC, n_z2)
            for i in range(NCC):
                ins = v.tensor_scalar(sl(B3, i), sl(B3, i), prm["ln2g"][:, i:i + 1],
                                      prm["ln2b"][:, i:i + 1], ALU.mult, ALU.add)
            b.inc(sVE, ins)
        bvec(cm_xc)
        n_xc = b.n(sVE)

        def cm_spx(s):
            b.wait(s, sSC, n_xatm)
            tapwait(s)
            b.inc(sSPa, s.dma_start(out=xasp, in_=B1), 16)
        bscal(cm_spx)
        n_xasp = b.n(sSPa)

        def mix2(dstbuf, cname, wve, tag):
            def fn(v):
                b.wait(v, sVE, wve)
                for i in range(NCC):
                    xc_i = sl(B3, i)
                    v.tensor_sub(S0[:, 1:2048], xc_i[:, 1:2048], xc_i[:, 0:2047])
                    v.tensor_sub(S0[:, 0:1], xc_i[:, 0:1], xcb[:, i:i + 1])
                    v.scalar_tensor_tensor(sl(dstbuf, i)[:, 1:2048], S0[:, 1:2048],
                                           prm[cname][:, i:i + 1], xc_i[:, 0:2047],
                                           ALU.mult, ALU.add)
                    ins = v.scalar_tensor_tensor(sl(dstbuf, i)[:, 0:1], S0[:, 0:1],
                                                 prm[cname][:, i:i + 1],
                                                 xcb[:, i:i + 1], ALU.mult, ALU.add)
                b.inc(sVE, ins)
            bvec(fn)

        mix2(B2, "cmr", max(n_xc, n_xcb), "xr2")
        n_xr2 = b.n(sVE)

        for j in range(NCC):
            wslot = W0 if (j % 2 == 0) else W1
            wsem = sW0d if (j % 2 == 0) else sW1d

            def ld(sync, j=j, wslot=wslot, wsem=wsem):
                gate(sync, ("w", id(wslot)))
                b.inc(wsem, sync.dma_start(
                    out=wslot.rearrange("p (a f) -> p a f", f=128),
                    in_=fwrT[:, j * 128:(j + 1) * 128]
                        .rearrange("(a p) f -> p a f", p=128)), 16)
            block.sync(ld)
            nld = b.n(wsem)
            for tb in range(4):
                bank = PC if (tb % 2 == 0) else PD

                def mm(t, tb=tb, wslot=wslot, bank=bank, nld=nld, wsem=wsem):
                    b.wait(t, wsem, nld)
                    b.wait(t, sVE, n_xr2)
                    b.wait(t, sSPa, n_xasp)
                    gate(t, ("bank", id(bank)))
                    wv = wslot.rearrange("p (a f) -> p a f", f=128)
                    for ci in range(NCC):
                        ins = t.matmul(bank[:, 0:512], wv[:, ci, :],
                                       sl(B2, ci)[:, tb * 512:(tb + 1) * 512],
                                       start=(ci == 0), stop=(ci == NCC - 1))
                    b.inc(sTE, ins)
                block.tensor(mm)
                setg(("w", id(wslot)), sTE)
                nte = b.n(sTE)

                def ev(s, j=j, tb=tb, bank=bank, nte=nte):
                    b.wait(s, sTE, nte)
                    tapwait(s)
                    b.inc(sSC, s.copy(sl(B1, j)[:, tb * 512:(tb + 1) * 512],
                                      bank[:, 0:512]))
                bscal(ev)
                setg(("bank", id(bank)), sSC)
        n_r2 = b.n(sSC)

        for i in range(NCC):
            def sg1(s, i=i):
                b.wait(s, sSC, n_r2)
                b.inc(sSC, s.activation(sl(B1, i), sl(B1, i), ACT.Exp, scale=-1.0))
            bscal(sg1)
            nsg = b.n(sSC)

            def sg2(v, i=i, nsg=nsg):
                b.wait(v, sSC, nsg)
                v.tensor_scalar_add(sl(B1, i), sl(B1, i), 1.0)
                b.inc(sVE, v.reciprocal(sl(B1, i), sl(B1, i)))
            bvec(sg2)
        n_sig2 = b.n(sVE)

        def cm_spsg(s):
            b.wait(s, sVE, n_sig2)
            b.inc(sSPa, s.dma_start(out=sgsp, in_=B1), 16)
        bscal(cm_spsg)
        n_sgsp = b.n(sSPa)

        def cm_gate_xk2(v):
            b.wait(v, sSC, n_r2)
            b.inc(sVE, v.engine_nop())
        bvec(cm_gate_xk2)
        mix2(B2, "cmk", b.n(sVE), "xk2")
        n_xk2 = b.n(sVE)

        for g8 in range(4):
            def ldg(sync, g8=g8):
                b.wait(sync, sSPa, n_sgsp)
                gate(sync, ("cmixw",))
                for hh in range(8):
                    h = g8 * 8 + hh
                    b.inc(sCW, sync.dma_start(
                        out=B1[:, hh * 1024:(hh + 1) * 1024]
                            .rearrange("p (a f) -> p a f", f=128),
                        in_=fwkT[:, h * 128:(h + 1) * 128]
                            .rearrange("(a p) f -> p a f", p=128)), 16)
                    b.inc(sCW, sync.dma_start(
                        out=B1[:, 8192 + hh * 1024: 8192 + (hh + 1) * 1024],
                        in_=fwvT[h * 128:(h + 1) * 128, :]), 16)
            block.sync(ldg)
            nldg = b.n(sCW)

            for tb in range(4):
                kvbank = [PA[:, 0:512], PA[:, 512:1024], PB[:, 0:512], PB[:, 512:1024]]
                mmv0_q = []
                for hh in range(8):
                    kbank = PC if (hh % 2 == 0) else PD
                    kkslot = (S0 if hh < 4 else S1)[:, (hh % 4) * 512:(hh % 4 + 1) * 512]

                    def mmk(t, g8=g8, tb=tb, hh=hh, kbank=kbank, nldg=nldg):
                        b.wait(t, sCW, nldg)
                        b.wait(t, sVE, n_xk2)
                        gate(t, ("bank", id(kbank)))
                        wv = B1[:, hh * 1024:(hh + 1) * 1024] \
                            .rearrange("p (a f) -> p a f", f=128)
                        for ci in range(NCC):
                            ins = t.matmul(kbank[:, 0:512], wv[:, ci, :],
                                           sl(B2, ci)[:, tb * 512:(tb + 1) * 512],
                                           start=(ci == 0), stop=(ci == NCC - 1))
                        b.inc(sTE, ins)
                    block.tensor(mmk)
                    ntk = b.n(sTE)

                    def evk(s, kbank=kbank, kkslot=kkslot, ntk=ntk):
                        b.wait(s, sTE, ntk)
                        gate(s, ("kkS",))
                        b.inc(sSC, s.activation(kkslot, kbank[:, 0:512], ACT.Relu))
                    bscal(evk)
                    setg(("bank", id(kbank)), sSC)
                    nkr = b.n(sSC)

                    def evk2(v, kkslot=kkslot, nkr=nkr):
                        b.wait(v, sSC, nkr)
                        b.inc(sVE, v.tensor_mul(kkslot, kkslot, kkslot))
                    bvec(evk2)
                    nkkh = b.n(sVE)

                    def mmv0(t, hh=hh, nkkh=nkkh, g8=g8, tb=tb):
                        b.wait(t, sVE, nkkh)
                        if hh == 0:
                            gate(t, ("kvev",))
                        kks = (S0 if hh < 4 else S1)[:, (hh % 4) * 512:(hh % 4 + 1) * 512]
                        for ci4 in range(4):
                            ci = ci4
                            ins = t.matmul(
                                kvbank[ci4],
                                B1[:, 8192 + hh * 1024 + ci * 128:
                                   8192 + hh * 1024 + (ci + 1) * 128],
                                kks, start=(hh == 0), stop=(hh == 7))
                        b.inc(sTE, ins)
                    mmv0_q.append(mmv0)
                    if hh > 0:
                        block.tensor(mmv0_q[hh - 1])
                block.tensor(mmv0_q[7])
                nkk = b.n(sVE)
                ntv0 = b.n(sTE)

                def evv0(z, g8=g8, tb=tb, ntv0=ntv0):
                    b.wait(z, sTE, ntv0)
                    for ci4 in range(4):
                        src_ = kvbank[ci4]
                        dst = sl(B3, ci4)[:, tb * 512:(tb + 1) * 512]
                        if g8 == 0:
                            ins = z.copy(dst, src_)
                        else:
                            ins = z.tensor_add(dst, dst, src_)
                    if g8 == 0:
                        b.inc(sSC, ins)
                    else:
                        b.inc(sVE, ins)
                if g8 == 0:
                    bscal(evv0)
                    nev0 = (sSC, b.n(sSC))
                else:
                    bvec(evv0)
                    nev0 = (sVE, b.n(sVE))

                def mmv1(t, g8=g8, tb=tb, nkk=nkk, nev0=nev0):
                    b.wait(t, sVE, nkk)
                    b.wait(t, nev0[0], nev0[1])
                    for hh in range(8):
                        kks = (S0 if hh < 4 else S1)[:, (hh % 4) * 512:(hh % 4 + 1) * 512]
                        for ci4 in range(4):
                            ci = 4 + ci4
                            ins = t.matmul(
                                kvbank[ci4],
                                B1[:, 8192 + hh * 1024 + ci * 128:
                                   8192 + hh * 1024 + (ci + 1) * 128],
                                kks, start=(hh == 0), stop=(hh == 7))
                    b.inc(sTE, ins)
                block.tensor(mmv1)
                ntv1 = b.n(sTE)

                def evv1(z, g8=g8, tb=tb, ntv1=ntv1):
                    b.wait(z, sTE, ntv1)
                    for ci4 in range(4):
                        ci = 4 + ci4
                        src_ = kvbank[ci4]
                        dst = sl(B3, ci)[:, tb * 512:(tb + 1) * 512]
                        if g8 == 0:
                            ins = z.copy(dst, src_)
                        else:
                            ins = z.tensor_add(dst, dst, src_)
                    if g8 == 0:
                        b.inc(sSC, ins)
                    else:
                        b.inc(sVE, ins)
                if g8 == 0:
                    bscal(evv1)
                    marks[("kvev",)] = (sSC, b.n(sSC))
                else:
                    bvec(evv1)
                    marks[("kvev",)] = (sVE, b.n(sVE))
                setg(("kkS",), sTE)
            setg(("cmixw",), sTE)
        n_cmix_te = b.n(sTE)
        n_kv_sc = b.n(sSC)
        n_kv_ve = b.n(sVE)

        for ci in range(NCC):
            zsem = sZR0 if (ci % 2 == 0) else sZR1

            def ldc(sync, ci=ci, zsem=zsem):
                b.wait(sync, sSPa, n_sgsp)
                b.wait(sync, sTE, n_cmix_te)
                gate(sync, ("cres", ci % 2))
                b.inc(zsem, sync.dma_start(
                    out=(S0 if ci % 2 == 0 else S1),
                    in_=sgsp[:, ci * 2048:(ci + 1) * 2048]), 16)
            block.sync(ldc)
            nld1 = b.n(zsem)

            def cmb1(v, ci=ci, nld1=nld1, zsem=zsem):
                b.wait(v, zsem, nld1)
                b.wait(v, sVE, n_kv_ve)
                b.wait(v, sSC, n_kv_sc)
                b.inc(sVE, v.tensor_mul(sl(B3, ci), sl(B3, ci),
                                        (S0 if ci % 2 == 0 else S1)))
            bvec(cmb1)
            nve1 = b.n(sVE)

            def ldc2(sync, ci=ci, nve1=nve1, zsem=zsem):
                b.wait(sync, sVE, nve1)
                b.inc(zsem, sync.dma_start(
                    out=(S0 if ci % 2 == 0 else S1),
                    in_=xasp[:, ci * 2048:(ci + 1) * 2048]), 16)
            block.sync(ldc2)
            nld2 = b.n(zsem)

            def cmb2(v, ci=ci, nld2=nld2, zsem=zsem):
                b.wait(v, zsem, nld2)
                b.inc(sVE, v.tensor_add(sl(B3, ci), sl(B3, ci),
                                        (S0 if ci % 2 == 0 else S1)))
            bvec(cmb2)
            marks[("cres", ci % 2)] = (sVE, b.n(sVE))
        n_bo = b.n(sVE)

        if "bo" in dbg:
            tap = mktap("bo", (128, 16384))

            def dt3(s, tap=tap):
                b.wait(s, sVE, n_bo)
                b.inc(sSPt, s.dma_start(out=tap, in_=B3), 16)
            bscal(dt3)
            setg(("tap", id(B3)), sSPt)

        for tt in range(NT):
            for bg in range(2):
                srcs = [B3[:, cb2 * 2048 + tt * 128: cb2 * 2048 + (tt + 1) * 128]
                        for cb2 in range(bg * 4, bg * 4 + 4)]
                dsts = [B2[:, tt * 1024 + bg * 512: tt * 1024 + (bg + 1) * 512]]
                emit_tp(srcs, dsts, wait_list=[(sVE, n_bo)], banks=(PC, PD, PE_, PF))
        n_botm = b.n(sSC)

        emit_ln(lambda i: tmt(B2, i), NT, wait_list=[(sSC, n_botm)])
        n_lnb = b.n(sVE)

        for tt in range(NT):
            zsem = sZR0 if (tt % 2 == 0) else sZR1

            def ldx(sync, tt=tt, zsem=zsem):
                b.wait(sync, sSC, n_botm)
                gate(sync, ("xres", tt % 2))
                b.inc(zsem, sync.dma_start(
                    out=(S0 if tt % 2 == 0 else S1)[:, 0:1024],
                    in_=x_sl[1 + tt * 128:1 + (tt + 1) * 128, :]), 16)
            block.sync(ldx)
            nldx = b.n(zsem)

            def x2t(v, tt=tt, nldx=nldx, zsem=zsem):
                b.wait(v, zsem, nldx)
                b.wait(v, sVE, n_lnb)
                tapwait(v)
                b.inc(sVE, v.tensor_add(tmt(B3, tt), tmt(B2, tt),
                                        (S0 if tt % 2 == 0 else S1)[:, 0:1024]))
            bvec(x2t)
            marks[("xres", tt % 2)] = (sVE, b.n(sVE))
        n_x2 = b.n(sVE)

        def spx2(s):
            b.wait(s, sVE, n_x2)
            tapwait(s)
            b.inc(sSPc3, s.dma_start(out=x2sp, in_=B3), 16)
        bscal(spx2)
        n_x2sp = b.n(sSPc3)

        emit_ln(lambda i: tmt(B3, i), NT, out_tile=lambda i: tmt(B2, i))
        n_y = b.n(sVE)

        if "y" in dbg:
            tap = mktap("y", (128, 16384))

            def dt4(s, tap=tap):
                b.wait(s, sVE, n_y)
                b.inc(sSPt, s.dma_start(out=tap, in_=B2), 16)
            bscal(dt4)
            setg(("tap", id(B2)), sSPt)

        def ldphi(sync):
            b.wait(sync, sSPc3, n_x2sp)
            b.wait(sync, sVE, n_y)
            gate(sync, ("cmixw",))
            b.inc(sLD, sync.dma_start(
                out=B1.rearrange("p (a f) -> p a f", f=2048),
                in_=phiT_d.rearrange("(a p) f -> p a f", p=128)), 16)
            b.inc(sLD, sync.dma_start(out=PHIB, in_=phib_d), 16)
            b.inc(sLD, sync.dma_start(out=MDs, in_=md_d), 16)
            b.inc(sLD, sync.dma_start(out=MCs, in_=mc_d), 16)
        block.sync(ldphi)
        n_ldphi = b.n(sLD)

        E_ODD = YT
        DC_ODD = XT
        CSTG2 = XTt

        def phi_banks(tt):
            if tt % 2 == 0:
                return [PA[:, 0:512], PA[:, 512:1024], PB[:, 0:512], PB[:, 512:1024]]
            return [PC[:, 0:512], PD[:, 0:512], PE_[:, 0:512], PF[:, 0:512]]

        def phi_bufs(tt):
            if tt % 2 == 0:
                return (S0, S1, CSTG)
            return (E_ODD, DC_ODD, CSTG2)

        def phi_stageA(tt):
            bks = phi_banks(tt)
            for bb_ in range(2):
                def ty(t, tt=tt, bb_=bb_, bks=bks):
                    b.wait(t, sVE, n_y)
                    gate(t, ("pbank", tt % 2))
                    for jj in range(4):
                        dd = bb_ * 4 + jj
                        ins = t.transpose(bks[bb_][:, jj * 128:(jj + 1) * 128],
                                          tmt(B2, tt)[:, dd * 128:(dd + 1) * 128],
                                          IDN)
                    b.inc(sTE, ins)
                block.tensor(ty)
                nty = b.n(sTE)

                def evy_(s, bb_=bb_, nty=nty, bks=bks, tt=tt):
                    b.wait(s, sTE, nty)
                    if bb_ == 0:
                        gate(s, ("w", id(W0)))
                    b.inc(sSC, s.copy(W0[:, bb_ * 512:(bb_ + 1) * 512], bks[bb_]))
                bscal(evy_)
            setg(("w", id(W0)), sSC)
            n_yfm = b.n(sSC)
            for q4 in range(4):
                def mm(t, q4=q4, n_yfm=n_yfm, tt=tt, bks=bks):
                    b.wait(t, sLD, n_ldphi)
                    b.wait(t, sSC, n_yfm)
                    wv = W0.rearrange("p (a f) -> p a f", f=128)
                    for dd in range(NCC):
                        ins = t.matmul(bks[q4], wv[:, dd, :],
                                       B1[:, dd * 2048 + q4 * 512:
                                          dd * 2048 + (q4 + 1) * 512],
                                       start=(dd == 0), stop=(dd == NCC - 1))
                    b.inc(sTE, ins)
                block.tensor(mm)
            setg(("w", id(W0)), sTE)
            marks[("philg", tt)] = (sTE, b.n(sTE))

        def phi_stageB1(tt):
            bks = phi_banks(tt)
            Ebuf, Dbuf, Cstg = phi_bufs(tt)
            n_lg = marks[("philg", tt)][1]

            def sm1(v, n_lg=n_lg, bks=bks, Ebuf=Ebuf, tt=tt):
                b.wait(v, sTE, n_lg)
                gate(v, ("Ebuf", tt % 2))
                for q4 in range(4):
                    v.tensor_add(Ebuf[:, q4 * 512:(q4 + 1) * 512], bks[q4],
                                 PHIB[:, q4 * 512:(q4 + 1) * 512])
                v.reduce_max(f["f5"][:, 0:1], Ebuf, AX.X)
                b.inc(sVE, v.tensor_scalar_mul(f["f5"][:, 0:1], f["f5"][:, 0:1], -1.0))
            bvec(sm1)
            nsm1 = b.n(sVE)

            def sm2(s, nsm1=nsm1, Ebuf=Ebuf):
                b.wait(s, sVE, nsm1)
                b.inc(sSC, s.activation(Ebuf, Ebuf, ACT.Exp, bias=f["f5"][:, 0:1]))
            bscal(sm2)

        def phi_stageB2(tt):
            bks = phi_banks(tt)
            Ebuf, Dbuf, Cstg = phi_bufs(tt)
            nsm2 = b.n(sSC)

            def sm3(v, nsm2=nsm2, Ebuf=Ebuf):
                b.wait(v, sSC, nsm2)
                v.tensor_reduce(sD, Ebuf.rearrange("p (n s) -> p n s", s=32),
                                AX.X, ALU.add)
                v.tensor_reduce(sCt, Ebuf.rearrange("p (n s) -> p s n", s=32),
                                AX.X, ALU.add)
                v.reciprocal(sD, sD)
                b.inc(sVE, v.reciprocal(sCt, sCt))
            bvec(sm3)
            nsm3 = b.n(sVE)

            def sm4(t, nsm3=nsm3, bks=bks):
                b.wait(t, sVE, nsm3)
                t.transpose(bks[0][0:64, 0:128], sD, IDN)
                b.inc(sTE, t.transpose(bks[0][0:32, 128:256], sCt, IDN))
            block.tensor(sm4)
            nsm4 = b.n(sTE)

            def sm5(s, nsm4=nsm4, bks=bks):
                b.wait(s, sTE, nsm4)
                s.copy(rDT, bks[0][0:64, 0:128])
                b.inc(sSC, s.copy(rCT, bks[0][0:32, 128:256]))
            bscal(sm5)
            nsm5 = b.n(sSC)

            for (which, rT, Ms) in (("D", rDT, MDs), ("C", rCT, MCs)):
                for q4 in range(4):
                    def mmb(t, rT=rT, Ms=Ms, q4=q4, nsm5=nsm5, which=which, tt=tt,
                            bks=bks):
                        b.wait(t, sSC, nsm5)
                        if which == "C" and q4 == 0:
                            gate(t, ("dcmul", tt % 2))
                        b.inc(sTE, t.matmul(bks[q4], rT,
                                            Ms[:, q4 * 512:(q4 + 1) * 512],
                                            start=True, stop=True))
                    block.tensor(mmb)
                nmb = b.n(sTE)

                def mlt(v, nmb=nmb, which=which, tt=tt, bks=bks, Ebuf=Ebuf,
                        Dbuf=Dbuf):
                    b.wait(v, sTE, nmb)
                    if which == "D":
                        gate(v, ("Dbuf", tt % 2))
                    else:
                        gate(v, ("Ddma", tt % 2))
                    for q4 in range(4):
                        ins = v.tensor_mul(Dbuf[:, q4 * 512:(q4 + 1) * 512],
                                           Ebuf[:, q4 * 512:(q4 + 1) * 512],
                                           bks[q4])
                    b.inc(sVE, ins)
                bvec(mlt)
                nmlt = b.n(sVE)
                if which == "D":
                    marks[("dcmul", tt % 2)] = (sVE, nmlt)

                    def smd(s, tt=tt, nmlt=nmlt, Dbuf=Dbuf):
                        b.wait(s, sVE, nmlt)
                        b.inc(sDsp, s.dma_start(
                            out=dsp[tt * 128:(tt + 1) * 128, :], in_=Dbuf), 16)
                    bscal(smd)
                    marks[("Ddma", tt % 2)] = (sDsp, b.n(sDsp))
                else:
                    marks[("Ebuf", tt % 2)] = (sVE, nmlt)

            nCmul = b.n(sVE)
            for bb_ in range(4):
                bk = bks[bb_ % 2]

                def tc(t, tt=tt, bb_=bb_, bk=bk, nCmul=nCmul, Dbuf=Dbuf):
                    b.wait(t, sVE, nCmul)
                    if bb_ >= 2:
                        gate(t, ("ctev", tt % 2, bb_ % 2))
                    for jj in range(4):
                        np_ = bb_ * 4 + jj
                        ins = t.transpose(bk[:, jj * 128:(jj + 1) * 128],
                                          Dbuf[:, np_ * 128:(np_ + 1) * 128], IDN)
                    b.inc(sTE, ins)
                block.tensor(tc)
                ntc = b.n(sTE)

                def evc(s, tt=tt, bb_=bb_, bk=bk, ntc=ntc, Cstg=Cstg):
                    b.wait(s, sTE, ntc)
                    if bb_ == 0:
                        gate(s, ("cstg", tt % 2))
                    b.inc(sSC, s.copy(Cstg[:, bb_ * 512:(bb_ + 1) * 512], bk))
                bscal(evc)
                marks[("ctev", tt % 2, bb_ % 2)] = (sSC, b.n(sSC))
            marks[("Dbuf", tt % 2)] = (sSC, b.n(sSC))
            marks[("pbank", tt % 2)] = (sSC, b.n(sSC))
            nctp = b.n(sSC)

            csem = sCs0 if (tt % 2 == 0) else sCs1

            def smc(s, tt=tt, nctp=nctp, Cstg=Cstg, csem=csem):
                b.wait(s, sSC, nctp)
                b.inc(csem, s.dma_start(
                    out=csp.rearrange("(a p) t -> p a t", p=128)[:, :, tt * 128:(tt + 1) * 128],
                    in_=Cstg.rearrange("p (a f) -> p a f", f=128)), 16)
            bscal(smc)
            marks[("cstg", tt % 2)] = (csem, b.n(csem))

        phi_stageA(0)
        for tt in range(NT):
            phi_stageB1(tt)
            if tt + 1 < NT:
                phi_stageA(tt + 1)
            phi_stageB2(tt)

        n_phi = b.n(sDsp)
        n_phic = (b.n(sCs0), b.n(sCs1))
        n_phi_ve = b.n(sVE)
        n_phi_te = b.n(sTE)

        for p16 in range(16):
            dslot = S0 if (p16 % 2 == 0) else S1
            zsem = sZR0 if (p16 % 2 == 0) else sZR1

            def ldd(sync, p16=p16, dslot=dslot, zsem=zsem):
                b.wait(sync, sDsp, n_phi)
                b.wait(sync, sVE, n_phi_ve)
                b.wait(sync, sTE, n_phi_te)
                gate(sync, ("dld", p16 % 2))
                b.inc(zsem, sync.dma_start(
                    out=dslot.rearrange("p (a f) -> p a f", f=128),
                    in_=dsp[:, p16 * 128:(p16 + 1) * 128]
                        .rearrange("(a p) f -> p a f", p=128)), 16)
            block.sync(ldd)
            nldd = b.n(zsem)
            pbank = PA if (p16 % 2 == 0) else PB

            def mmx(t, p16=p16, dslot=dslot, pbank=pbank, nldd=nldd, zsem=zsem):
                b.wait(t, zsem, nldd)
                gate(t, ("bank", id(pbank)))
                dv = dslot.rearrange("p (a f) -> p a f", f=128)
                for half in range(2):
                    for tt in range(NT):
                        ins = t.matmul(pbank[:, half * 512:(half + 1) * 512],
                                       dv[:, tt, :],
                                       tmt(B2, tt)[:, half * 512:(half + 1) * 512],
                                       start=(tt == 0), stop=(tt == NT - 1))
                b.inc(sTE, ins)
            block.tensor(mmx)
            nmx = b.n(sTE)

            def evx(s, p16=p16, pbank=pbank, nmx=nmx):
                b.wait(s, sTE, nmx)
                gate(s, ("xstg", p16 % 2))
                gate(s, ("w", id(W0 if p16 % 2 == 0 else W1)))
                b.inc(sSC, s.copy(W0 if p16 % 2 == 0 else W1, pbank))
            bscal(evx)
            setg(("bank", id(pbank)), sSC)
            setg(("w", id(W0 if p16 % 2 == 0 else W1)), sSC)
            nex = b.n(sSC)

            wsem3 = sSPw0 if (p16 % 2 == 0) else sSPw1

            def spx(s, p16=p16, nex=nex, wsem3=wsem3):
                b.wait(s, sSC, nex)
                b.inc(wsem3, s.dma_start(out=rsi[p16 * 128:(p16 + 1) * 128, :],
                                         in_=W0 if p16 % 2 == 0 else W1), 16)
            bscal(spx)
            marks[("xstg", p16 % 2)] = (wsem3, b.n(wsem3))
            marks[("w", id(W0 if p16 % 2 == 0 else W1))] = (wsem3, b.n(wsem3))
            marks[("dld", p16 % 2)] = (sTE, nmx)
        n_xp = (b.n(sSPw0), b.n(sSPw1))

        def rs1(g):
            b.wait(g, sSPw0, n_xp[0])
            b.wait(g, sSPw1, n_xp[1])
            b.inc(sCC, g.collective_compute(
                "ReduceScatter", ALU.add, replica_groups=[list(range(NCORE))],
                ins=[rsi.opt()], outs=[rso.opt()]))
        block.gpsimd(rs1)

        def rs1rd(sync):
            sync.wait_ge(b.cnt[sCC][0], 2)
            b.wait(sync, sVE, n_phi_ve)
            b.wait(sync, sTE, n_phi_te)
            b.inc(sLD, sync.dma_start(
                out=XT.rearrange("p (a f) -> p a f", f=1024),
                in_=rso.rearrange("(a p) f -> p a f", p=128)), 16)
        block.sync(rs1rd)
        n_rs = b.n(sLD)

        def xt1(v):
            b.wait(v, sLD, n_rs)
            v.tensor_scalar_max(XT, XT, -33000.0)
            b.inc(sVE, v.tensor_scalar_min(XT, XT, 65000.0))
        bvec(xt1)
        n_clip = b.n(sVE)
        emit_ln(lambda i: XT[:, i * 1024:(i + 1) * 1024], 2,
                wait_list=[(sLD, n_rs), (sVE, n_clip)])
        nxt = b.n(sVE)

        def xt2(s, nxt=nxt):
            b.wait(s, sVE, nxt)
            b.inc(sSC, s.activation(XT, XT, ACT.Tanh))
        bscal(xt2)
        n_xt = b.n(sSC)

        srcsX = [XT[:, a * 1024 + dd * 128: a * 1024 + (dd + 1) * 128]
                 for dd in range(8) for a in range(2)]
        dstsX = [XTt[:, q * 512:(q + 1) * 512] for q in range(4)]
        emit_tp(srcsX, dstsX, wait_list=[(sSC, n_xt), (sCs1, n_phic[1])])
        n_xtt = b.n(sSC)

        def ldbq(sync):
            b.wait(sync, sSPc3, n_x2sp)
            b.wait(sync, sVE, n_y)
            b.inc(sCW, sync.dma_start(out=BQ, in_=biasq_d), 16)
        block.sync(ldbq)
        n_bq = b.n(sCW)

        qb = {(0, 0): PC, (0, 1): PD, (1, 0): PE_, (1, 1): PF}
        for q in range(2):
            for e in range(4):
                n_loc = q * 4 + e
                for dd in range(NCC):
                    wslot = W0 if ((q * 32 + e * 8 + dd) % 2 == 0) else W1

                    wsem = sW0d if ((q * 32 + e * 8 + dd) % 2 == 0) else sW1d

                    def ldw(sync, n_loc=n_loc, dd=dd, wslot=wslot, wsem=wsem):
                        gate(sync, ("w", id(wslot)))
                        b.inc(wsem, sync.dma_start(
                            out=wslot,
                            in_=expw[n_loc, dd * 128:(dd + 1) * 128, :]), 16)
                    block.sync(ldw)
                    nldw = b.n(wsem)

                    def mme(t, q=q, e=e, dd=dd, wslot=wslot, nldw=nldw, wsem=wsem):
                        b.wait(t, wsem, nldw)
                        b.wait(t, sSC, n_xtt)
                        if q == 0 and e == 0 and dd == 0:
                            for bank in (PC, PD, PE_, PF):
                                gate(t, ("bank", id(bank)))
                        for Hh in range(2):
                            bank = qb[(q, Hh)]
                            ins = t.matmul(
                                bank[32 * e:32 * (e + 1), 0:512],
                                XTt[:, dd * 256 + q * 128 + e * 32:
                                    dd * 256 + q * 128 + (e + 1) * 32],
                                wslot[:, Hh * 512:(Hh + 1) * 512],
                                start=(dd == 0), stop=(dd == NCC - 1),
                                tile_position=(0, 32 * e))
                        b.inc(sTE, ins)
                    block.tensor(mme)
                    setg(("w", id(wslot)), sTE)
        n_exp = b.n(sTE)

        def eve(v):
            b.wait(v, sTE, n_exp)
            b.wait(v, sCW, n_bq)
            for q in range(2):
                for Hh in range(2):
                    ins = v.tensor_add(
                        YT[:, q * 1024 + Hh * 512: q * 1024 + (Hh + 1) * 512],
                        qb[(q, Hh)][:, 0:512],
                        BQ[:, q * 1024 + Hh * 512: q * 1024 + (Hh + 1) * 512])
            b.inc(sVE, ins)
        bvec(eve)
        for bank in (PC, PD, PE_, PF):
            setg(("bank", id(bank)), sVE)
        n_yt = b.n(sVE)

        def ag2st(sync):
            b.wait(sync, sVE, n_yt)
            b.inc(sLD, sync.dma_start(
                out=ag2i.rearrange("(a p) f -> p a f", p=128),
                in_=YT.rearrange("p (a f) -> p a f", f=1024)), 16)
        block.sync(ag2st)
        n_ag2st = b.n(sLD)

        def ag2(g):
            b.wait(g, sLD, n_ag2st)
            b.inc(sCC, g.collective_compute(
                "AllGather", ALU.bypass, replica_groups=[list(range(NCORE))],
                ins=[ag2i.opt()], outs=[ag2o.opt()]))
        block.gpsimd(ag2)

        def ag2rd(sync):
            sync.wait_ge(b.cnt[sCC][0], 3)
            b.inc(sLD, sync.dma_start(
                out=B1.rearrange("p (a f) -> p a f", f=1024),
                in_=ag2o.rearrange("(a p) f -> p a f", p=128)), 16)
        block.sync(ag2rd)
        n_ag2 = b.n(sLD)

        ybanks = [
            [PA[:, 0:512], PA[:, 512:1024], PB[:, 0:512], PB[:, 512:1024]],
            [PC[:, 0:512], PD[:, 0:512], PE_[:, 0:512], PF[:, 0:512]],
        ]
        for pg in range(8):
            bks = ybanks[pg % 2]
            for np_ in range(16):
                cslot = (S0 if (np_ % 2 == 0) else S1)
                zsem = sZR0 if (np_ % 2 == 0) else sZR1

                def ldcf(sync, pg=pg, np_=np_, cslot=cslot, zsem=zsem):
                    b.wait(sync, sLD, n_ag2)
                    b.wait(sync, sCs0, n_phic[0])
                    b.wait(sync, sCs1, n_phic[1])
                    gate(sync, ("cld", np_ % 2))
                    b.inc(zsem, sync.dma_start(
                        out=cslot[:, 0:256],
                        in_=csp[np_ * 128:(np_ + 1) * 128, pg * 256:(pg + 1) * 256]), 16)
                block.sync(ldcf)
                nldc = b.n(zsem)

                def mmy(t, pg=pg, np_=np_, cslot=cslot, nldc=nldc, zsem=zsem,
                        bks=bks):
                    b.wait(t, zsem, nldc)
                    if np_ == 0:
                        gate(t, ("ybank", pg % 2))
                    for tt2 in range(2):
                        for half in range(2):
                            ins = t.matmul(
                                bks[tt2 * 2 + half],
                                cslot[:, tt2 * 128:(tt2 + 1) * 128],
                                B1[:, np_ * 1024 + half * 512: np_ * 1024 + (half + 1) * 512],
                                start=(np_ == 0), stop=(np_ == 15))
                    b.inc(sTE, ins)
                block.tensor(mmy)
                marks[("cld", np_ % 2)] = (sTE, b.n(sTE))
            nym = b.n(sTE)

            for tt2 in range(2):
                tt = pg * 2 + tt2

                def evy(v, pg=pg, tt2=tt2, tt=tt, nym=nym, bks=bks):
                    b.wait(v, sTE, nym)
                    tapwait(v)
                    v.tensor_copy(tmt(B2, tt)[:, 0:512], bks[tt2 * 2])
                    b.inc(sVE, v.tensor_copy(tmt(B2, tt)[:, 512:1024], bks[tt2 * 2 + 1]))
                bvec(evy)
            nevy = b.n(sVE)

            emit_ln(lambda i, pg=pg: tmt(B2, pg * 2 + i), 2,
                    wait_list=[(sVE, nevy)])
            nlny = b.n(sVE)
            marks[("ybank", pg % 2)] = (sVE, nlny)
            for tt2 in range(2):
                tt = pg * 2 + tt2
                zsem2 = sRG0 if (tt % 2 == 0) else sRG1

                def ldx2(sync, tt=tt, zsem2=zsem2):
                    b.wait(sync, sSPc3, n_x2sp)
                    gate(sync, ("x2res", tt % 2))
                    b.inc(zsem2, sync.dma_start(
                        out=(S0 if tt % 2 == 0 else S1)[:, 1024:2048],
                        in_=x2sp[:, tt * 1024:(tt + 1) * 1024]), 16)
                block.sync(ldx2)
                nlx2 = b.n(zsem2)

                def addo(v, tt=tt, nlx2=nlx2, nlny=nlny, zsem2=zsem2):
                    b.wait(v, zsem2, nlx2)
                    b.wait(v, sVE, nlny)
                    b.inc(sVE, v.tensor_add(
                        tmt(B2, tt), tmt(B2, tt),
                        (S0 if tt % 2 == 0 else S1)[:, 1024:2048]))
                bvec(addo)
                marks[("x2res", tt % 2)] = (sVE, b.n(sVE))
                nado = b.n(sVE)

                def sto(s, tt=tt, nado=nado):
                    b.wait(s, sVE, nado)
                    b.inc(sSPo, s.dma_start(
                        out=o_out[tt * 128:(tt + 1) * 128, :], in_=tmt(B2, tt)), 16)
                bscal(sto)
        n_store = b.n(sSPo)
        n_taps = b.n(sSPt)

        def fin(sync):
            b.wait(sync, sSPo, n_store)
            if n_taps:
                b.wait(sync, sSPt, n_taps)
        block.sync(fin)

    return nc, d_dbg


_NC_CACHE = {}


def _get_nc(dbg=()):
    key = tuple(sorted(dbg))
    if key not in _NC_CACHE:
        _NC_CACHE[key] = build_nc(dbg)
    return _NC_CACHE[key]


def make_inmaps(inputs, dbg=()):
    x = np.ascontiguousarray(np.asarray(inputs["L0_attn_x"], dtype=np.float32)[0])

    def t(a):
        return np.ascontiguousarray(np.asarray(a, dtype=np.float32).T)

    def p8(a):
        return np.ascontiguousarray(
            np.asarray(a, dtype=np.float32).reshape(NCC, 128).T)

    base = {
        "wvT": t(inputs["L0_attn_att_Wv"]),
        "wkT": t(inputs["L0_attn_att_Wk"]),
        "wrT": t(inputs["L0_attn_att_Wr"]),
        "woT": t(inputs["L0_attn_att_Wo"]),
        "fwkT": t(inputs["L0_attn_ffn_Wk"]),
        "fwvT": t(inputs["L0_attn_ffn_Wv"]),
        "fwrT": t(inputs["L0_attn_ffn_Wr"]),
        "phiT": t(inputs["L0_ffn_phi_w"]),
        "phib": np.ascontiguousarray(np.broadcast_to(
            np.asarray(inputs["L0_ffn_phi_b"], dtype=np.float32), (128, E2))),
        "md": np.ascontiguousarray(
            np.repeat(np.eye(64, dtype=np.float32), 32, axis=1)),
        "mc": np.ascontiguousarray(
            np.tile(np.eye(32, dtype=np.float32), (1, 64))),
        "iden": np.eye(128, dtype=np.float32),
        "ln1g": p8(inputs["L0_attn_ln1_g"]),
        "ln1b": p8(inputs["L0_attn_ln1_b"]),
        "ln2g": p8(inputs["L0_attn_ln2_g"]),
        "ln2b": p8(inputs["L0_attn_ln2_b"]),
        "tmk": p8(inputs["L0_attn_time_mix_k"]),
        "tmv": p8(inputs["L0_attn_time_mix_v"]),
        "tmr": p8(inputs["L0_attn_time_mix_r"]),
        "cmk": p8(inputs["L0_attn_cm_k"]),
        "cmr": p8(inputs["L0_attn_cm_r"]),
        "decay": p8(inputs["L0_attn_time_decay"]),
        "u8": p8(inputs["L0_attn_time_first"]),
    }
    expw_f = np.asarray(inputs["L0_ffn_expert_w"], dtype=np.float32)
    expb_f = np.asarray(inputs["L0_ffn_expert_b"], dtype=np.float32)

    in_maps = []
    for m in range(NCORE):
        im = dict(base)
        xs = np.zeros((TC + 1, C), dtype=np.float32)
        xs[1:] = x[m * TC:(m + 1) * TC]
        if m > 0:
            xs[0] = x[m * TC - 1]
        im["x_sl"] = xs
        im["expw"] = np.ascontiguousarray(expw_f[m * NEC:(m + 1) * NEC])
        bq = np.zeros((128, 2 * C), dtype=np.float32)
        for q in range(2):
            for e in range(4):
                n_loc = q * 4 + e
                bq[e * 32:(e + 1) * 32, q * C:(q + 1) * C] = \
                    expb_f[m * NEC + n_loc][None, :]
        im["biasq"] = bq
        ms = np.zeros((128, NCORE), dtype=np.float32)
        mp = np.full((128, NCORE), NEG, dtype=np.float32)
        ms[:, :m] = 1.0
        mp[:, :m] = 0.0
        im["maskS"], im["maskP"] = ms, mp
        m2s = np.zeros((128, NCORE), dtype=np.float32)
        m2p = np.full((128, NCORE), NEG, dtype=np.float32)
        m2s[:, :max(m - 1, 0)] = 1.0
        m2p[:, :max(m - 1, 0)] = 0.0
        im["m2S"], im["m2P"] = m2s, m2p
        m3s = np.zeros((128, NCORE), dtype=np.float32)
        m3p = np.full((128, NCORE), NEG, dtype=np.float32)
        if m > 0:
            m3s[:, m - 1] = 1.0
            m3p[:, m - 1] = 0.0
        im["m3S"], im["m3P"] = m3s, m3p
        im["c0m"] = np.full((128, 1), 0.0 if m == 0 else 1.0, dtype=np.float32)
        in_maps.append(im)
    return in_maps


def kernel(**inputs):
    nc, _ = _get_nc()
    in_maps = make_inmaps(inputs)
    res = run_bass_kernel_spmd(nc, in_maps, core_ids=list(range(NCORE)))
    out = np.concatenate([res.results[m]["o_out"] for m in range(NCORE)], axis=0)
    return out[None, :, :].astype(np.float32)


# version 13
# speedup vs baseline: 1.0088x; 1.0088x over previous
import numpy as np
import concourse.bass as bass
import concourse.mybir as mybir
from concourse.bass_utils import run_bass_kernel_spmd

F32 = mybir.dt.float32
ALU = mybir.AluOpType
ACT = mybir.ActivationFunctionType
AX = mybir.AxisListType

NCORE = 8
T = 16384
TC = 2048
C = 1024
NCC = 8
NT = 16
H4 = 4096
NEC = 8
E2 = 2048
EPS = 1e-5
CLAMP = 80.0
NEG = -1e38


class Bld:
    def __init__(self, nc, block):
        self.nc = nc
        self.block = block
        self.cnt = {}

    def sem(self, name):
        self.cnt[name] = [self.nc.alloc_semaphore(name), 0]
        return name

    def inc(self, name, ins, k=1):
        ins.then_inc(self.cnt[name][0], k)
        self.cnt[name][1] += k
        return ins

    def n(self, name):
        return self.cnt[name][1]

    def wait(self, eng, name, val=None):
        h, v = self.cnt[name]
        eng.wait_ge(h, v if val is None else val)


class AutoDrain:
    _PASS = {"wait_ge", "drain", "engine_nop", "sem_inc", "nop", "wait_op"}

    def __init__(self, eng, state):
        self._e = eng
        self._st = state

    @staticmethod
    def _span(ap):
        try:
            name = ap.name
            name = name() if callable(name) else name
        except Exception:
            return None
        try:
            off = int(ap.offset)
            reach = 1
            for st, ct in ap.ap:
                reach += abs(int(st)) * (int(ct) - 1)
            return (str(name), off, off + reach)
        except Exception:
            return (str(name), 0, 1 << 60)

    @staticmethod
    def _hits(spans, lst):
        return any(s[0] == wn and s[1] < whi and wlo < s[2]
                   for s in spans for (wn, wlo, whi) in lst)

    def __getattr__(self, nm):
        fn = getattr(self._e, nm)
        if nm in self._PASS or not callable(fn):
            return fn

        def wrap(*a, **k):
            aps = [x for x in a if isinstance(x, bass.AP)]
            aps += [x for x in k.values() if isinstance(x, bass.AP)]
            outs, ins = [], []
            if aps and nm != "dma_start":
                outs = [self._span(aps[0])]
                ac = k.get("accum_out")
                if isinstance(ac, bass.AP):
                    outs.append(self._span(ac))
                ins = [self._span(x) for x in aps[1:] if not isinstance(x, type(None))]
            else:
                ins = [self._span(x) for x in aps]
            outs = [s for s in outs if s is not None]
            ins = [s for s in ins if s is not None]
            need = (self._hits(outs, self._st["r"]) or
                    self._hits(outs, self._st["w"]) or
                    self._hits(ins, self._st["w"]))
            if need:
                self._e.drain()
                self._st["w"] = []
                self._st["r"] = []
            res = fn(*a, **k)
            self._st["w"].extend(outs)
            self._st["r"].extend(ins)
            return res
        return wrap


def build_nc(dbg=()):
    dbg = set(dbg)
    nc = bass.Bass("TRN2", target_bir_lowering=False,
                   dynamic_dma_scratch_size=4096)

    def di(name, shape):
        return nc.dram_tensor(name, list(shape), F32, kind="ExternalInput").ap()

    x_sl = di("x_sl", (TC + 1, C))
    wvT = di("wvT", (C, C))
    wkT = di("wkT", (C, C))
    wrT = di("wrT", (C, C))
    woT = di("woT", (C, C))
    fwkT = di("fwkT", (C, H4))
    fwvT = di("fwvT", (H4, C))
    fwrT = di("fwrT", (C, C))
    phiT_d = di("phiT", (C, E2))
    phib_d = di("phib", (128, E2))
    expw = di("expw", (NEC, C, C))
    biasq_d = di("biasq", (128, 2 * C))
    md_d = di("md", (64, E2))
    mc_d = di("mc", (32, E2))
    iden = di("iden", (128, 128))
    pn = ["ln1g", "ln1b", "ln2g", "ln2b", "tmk", "tmv", "tmr", "cmk", "cmr",
          "decay", "u8"]
    p_in = {nm: di(nm, (128, NCC)) for nm in pn}
    mk_nm = ["maskS", "maskP", "m2S", "m2P", "m3S", "m3P"]
    mk_in = {nm: di(nm, (128, NCORE)) for nm in mk_nm}
    c0m = di("c0m", (128, 1))

    o_out = nc.dram_tensor("o_out", [TC, C], F32, kind="ExternalOutput").ap()

    z0sp = nc.dram_tensor("z0sp", [128, NCC * TC], F32).ap()
    ksp = nc.dram_tensor("ksp", [128, NCC * TC], F32).ap()
    vsp = nc.dram_tensor("vsp", [128, NCC * TC], F32).ap()
    rsp = nc.dram_tensor("rsp", [128, NCC * TC], F32).ap()
    xasp = nc.dram_tensor("xasp", [128, NCC * TC], F32).ap()
    sgsp = nc.dram_tensor("sgsp", [128, NCC * TC], F32).ap()
    x2sp = nc.dram_tensor("x2sp", [128, NT * C], F32).ap()
    dsp = nc.dram_tensor("dsp", [TC, E2], F32).ap()
    csp = nc.dram_tensor("csp", [E2, TC], F32).ap()
    ag1i = nc.dram_tensor("ag1i", [128, 72], F32).ap()
    ag1o = nc.dram_tensor("ag1o", [128 * NCORE, 72], F32).ap()
    rsi = nc.dram_tensor("rsi", [E2, C], F32).ap()
    rso = nc.dram_tensor("rso", [E2 // NCORE, C], F32).ap()
    ag2i = nc.dram_tensor("ag2i", [E2 // NCORE, C], F32).ap()
    ag2o = nc.dram_tensor("ag2o", [E2, C], F32).ap()

    B1 = nc.alloc_sbuf_tensor("B1", [128, 16384], F32).ap()
    B2 = nc.alloc_sbuf_tensor("B2", [128, 16384], F32).ap()
    B3 = nc.alloc_sbuf_tensor("B3", [128, 16384], F32).ap()
    S0 = nc.alloc_sbuf_tensor("S0", [128, 2048], F32).ap()
    S1 = nc.alloc_sbuf_tensor("S1", [128, 2048], F32).ap()
    W0 = nc.alloc_sbuf_tensor("W0", [128, 1024], F32).ap()
    W1 = nc.alloc_sbuf_tensor("W1", [128, 1024], F32).ap()

    sm = {}

    def asm(name, shape):
        sm[name] = nc.alloc_sbuf_tensor("s_" + name, list(shape), F32).ap()
        return sm[name]

    IDN = asm("IDN", (128, 128))
    prm = {nm: asm("p_" + nm, (128, NCC)) for nm in pn}
    w8 = asm("w8", (128, NCC))
    tw8 = asm("tw8", (128, NCC))
    tw47 = asm("tw47", (128, NCC))
    MK = {nm: asm("m_" + nm, (128, NCORE)) for nm in mk_nm}
    C0M = asm("C0M", (128, 1))
    ONESC = asm("ONESC", (128, 1))
    z0b = asm("z0b", (128, NCC))
    xab = asm("xab", (128, NCC))
    xcb = asm("xcb", (128, NCC))
    stat = asm("stat", (128, 96))
    stat1 = asm("stat1", (1, 24))
    sS = {nm: asm(nm, (128, NCC)) for nm in
          ["PLe", "ALe", "BLe", "PLp", "ALp", "BLp", "KL", "VL", "RL"]}
    ppin = asm("ppin", (128, NCC))
    aain = asm("aain", (128, NCC))
    bbin = asm("bbin", (128, NCC))
    f = {nm: asm(nm, (128, NCC)) for nm in
         ["f1", "f2", "f3", "f4", "f5", "f6", "f7", "fA", "fB", "fP",
          "bk8", "bv8", "br8", "bm2x"]}

    PA = nc.alloc_psum_tensor("PA", [128, 1024], F32).ap()
    PB = nc.alloc_psum_tensor("PB", [128, 1024], F32).ap()
    PC = nc.alloc_psum_tensor("PC", [128, 512], F32).ap()
    PD = nc.alloc_psum_tensor("PD", [128, 512], F32).ap()
    PE_ = nc.alloc_psum_tensor("PE_", [128, 512], F32).ap()
    PF = nc.alloc_psum_tensor("PF", [128, 512], F32).ap()

    def sl(buf, i, w=2048):
        return buf[:, i * w:(i + 1) * w]

    def tmt(buf, tt):
        return buf[:, tt * 1024:(tt + 1) * 1024]

    VBUF = B1[:, 0:2048]
    R_W = B1[:, 2048:4096]
    R_PL = B1[:, 4096:6145]
    R_AL = B1[:, 6145:8194]
    R_BL = B1[:, 8194:10243]
    R_T2 = B1[:, 10243:12291]
    R_T3 = B1[:, 12291:14339]
    ROWB = W1[0:1, 0:1024]
    PHIB = B3[:, 0:2048]
    MDs = B3[0:64, 2048:4096]
    MCs = B3[0:32, 4096:6144]
    CSTG = B3[:, 6144:8192]
    XT = B3[:, 8192:10240]
    XTt = B3[:, 10240:12288]
    YT = B3[:, 12288:14336]
    BQ = B3[:, 14336:16384]
    agbuf = B3[:, 8192:8768].rearrange("p (r f) -> p r f", f=72)
    rDT = asm("rDT", (64, 128))
    rCT = asm("rCT", (32, 128))
    sD = stat[:, 0:64]
    sCt = stat[:, 64:96]
    ONESR = S0[0:1, 1024:1152]

    d_dbg = {}

    def mktap(name, shape):
        d_dbg[name] = nc.dram_tensor("dbg_" + name, list(shape), F32,
                                     kind="ExternalOutput").ap()
        return d_dbg[name]

    with nc.Block() as block:
        b = Bld(nc, block)

        _wve = {"w": [], "r": []}
        _wsc = {"w": [], "r": []}

        def bvec(fn):
            block.vector(lambda e: fn(AutoDrain(e, _wve)))

        def bscal(fn):
            block.scalar(lambda e: fn(AutoDrain(e, _wsc)))
        sLD = b.sem("sLD")
        sSP = b.sem("sSP")
        sTE = b.sem("sTE")
        sVE = b.sem("sVE")
        sSC = b.sem("sSC")
        sGP = b.sem("sGP")
        sCC = b.sem("sCC")
        sW0d = b.sem("sW0d")
        sW1d = b.sem("sW1d")
        sRG0 = b.sem("sRG0")
        sRG1 = b.sem("sRG1")
        sZR0 = b.sem("sZR0")
        sZR1 = b.sem("sZR1")
        sZR2 = b.sem("sZR2")
        sZR3 = b.sem("sZR3")
        sCW = b.sem("sCW")
        sSPb = b.sem("sSPb")
        sSP0 = b.sem("sSP0")
        sSP1 = b.sem("sSP1")
        sSPa = b.sem("sSPa")
        sSPc3 = b.sem("sSPc3")
        sSPw0 = b.sem("sSPw0")
        sSPw1 = b.sem("sSPw1")
        sSPo = b.sem("sSPo")
        sVs0 = b.sem("sVs0"); sVs1 = b.sem("sVs1")
        sKs0 = b.sem("sKs0"); sKs1 = b.sem("sKs1")
        sRs0 = b.sem("sRs0"); sRs1 = b.sem("sRs1")
        sDsp = b.sem("sDsp")
        sCs0 = b.sem("sCs0"); sCs1 = b.sem("sCs1")
        sSPt = b.sem("sSPt")
        marks = {}

        def gate(eng, key):
            if key in marks:
                nm, v = marks[key]
                b.wait(eng, nm, v)

        def setg(key, nm):
            marks[key] = (nm, b.n(nm))

        def tapwait(eng):
            if b.n(sSPt):
                b.wait(eng, sSPt, b.n(sSPt))

        def emit_ln(get_tile, ntile, out_tile=None, wait_list=()):
            P0 = get_tile(0).shape[0]
            if P0 != 128:
                _emit_ln_one(get_tile(0), get_tile(0) if out_tile is None
                             else out_tile(0), wait_list)
                return
            for i in range(ntile):
                tile = get_tile(i)
                F = tile.shape[-1]
                c = i * 6

                def v1(v, tile=tile, F=F, c=c, i=i, wl=tuple(wait_list)):
                    if i == 0:
                        for (nm, val) in wl:
                            b.wait(v, nm, val)
                        gate(v, ("w", id(W0)))
                    v.reduce_sum(stat[:, c:c + 1], tile, AX.X)
                    v.scalar_tensor_tensor(W0[:, 0:F], tile, 1.0, tile,
                                           ALU.mult, ALU.mult,
                                           accum_out=stat[:, c + 1:c + 2])
                    v.tensor_scalar_mul(stat[:, c + 2:c + 3], stat[:, c:c + 1], 1.0 / F)
                    v.tensor_scalar_mul(stat[:, c + 3:c + 4], stat[:, c + 1:c + 2], 1.0 / F)
                    v.scalar_tensor_tensor(stat[:, c + 4:c + 5], stat[:, c + 2:c + 3],
                                           stat[:, c + 2:c + 3], stat[:, c + 3:c + 4],
                                           ALU.mult, ALU.subtract)
                    ins = v.tensor_scalar(stat[:, c + 4:c + 5], stat[:, c + 4:c + 5],
                                          -1.0, EPS, ALU.mult, ALU.add)
                    if i == ntile - 1:
                        b.inc(sVE, ins)
                bvec(v1)
            setg(("w", id(W0)), sVE)
            nve = b.n(sVE)

            def s2(s, nve=nve, ntile=ntile):
                b.wait(s, sVE, nve)
                for i in range(ntile):
                    c = i * 6
                    ins = s.activation(stat[:, c + 5:c + 6], stat[:, c + 4:c + 5],
                                       ACT.Sqrt)
                b.inc(sSC, ins)
            bscal(s2)
            nsc = b.n(sSC)

            for i in range(ntile):
                tile = get_tile(i)
                dst = tile if out_tile is None else out_tile(i)
                c = i * 6

                def v3(v, tile=tile, dst=dst, c=c, i=i, nsc=nsc):
                    if i == 0:
                        b.wait(v, sSC, nsc)
                    v.reciprocal(stat[:, c:c + 1], stat[:, c + 5:c + 6])
                    ins = v.tensor_scalar(dst, tile, stat[:, c + 2:c + 3],
                                          stat[:, c:c + 1], ALU.subtract, ALU.mult)
                    if i == ntile - 1:
                        b.inc(sVE, ins)
                bvec(v3)

        def _emit_ln_one(tile, dst, wait_list):
            F = tile.shape[-1]
            P = tile.shape[0]
            st = stat1

            def v1(v, wl=tuple(wait_list)):
                for (nm, val) in wl:
                    b.wait(v, nm, val)
                gate(v, ("w", id(W0)))
                v.reduce_sum(st[:, 0:1], tile, AX.X)
                v.scalar_tensor_tensor(W0[0:P, 0:F], tile, 1.0, tile,
                                       ALU.mult, ALU.mult, accum_out=st[:, 1:2])
                v.tensor_scalar_mul(st[:, 2:3], st[:, 0:1], 1.0 / F)
                v.tensor_scalar_mul(st[:, 3:4], st[:, 1:2], 1.0 / F)
                v.scalar_tensor_tensor(st[:, 4:5], st[:, 2:3], st[:, 2:3],
                                       st[:, 3:4], ALU.mult, ALU.subtract)
                b.inc(sVE, v.tensor_scalar(st[:, 4:5], st[:, 4:5], -1.0, EPS,
                                           ALU.mult, ALU.add))
            bvec(v1)
            setg(("w", id(W0)), sVE)
            nve1 = b.n(sVE)

            def s2(s, nve1=nve1):
                b.wait(s, sVE, nve1)
                b.inc(sSC, s.activation(st[:, 5:6], st[:, 4:5], ACT.Sqrt))
            bscal(s2)
            nsc2 = b.n(sSC)

            def v3(v, nsc2=nsc2):
                b.wait(v, sSC, nsc2)
                v.reciprocal(st[:, 6:7], st[:, 5:6])
                b.inc(sVE, v.tensor_scalar(dst, tile, st[:, 2:3],
                                           st[:, 6:7], ALU.subtract, ALU.mult))
            bvec(v3)

        tps = [0]

        def emit_tp(srcs, dsts512, wait_list=()):
            nb = (len(srcs) + 3) // 4
            assert len(dsts512) == nb
            for bb_ in range(nb):
                bank = PE_ if (tps[0] % 2 == 0) else PF
                tps[0] += 1
                lo, hi = bb_ * 4, min(bb_ * 4 + 4, len(srcs))

                def t1(t, lo=lo, hi=hi, bank=bank, wl=tuple(wait_list)):
                    for (nm, v) in wl:
                        b.wait(t, nm, v)
                    gate(t, ("bank", id(bank)))
                    for j in range(lo, hi):
                        ins = t.transpose(bank[:, (j - lo) * 128:(j - lo + 1) * 128],
                                          srcs[j], IDN)
                    b.inc(sTE, ins)
                block.tensor(t1)
                nte = b.n(sTE)

                def s1(s, bank=bank, bb_=bb_, nte=nte, lo=lo, hi=hi):
                    b.wait(s, sTE, nte)
                    b.inc(sSC, s.copy(dsts512[bb_][:, 0:(hi - lo) * 128],
                                      bank[:, 0:(hi - lo) * 128]))
                bscal(s1)
                setg(("bank", id(bank)), sSC)

        def p0_loads(sync):
            for nm in pn:
                b.inc(sLD, sync.dma_start(out=prm[nm], in_=p_in[nm]), 16)
            for nm in mk_nm:
                b.inc(sLD, sync.dma_start(out=MK[nm], in_=mk_in[nm]), 16)
            b.inc(sLD, sync.dma_start(out=IDN, in_=iden), 16)
            b.inc(sLD, sync.dma_start(out=C0M, in_=c0m), 16)
            b.inc(sLD, sync.dma_start(
                out=B1.rearrange("p (a f) -> p a f", f=1024),
                in_=x_sl[1:TC + 1, :].rearrange("(a p) f -> p a f", p=128)), 16)
            b.inc(sLD, sync.dma_start(out=ROWB, in_=x_sl[0:1, :]), 16)
        block.sync(p0_loads)
        n_ld0 = b.n(sLD)

        def p0_ones(v):
            b.wait(v, sLD, n_ld0)
            b.inc(sVE, v.memset(ONESC, 1.0))
        bvec(p0_ones)

        def p0_w(s):
            b.wait(s, sLD, n_ld0)
            b.inc(sSC, s.activation(w8, prm["decay"], ACT.Exp))
        bscal(p0_w)
        n_w = b.n(sSC)

        def p0_w2(v):
            b.wait(v, sSC, n_w)
            v.tensor_scalar_mul(w8, w8, -1.0)
            v.tensor_scalar_mul(tw8, w8, float(TC))
            b.inc(sVE, v.tensor_scalar_mul(tw47, w8, float(TC - 1)))
        bvec(p0_w2)

        emit_ln(lambda i: tmt(B1, i), NT)
        emit_ln(lambda i: ROWB, 1)
        setg(("w", id(W1)), sVE)
        n_z0 = b.n(sVE)

        for i in range(NCC):
            srcs = [tmt(B1, tt)[:, i * 128:(i + 1) * 128] for tt in range(NT)]
            dsts = [B2[:, i * 2048 + q * 512: i * 2048 + (q + 1) * 512]
                    for q in range(4)]
            emit_tp(srcs, dsts, wait_list=[(sVE, n_z0)])
        n_z0fm = b.n(sSC)

        def p0_bt(t):
            b.wait(t, sVE, n_z0)
            gate(t, ("bank", id(PE_)))
            for i in range(NCC):
                ins = t.transpose(PE_[:, i:i + 1], ROWB[:, i * 128:(i + 1) * 128],
                                  IDN[0:1, 0:1])
            b.inc(sTE, ins)
        block.tensor(p0_bt)
        n_bt = b.n(sTE)

        def p0_bt2(s):
            b.wait(s, sTE, n_bt)
            b.inc(sSC, s.copy(z0b, PE_[:, 0:NCC]))
        bscal(p0_bt2)
        setg(("bank", id(PE_)), sSC)
        setg(("w", id(W1)), sTE)
        n_z0b = b.n(sSC)

        def p0_spill(s):
            b.wait(s, sSC, n_z0fm)
            b.inc(sSPb, s.dma_start(out=z0sp, in_=B2), 16)
        bscal(p0_spill)
        n_z0sp = b.n(sSPb)

        if "z0fm" in dbg:
            tap = mktap("z0fm", (128, 16384))

            def dt0(s, tap=tap):
                b.wait(s, sSC, n_z0fm)
                b.inc(sSPt, s.dma_start(out=tap, in_=B2), 16)
            bscal(dt0)
            setg(("tap", id(B2)), sSPt)

        def p0_xa(v):
            b.wait(v, sSC, n_z0fm)
            for i in range(NCC):
                v.tensor_scalar(sl(B3, i), sl(B2, i), prm["ln1g"][:, i:i + 1],
                                prm["ln1b"][:, i:i + 1], ALU.mult, ALU.add)
            b.wait(v, sSC, n_z0b)
            v.tensor_mul(xab, z0b, prm["ln1g"])
            v.tensor_add(xab, xab, prm["ln1b"])
            b.inc(sVE, v.tensor_scalar_mul(xab, xab, C0M[:, 0:1]))
        bvec(p0_xa)
        n_xa = b.n(sVE)

        def pa_mix(v):
            b.wait(v, sVE, n_xa)
            b.wait(v, sSPb, n_z0sp)
            tapwait(v)
            for i in range(NCC):
                xa_i = sl(B3, i)
                v.tensor_sub(S0[:, 1:2048], xa_i[:, 1:2048], xa_i[:, 0:2047])
                v.tensor_sub(S0[:, 0:1], xa_i[:, 0:1], xab[:, i:i + 1])
                for (buf, tnm) in ((B1, "tmv"), (B2, "tmr"), (B3, "tmk")):
                    dst = sl(buf, i)
                    v.scalar_tensor_tensor(dst[:, 1:2048], S0[:, 1:2048],
                                           prm[tnm][:, i:i + 1], xa_i[:, 0:2047],
                                           ALU.mult, ALU.add)
                    ins = v.scalar_tensor_tensor(dst[:, 0:1], S0[:, 0:1],
                                                 prm[tnm][:, i:i + 1],
                                                 xab[:, i:i + 1], ALU.mult, ALU.add)
                b.inc(sVE, ins)
        bvec(pa_mix)
        n_mix = b.n(sVE)

        def proj(pnm, wT, mixbuf, spdst, lastc, ssems):
            for j in range(NCC):
                wslot = W0 if (j % 2 == 0) else W1

                wsem = sW0d if (j % 2 == 0) else sW1d

                def ld(sync, wT=wT, j=j, wslot=wslot, wsem=wsem):
                    gate(sync, ("w", id(wslot)))
                    b.inc(wsem, sync.dma_start(
                        out=wslot.rearrange("p (a f) -> p a f", f=128),
                        in_=wT[:, j * 128:(j + 1) * 128]
                            .rearrange("(a p) f -> p a f", p=128)), 16)
                block.sync(ld)
                nld = b.n(wsem)
                sslot = S0 if (j % 2 == 0) else S1
                for tb in range(4):
                    bank = PC if (tb % 2 == 0) else PD

                    wsem2 = sW0d if (j % 2 == 0) else sW1d

                    def mm(t, j=j, tb=tb, wslot=wslot, mixbuf=mixbuf, bank=bank,
                           nld=nld, sslot=sslot, wsem2=wsem2):
                        b.wait(t, wsem2, nld)
                        b.wait(t, sVE, n_mix)
                        gate(t, ("bank", id(bank)))
                        if tb == 0:
                            gate(t, ("spill", id(sslot)))
                            gate(t, ("lc", id(sslot)))
                        wv = wslot.rearrange("p (a f) -> p a f", f=128)
                        for ci in range(NCC):
                            ins = t.matmul(bank[:, 0:512], wv[:, ci, :],
                                           sl(mixbuf, ci)[:, tb * 512:(tb + 1) * 512],
                                           start=(ci == 0), stop=(ci == NCC - 1))
                        b.inc(sTE, ins)
                    block.tensor(mm)
                    setg(("w", id(wslot)), sTE)
                    nte = b.n(sTE)

                    def ev(s, tb=tb, bank=bank, nte=nte, sslot=sslot):
                        b.wait(s, sTE, nte)
                        b.inc(sSC, s.copy(sslot[:, tb * 512:(tb + 1) * 512],
                                          bank[:, 0:512]))
                    bscal(ev)
                    setg(("bank", id(bank)), sSC)
                nev = b.n(sSC)

                def lc(v, j=j, sslot=sslot, lastc=lastc, nev=nev):
                    b.wait(v, sSC, nev)
                    b.inc(sVE, v.tensor_copy(lastc[:, j:j + 1], sslot[:, 2047:2048]))
                bvec(lc)
                setg(("lc", id(sslot)), sVE)

                ssem = ssems[j % 2]

                def spill(s, j=j, sslot=sslot, spdst=spdst, nev=nev, ssem=ssem):
                    b.wait(s, sSC, nev)
                    b.inc(ssem, s.dma_start(out=spdst[:, j * 2048:(j + 1) * 2048],
                                            in_=sslot), 16)
                bscal(spill)
                setg(("spill", id(sslot)), ssem)

        proj("v", wvT, B1, vsp, sS["VL"], (sVs0, sVs1))
        n_v_sp = (b.n(sVs0), b.n(sVs1))

        proj("r", wrT, B2, rsp, sS["RL"], (sRs0, sRs1))
        n_r_sp = (b.n(sRs0), b.n(sRs1))
        n_r_lc = b.n(sVE)

        ZT = B2[:, 12288:14336]

        def zts(s):
            b.wait(s, sRs0, n_r_sp[0])
            b.wait(s, sRs1, n_r_sp[1])
            b.inc(sSC, s.memzero(ZT))
        bscal(zts)
        nprev1 = n_mix
        for j in range(NCC):
            wslot = W0 if (j % 2 == 0) else W1
            wsem = sW0d if (j % 2 == 0) else sW1d
            sslot = S0 if (j % 2 == 0) else S1
            ksem = sKs0 if (j % 2 == 0) else sKs1

            def ldw(sync, j=j, wslot=wslot, wsem=wsem):
                gate(sync, ("w", id(wslot)))
                b.inc(wsem, sync.dma_start(
                    out=wslot.rearrange("p (a f) -> p a f", f=128),
                    in_=wkT[:, j * 128:(j + 1) * 128]
                        .rearrange("(a p) f -> p a f", p=128)), 16)
            block.sync(ldw)
            nld = b.n(wsem)

            def ldv(sync, j=j):
                b.wait(sync, sVs0, n_v_sp[0])
                b.wait(sync, sVs1, n_v_sp[1])
                gate(sync, ("vbuf",))
                b.inc(sRG0, sync.dma_start(out=VBUF,
                                           in_=vsp[:, j * 2048:(j + 1) * 2048]), 16)
            block.sync(ldv)
            nldv = b.n(sRG0)

            for tb in range(4):
                bank = PC if (tb % 2 == 0) else PD

                def mm(t, j=j, tb=tb, wslot=wslot, bank=bank, nld=nld, wsem=wsem,
                       sslot=sslot):
                    b.wait(t, wsem, nld)
                    b.wait(t, sVE, n_mix)
                    gate(t, ("bank", id(bank)))
                    if tb == 0:
                        gate(t, ("spill", id(sslot)))
                        gate(t, ("lc", id(sslot)))
                        gate(t, ("pscan", id(sslot)))
                    wv = wslot.rearrange("p (a f) -> p a f", f=128)
                    for ci in range(NCC):
                        ins = t.matmul(bank[:, 0:512], wv[:, ci, :],
                                       sl(B3, ci)[:, tb * 512:(tb + 1) * 512],
                                       start=(ci == 0), stop=(ci == NCC - 1))
                    b.inc(sTE, ins)
                block.tensor(mm)
                setg(("w", id(wslot)), sTE)
                nte = b.n(sTE)

                def ev(s, tb=tb, bank=bank, nte=nte, sslot=sslot):
                    b.wait(s, sTE, nte)
                    b.inc(sSC, s.copy(sslot[:, tb * 512:(tb + 1) * 512],
                                      bank[:, 0:512]))
                bscal(ev)
                setg(("bank", id(bank)), sSC)
            nev = b.n(sSC)

            def lc(v, j=j, sslot=sslot, nev=nev):
                b.wait(v, sSC, nev)
                b.inc(sVE, v.tensor_copy(sS["KL"][:, j:j + 1], sslot[:, 2047:2048]))
            bvec(lc)
            setg(("lc", id(sslot)), sVE)

            def spill(s, j=j, sslot=sslot, nev=nev, ksem=ksem):
                b.wait(s, sSC, nev)
                b.inc(ksem, s.dma_start(out=ksp[:, j * 2048:(j + 1) * 2048],
                                        in_=sslot), 16)
            bscal(spill)
            setg(("spill", id(sslot)), ksem)

            def s0p(s, j=j, nprev1=nprev1):
                b.wait(s, sVE, nprev1)
                b.inc(sSC, s.add(R_W, ZT, w8[:, j:j + 1]))
            bscal(s0p)
            ns0p = b.n(sSC)

            def v1(v, j=j, sslot=sslot, ns0p=ns0p, nldv=nldv):
                b.wait(v, sSC, ns0p)
                b.wait(v, sRG0, nldv)
                v.memset(R_PL[:, 0:1], NEG)
                v.tensor_tensor_scan(R_PL[:, 1:2049], R_W, sslot, NEG, ALU.add, ALU.max)
                b.inc(sVE, v.tensor_sub(R_T2, R_PL[:, 0:2048], R_PL[:, 1:2049]))
            bvec(v1)
            nv1 = b.n(sVE)
            nprev1 = nv1

            def s1(s, j=j, nv1=nv1):
                b.wait(s, sVE, nv1)
                b.inc(sSC, s.activation(R_T2, R_T2, ACT.Exp, bias=w8[:, j:j + 1]))
            bscal(s1)
            ns1 = b.n(sSC)

            def v2(v, sslot=sslot, ns1=ns1):
                b.wait(v, sSC, ns1)
                b.inc(sVE, v.tensor_sub(R_T3, sslot, R_PL[:, 1:2049]))
            bvec(v2)
            nv2 = b.n(sVE)

            def s2(s, nv2=nv2):
                b.wait(s, sVE, nv2)
                b.inc(sSC, s.activation(R_T3, R_T3, ACT.Exp))
            bscal(s2)
            ns2 = b.n(sSC)

            def v3(v, j=j, ns2=ns2):
                b.wait(v, sSC, ns2)
                v.tensor_tensor_scan(R_BL[:, 1:2049], R_T2, R_T3, 0.0, ALU.mult, ALU.add)
                v.tensor_mul(R_T3, R_T3, VBUF)
                v.tensor_tensor_scan(R_AL[:, 1:2049], R_T2, R_T3, 0.0, ALU.mult, ALU.add)
                v.tensor_copy(sS["PLe"][:, j:j + 1], R_PL[:, 2048:2049])
                v.tensor_copy(sS["ALe"][:, j:j + 1], R_AL[:, 2048:2049])
                v.tensor_copy(sS["BLe"][:, j:j + 1], R_BL[:, 2048:2049])
                v.tensor_copy(sS["PLp"][:, j:j + 1], R_PL[:, 2047:2048])
                v.tensor_copy(sS["ALp"][:, j:j + 1], R_AL[:, 2047:2048])
                b.inc(sVE, v.tensor_copy(sS["BLp"][:, j:j + 1], R_BL[:, 2047:2048]))
            bvec(v3)
            setg(("pscan", id(sslot)), sVE)
            setg(("vbuf",), sVE)
        n_p1 = b.n(sVE)
        n_k_sp = (b.n(sKs0), b.n(sKs1))

        def ag1st(sync):
            b.wait(sync, sVE, max(n_p1, n_r_lc))
            cols = [("PLe", 0), ("ALe", 8), ("BLe", 16), ("PLp", 24), ("ALp", 32),
                    ("BLp", 40), ("KL", 48), ("VL", 56), ("RL", 64)]
            for nm, lo in cols:
                b.inc(sLD, sync.dma_start(out=ag1i[:, lo:lo + 8], in_=sS[nm]), 16)
        block.sync(ag1st)
        n_ag1st = b.n(sLD)

        def ag1(g):
            b.wait(g, sLD, n_ag1st)
            b.inc(sCC, g.collective_compute(
                "AllGather", ALU.bypass, replica_groups=[list(range(NCORE))],
                ins=[ag1i.opt()], outs=[ag1o.opt()]))
        block.gpsimd(ag1)

        def ag1rd(sync):
            sync.wait_ge(b.cnt[sCC][0], 1)
            b.inc(sLD, sync.dma_start(out=agbuf,
                                      in_=ag1o.rearrange("(r p) f -> p r f", p=128)), 16)
        block.sync(ag1rd)
        n_ag1 = b.n(sLD)

        def emit_fold(dstP, dstA, dstB, chain2):
            def fv0(v):
                b.wait(v, sLD, n_ag1)
                v.memset(f["fP"], NEG)
                v.memset(f["fA"], 0.0)
                b.inc(sVE, v.memset(f["fB"], 0.0))
            bvec(fv0)
            for j in range(NCORE - 1):
                def fv1(v, j=j):
                    g = lambda lo: agbuf[:, j, lo:lo + 8]
                    if not chain2:
                        v.tensor_scalar_mul(f["f1"], tw8, MK["maskS"][:, j:j + 1])
                        v.tensor_scalar_add(f["f3"], g(0), MK["maskP"][:, j:j + 1])
                    else:
                        v.tensor_scalar_mul(f["f1"], tw8, MK["m2S"][:, j:j + 1])
                        v.tensor_scalar_mul(f["f2"], tw47, MK["m3S"][:, j:j + 1])
                        v.tensor_add(f["f1"], f["f1"], f["f2"])
                        v.tensor_scalar_add(f["f3"], g(24), MK["m3P"][:, j:j + 1])
                        v.tensor_scalar_add(f["f4"], g(0), MK["m2P"][:, j:j + 1])
                        v.tensor_max(f["f3"], f["f3"], f["f4"])
                    v.tensor_add(f["f1"], f["f1"], f["fP"])
                    v.tensor_max(f["f4"], f["f1"], f["f3"])
                    v.tensor_sub(f["f5"], f["f1"], f["f4"])
                    b.inc(sVE, v.tensor_sub(f["f6"], f["f3"], f["f4"]))
                bvec(fv1)
                nv = b.n(sVE)

                def fs1(s, nv=nv):
                    b.wait(s, sVE, nv)
                    s.activation(f["f5"], f["f5"], ACT.Exp)
                    b.inc(sSC, s.activation(f["f6"], f["f6"], ACT.Exp))
                bscal(fs1)
                ns = b.n(sSC)

                def fv2(v, j=j, ns=ns):
                    g = lambda lo: agbuf[:, j, lo:lo + 8]
                    b.wait(v, sSC, ns)
                    for (acc, loF, loP) in ((f["fA"], 8, 32), (f["fB"], 16, 40)):
                        if not chain2:
                            v.tensor_scalar_mul(f["f7"], g(loF), MK["maskS"][:, j:j + 1])
                        else:
                            v.tensor_scalar_mul(f["f7"], g(loP), MK["m3S"][:, j:j + 1])
                            v.tensor_scalar_mul(f["f2"], g(loF), MK["m2S"][:, j:j + 1])
                            v.tensor_add(f["f7"], f["f7"], f["f2"])
                        v.tensor_mul(acc, acc, f["f5"])
                        v.tensor_mul(f["f7"], f["f7"], f["f6"])
                        v.tensor_add(acc, acc, f["f7"])
                    b.inc(sVE, v.tensor_copy(f["fP"], f["f4"]))
                bvec(fv2)

            def fv3(v):
                v.tensor_copy(dstP, f["fP"])
                v.tensor_copy(dstA, f["fA"])
                b.inc(sVE, v.tensor_copy(dstB, f["fB"]))
            bvec(fv3)

        emit_fold(ppin, aain, bbin, False)
        emit_fold(sS["PLp"], sS["ALp"], sS["BLp"], True)
        n_folds = b.n(sVE)

        def bnd1(v):
            b.wait(v, sVE, n_folds)
            v.memset(f["bk8"], 0.0)
            v.memset(f["bv8"], 0.0)
            v.memset(f["br8"], 0.0)
            for j in range(NCORE - 1):
                for (acc, lo) in ((f["bk8"], 48), (f["bv8"], 56), (f["br8"], 64)):
                    v.tensor_scalar_mul(f["f4"], agbuf[:, j, lo:lo + 8],
                                        MK["m3S"][:, j:j + 1])
                    v.tensor_add(acc, acc, f["f4"])
            v.tensor_sub(f["f4"], f["bk8"], sS["PLp"])
            v.tensor_add(f["f4"], f["f4"], prm["u8"])
            b.inc(sVE, v.tensor_scalar_min(f["f4"], f["f4"], CLAMP))
        bvec(bnd1)
        nbv = b.n(sVE)

        def bnd2(s):
            b.wait(s, sVE, nbv)
            s.activation(f["f4"], f["f4"], ACT.Exp)
            b.inc(sSC, s.activation(f["br8"], f["br8"], ACT.Exp, scale=-1.0))
        bscal(bnd2)
        nbs = b.n(sSC)

        def bnd3(v):
            b.wait(v, sSC, nbs)
            v.tensor_mul(f["f5"], f["f4"], f["bv8"])
            v.tensor_add(f["f5"], sS["ALp"], f["f5"])
            v.tensor_add(f["f6"], sS["BLp"], f["f4"])
            v.reciprocal(f["f6"], f["f6"])
            v.tensor_mul(f["f5"], f["f5"], f["f6"])
            v.tensor_scalar_add(f["br8"], f["br8"], 1.0)
            v.reciprocal(f["br8"], f["br8"])
            b.inc(sVE, v.tensor_mul(f["f5"], f["f5"], f["br8"]))
        bvec(bnd3)
        n_rwkvl = b.n(sVE)

        nprev = n_folds
        for i in range(NCC):
            base = (i % 2) * 3
            rk = sl(B2, base)
            rv = sl(B2, base + 1)
            rr = sl(B2, base + 2)

            rsem = sRG0 if (i % 2 == 0) else sRG1

            def s0(s, i=i, nprev=nprev):
                b.wait(s, sVE, nprev)
                b.inc(sSC, s.add(R_W, ZT, w8[:, i:i + 1]))
            bscal(s0)
            ns0 = b.n(sSC)

            def ld(sync, i=i, rk=rk, rv=rv, rr=rr, rsem=rsem):
                b.wait(sync, sRs0, n_r_sp[0])
                b.wait(sync, sRs1, n_r_sp[1])
                b.wait(sync, sKs0, n_k_sp[0])
                b.wait(sync, sKs1, n_k_sp[1])
                b.wait(sync, sLD, n_ag1)
                gate(sync, ("ring2", i % 2))
                b.inc(rsem, sync.dma_start(out=rk, in_=ksp[:, i * 2048:(i + 1) * 2048]), 16)
                b.inc(rsem, sync.dma_start(out=rv, in_=vsp[:, i * 2048:(i + 1) * 2048]), 16)
                b.inc(rsem, sync.dma_start(out=rr, in_=rsp[:, i * 2048:(i + 1) * 2048]), 16)
            block.sync(ld)
            nld = b.n(rsem)

            def v1(v, i=i, rk=rk, nld=nld, rsem=rsem, ns0=ns0):
                b.wait(v, rsem, nld)
                b.wait(v, sVE, n_folds)
                b.wait(v, sSC, ns0)
                v.tensor_copy(R_PL[:, 0:1], ppin[:, i:i + 1])
                v.tensor_tensor_scan(R_PL[:, 1:2049], R_W, rk, ppin[:, i:i + 1],
                                     ALU.add, ALU.max)
                b.inc(sVE, v.tensor_sub(R_T2, R_PL[:, 0:2048], R_PL[:, 1:2049]))
            bvec(v1)
            nv1 = b.n(sVE)
            nprev = nv1

            def s1(s, i=i, nv1=nv1):
                b.wait(s, sVE, nv1)
                b.inc(sSC, s.activation(R_T2, R_T2, ACT.Exp, bias=w8[:, i:i + 1]))
            bscal(s1)
            ns1 = b.n(sSC)

            def v2(v, rk=rk, ns1=ns1):
                b.wait(v, sSC, ns1)
                b.inc(sVE, v.tensor_sub(R_T3, rk, R_PL[:, 1:2049]))
            bvec(v2)
            nv2 = b.n(sVE)

            def s2(s, nv2=nv2):
                b.wait(s, sVE, nv2)
                b.inc(sSC, s.activation(R_T3, R_T3, ACT.Exp))
            bscal(s2)
            ns2 = b.n(sSC)

            def v3(v, i=i, rv=rv, rk=rk, ns2=ns2):
                b.wait(v, sSC, ns2)
                v.tensor_copy(R_BL[:, 0:1], bbin[:, i:i + 1])
                v.tensor_tensor_scan(R_BL[:, 1:2049], R_T2, R_T3, bbin[:, i:i + 1],
                                     ALU.mult, ALU.add)
                v.tensor_mul(R_T3, R_T3, rv)
                v.tensor_copy(R_AL[:, 0:1], aain[:, i:i + 1])
                v.tensor_tensor_scan(R_AL[:, 1:2049], R_T2, R_T3, aain[:, i:i + 1],
                                     ALU.mult, ALU.add)
                v.tensor_sub(R_T2, rk, R_PL[:, 0:2048])
                b.inc(sVE, v.tensor_scalar(R_T2, R_T2, prm["u8"][:, i:i + 1],
                                           CLAMP, ALU.add, ALU.min))
            bvec(v3)
            nv3 = b.n(sVE)

            def s3(s, rr=rr, nv3=nv3):
                b.wait(s, sVE, nv3)
                s.activation(R_T2, R_T2, ACT.Exp)
                s.activation(rr, rr, ACT.Exp, scale=-1.0)
                b.inc(sSC, s.add(rr, rr, 1.0))
            bscal(s3)
            ns3 = b.n(sSC)

            def v4(v, i=i, rv=rv, rr=rr, ns3=ns3):
                b.wait(v, sSC, ns3)
                v.tensor_mul(R_T3, R_T2, rv)
                v.tensor_add(R_T3, R_AL[:, 0:2048], R_T3)
                v.tensor_add(R_T2, R_BL[:, 0:2048], R_T2)
                v.reciprocal(R_T2, R_T2)
                v.tensor_mul(R_T3, R_T3, R_T2)
                v.reciprocal(rr, rr)
                b.inc(sVE, v.tensor_mul(sl(B3, i), R_T3, rr))
            bvec(v4)
            marks[("ring2", i % 2)] = (sVE, b.n(sVE))
        n_p2 = b.n(sVE)

        if "rwkv" in dbg:
            tap = mktap("rwkv", (128, 16384))

            def dt1(s, tap=tap):
                b.wait(s, sVE, n_p2)
                b.inc(sSPt, s.dma_start(out=tap, in_=B3), 16)
            bscal(dt1)
            setg(("tap", id(B3)), sSPt)

        PAlo = PA[:, 0:512]
        PAhi = PA[:, 512:1024]
        dbanks = [PC, PD, PAlo, PAhi]
        zs4 = [S0[:, 0:512], S0[:, 512:1024], S1[:, 0:512], S1[:, 512:1024]]
        for cb in range(NCC):
            wslot = W0 if (cb % 2 == 0) else W1
            wsem = sW0d if (cb % 2 == 0) else sW1d

            def ld(sync, cb=cb, wslot=wslot, wsem=wsem):
                b.wait(sync, sVE, n_p2)
                gate(sync, ("w", id(wslot)))
                b.inc(wsem, sync.dma_start(
                    out=wslot.rearrange("p (a f) -> p a f", f=128),
                    in_=woT[:, cb * 128:(cb + 1) * 128]
                        .rearrange("(a p) f -> p a f", p=128)), 16)
            block.sync(ld)
            nld = b.n(wsem)

            for tb in range(4):
                bank = dbanks[tb]
                zslot = zs4[tb]

                zsem = [sZR0, sZR1, sZR2, sZR3][tb]

                def zld(sync, cb=cb, tb=tb, zslot=zslot, zsem=zsem):
                    b.wait(sync, sRs0, n_r_sp[0])
                    b.wait(sync, sRs1, n_r_sp[1])
                    b.wait(sync, sSPb, n_z0sp)
                    gate(sync, ("zres", id(zslot)))
                    b.inc(zsem, sync.dma_start(
                        out=zslot,
                        in_=z0sp[:, cb * 2048 + tb * 512: cb * 2048 + (tb + 1) * 512]), 16)
                block.sync(zld)
                nzld = b.n(zsem)

                def mm(t, tb=tb, wslot=wslot, bank=bank, nld=nld, wsem=wsem):
                    b.wait(t, wsem, nld)
                    gate(t, ("bank", id(bank)))
                    wv = wslot.rearrange("p (a f) -> p a f", f=128)
                    for a in range(NCC):
                        ins = t.matmul(bank, wv[:, a, :],
                                       sl(B3, a)[:, tb * 512:(tb + 1) * 512],
                                       start=(a == 0), stop=(a == NCC - 1))
                    b.inc(sTE, ins)
                block.tensor(mm)
                nte = b.n(sTE)

                def ev(v, cb=cb, tb=tb, bank=bank, nte=nte, nzld=nzld, zslot=zslot,
                       zsem=zsem):
                    b.wait(v, sTE, nte)
                    b.wait(v, zsem, nzld)
                    b.inc(sVE, v.tensor_add(
                        sl(B1, cb)[:, tb * 512:(tb + 1) * 512],
                        bank, zslot))
                bvec(ev)
                setg(("bank", id(bank)), sVE)
                setg(("zres", id(zslot)), sVE)

            def bmm(t, cb=cb, wslot=wslot):
                b.wait(t, sVE, n_rwkvl)
                if cb == 0:
                    gate(t, ("bank", id(PE_)))
                wv = wslot.rearrange("p (a f) -> p a f", f=128)
                for a in range(NCC):
                    ins = t.matmul(PE_[:, cb:cb + 1], wv[:, a, :],
                                   f["f5"][:, a:a + 1],
                                   start=(a == 0), stop=(a == NCC - 1))
                b.inc(sTE, ins)
            block.tensor(bmm)
            setg(("w", id(wslot)), sTE)
        n_delta = b.n(sVE)
        n_dl = b.n(sTE)

        def dl1(s):
            b.wait(s, sTE, n_dl)
            b.inc(sSC, s.copy(f["f7"], PE_[:, 0:NCC]))
        bscal(dl1)
        setg(("bank", id(PE_)), sSC)
        ndl1 = b.n(sSC)

        def dl2(v):
            b.wait(v, sSC, ndl1)
            b.inc(sVE, v.tensor_add(f["f7"], f["f7"], z0b))
        bvec(dl2)
        ndl2 = b.n(sVE)

        def dl3(s):
            b.wait(s, sVE, ndl2)
            b.inc(sSC, s.activation(f["f6"], f["f7"], ACT.Square))
        bscal(dl3)
        ndl3 = b.n(sSC)

        def dl4(t):
            b.wait(t, sSC, ndl3)
            gate(t, ("bank", id(PF)))
            t.matmul(PF[0:1, 0:8], ONESC, f["f7"], start=True, stop=True)
            b.inc(sTE, t.matmul(PF[0:1, 8:16], ONESC, f["f6"], start=True, stop=True))
        block.tensor(dl4)
        ndl4 = b.n(sTE)

        def dl5(v):
            b.wait(v, sTE, ndl4)
            v.reduce_sum(stat1[:, 0:1], PF[0:1, 0:8], AX.X)
            v.reduce_sum(stat1[:, 1:2], PF[0:1, 8:16], AX.X)
            v.tensor_scalar_mul(stat1[:, 2:3], stat1[:, 0:1], 1.0 / C)
            v.tensor_scalar_mul(stat1[:, 3:4], stat1[:, 1:2], 1.0 / C)
            v.scalar_tensor_tensor(stat1[:, 4:5], stat1[:, 2:3], stat1[:, 2:3],
                                   stat1[:, 3:4], ALU.mult, ALU.subtract)
            b.inc(sVE, v.tensor_scalar(stat1[:, 4:5], stat1[:, 4:5], -1.0, EPS,
                                       ALU.mult, ALU.add))
        bvec(dl5)
        ndl5 = b.n(sVE)

        def dl6(s):
            b.wait(s, sVE, ndl5)
            b.inc(sSC, s.activation(stat1[:, 5:6], stat1[:, 4:5], ACT.Sqrt))
        bscal(dl6)
        ndl6 = b.n(sSC)

        def dl7(v):
            b.wait(v, sSC, ndl6)
            v.memset(ONESR, 1.0)
            v.reciprocal(stat1[:, 6:7], stat1[:, 5:6])
            b.inc(sVE, v.tensor_copy(stat1[:, 7:8], stat1[:, 2:3]))
        bvec(dl7)
        ndl7 = b.n(sVE)

        def dl8(t):
            b.wait(t, sVE, ndl7)
            b.inc(sTE, t.matmul(PF[:, 16:18], ONESR, stat1[:, 6:8],
                                start=True, stop=True))
        block.tensor(dl8)
        ndl8 = b.n(sTE)

        def dl9(v):
            b.wait(v, sTE, ndl8)
            v.tensor_copy(f["bm2x"][:, 0:1], PF[:, 17:18])
            v.tensor_copy(f["bm2x"][:, 1:2], PF[:, 16:17])
            v.tensor_scalar(xcb, f["f7"], f["bm2x"][:, 0:1], f["bm2x"][:, 1:2],
                            ALU.subtract, ALU.mult)
            v.tensor_mul(xcb, xcb, prm["ln2g"])
            v.tensor_add(xcb, xcb, prm["ln2b"])
            b.inc(sVE, v.tensor_scalar_mul(xcb, xcb, C0M[:, 0:1]))
        bvec(dl9)
        setg(("bank", id(PF)), sVE)
        n_xcb = b.n(sVE)

        if "xatt" in dbg:
            tap = mktap("xatt", (128, 16384))

            def dt2(s, tap=tap):
                b.wait(s, sVE, n_delta)
                b.inc(sSPt, s.dma_start(out=tap, in_=B1), 16)
            bscal(dt2)
            setg(("tap", id(B1)), sSPt)

        for tt in range(NT):
            for bg in range(2):
                srcs = [B1[:, cb2 * 2048 + tt * 128: cb2 * 2048 + (tt + 1) * 128]
                        for cb2 in range(bg * 4, bg * 4 + 4)]
                dsts = [B2[:, tt * 1024 + bg * 512: tt * 1024 + (bg + 1) * 512]]
                emit_tp(srcs, dsts, wait_list=[(sVE, n_delta)])
        n_xatm = b.n(sSC)

        emit_ln(lambda i: tmt(B2, i), NT, wait_list=[(sSC, n_xatm)])
        n_ln2 = b.n(sVE)

        for i in range(NCC):
            srcs = [tmt(B2, tt)[:, i * 128:(i + 1) * 128] for tt in range(NT)]
            dsts = [B3[:, i * 2048 + q * 512: i * 2048 + (q + 1) * 512]
                    for q in range(4)]
            wl = [(sVE, max(n_ln2, n_p2))]
            if b.n(sSPt):
                wl.append((sSPt, b.n(sSPt)))
            emit_tp(srcs, dsts, wait_list=wl)
        n_z2 = b.n(sSC)

        def cm_xc(v):
            b.wait(v, sSC, n_z2)
            for i in range(NCC):
                ins = v.tensor_scalar(sl(B3, i), sl(B3, i), prm["ln2g"][:, i:i + 1],
                                      prm["ln2b"][:, i:i + 1], ALU.mult, ALU.add)
            b.inc(sVE, ins)
        bvec(cm_xc)
        n_xc = b.n(sVE)

        def cm_spx(s):
            b.wait(s, sSC, n_xatm)
            tapwait(s)
            b.inc(sSPa, s.dma_start(out=xasp, in_=B1), 16)
        bscal(cm_spx)
        n_xasp = b.n(sSPa)

        def mix2(dstbuf, cname, wve, tag):
            def fn(v):
                b.wait(v, sVE, wve)
                for i in range(NCC):
                    xc_i = sl(B3, i)
                    v.tensor_sub(S0[:, 1:2048], xc_i[:, 1:2048], xc_i[:, 0:2047])
                    v.tensor_sub(S0[:, 0:1], xc_i[:, 0:1], xcb[:, i:i + 1])
                    v.scalar_tensor_tensor(sl(dstbuf, i)[:, 1:2048], S0[:, 1:2048],
                                           prm[cname][:, i:i + 1], xc_i[:, 0:2047],
                                           ALU.mult, ALU.add)
                    ins = v.scalar_tensor_tensor(sl(dstbuf, i)[:, 0:1], S0[:, 0:1],
                                                 prm[cname][:, i:i + 1],
                                                 xcb[:, i:i + 1], ALU.mult, ALU.add)
                b.inc(sVE, ins)
            bvec(fn)

        mix2(B2, "cmr", max(n_xc, n_xcb), "xr2")
        n_xr2 = b.n(sVE)

        for j in range(NCC):
            wslot = W0 if (j % 2 == 0) else W1
            wsem = sW0d if (j % 2 == 0) else sW1d

            def ld(sync, j=j, wslot=wslot, wsem=wsem):
                gate(sync, ("w", id(wslot)))
                b.inc(wsem, sync.dma_start(
                    out=wslot.rearrange("p (a f) -> p a f", f=128),
                    in_=fwrT[:, j * 128:(j + 1) * 128]
                        .rearrange("(a p) f -> p a f", p=128)), 16)
            block.sync(ld)
            nld = b.n(wsem)
            for tb in range(4):
                bank = PC if (tb % 2 == 0) else PD

                def mm(t, tb=tb, wslot=wslot, bank=bank, nld=nld, wsem=wsem):
                    b.wait(t, wsem, nld)
                    b.wait(t, sVE, n_xr2)
                    b.wait(t, sSPa, n_xasp)
                    gate(t, ("bank", id(bank)))
                    wv = wslot.rearrange("p (a f) -> p a f", f=128)
                    for ci in range(NCC):
                        ins = t.matmul(bank[:, 0:512], wv[:, ci, :],
                                       sl(B2, ci)[:, tb * 512:(tb + 1) * 512],
                                       start=(ci == 0), stop=(ci == NCC - 1))
                    b.inc(sTE, ins)
                block.tensor(mm)
                setg(("w", id(wslot)), sTE)
                nte = b.n(sTE)

                def ev(s, j=j, tb=tb, bank=bank, nte=nte):
                    b.wait(s, sTE, nte)
                    tapwait(s)
                    b.inc(sSC, s.copy(sl(B1, j)[:, tb * 512:(tb + 1) * 512],
                                      bank[:, 0:512]))
                bscal(ev)
                setg(("bank", id(bank)), sSC)
        n_r2 = b.n(sSC)

        for i in range(NCC):
            def sg1(s, i=i):
                b.wait(s, sSC, n_r2)
                b.inc(sSC, s.activation(sl(B1, i), sl(B1, i), ACT.Exp, scale=-1.0))
            bscal(sg1)
            nsg = b.n(sSC)

            def sg2(v, i=i, nsg=nsg):
                b.wait(v, sSC, nsg)
                v.tensor_scalar_add(sl(B1, i), sl(B1, i), 1.0)
                b.inc(sVE, v.reciprocal(sl(B1, i), sl(B1, i)))
            bvec(sg2)
        n_sig2 = b.n(sVE)

        def cm_spsg(s):
            b.wait(s, sVE, n_sig2)
            b.inc(sSPa, s.dma_start(out=sgsp, in_=B1), 16)
        bscal(cm_spsg)
        n_sgsp = b.n(sSPa)

        def cm_gate_xk2(v):
            b.wait(v, sSC, n_r2)
            b.inc(sVE, v.engine_nop())
        bvec(cm_gate_xk2)
        mix2(B2, "cmk", b.n(sVE), "xk2")
        n_xk2 = b.n(sVE)

        for g8 in range(4):
            def ldg(sync, g8=g8):
                b.wait(sync, sSPa, n_sgsp)
                gate(sync, ("cmixw",))
                for hh in range(8):
                    h = g8 * 8 + hh
                    b.inc(sCW, sync.dma_start(
                        out=B1[:, hh * 1024:(hh + 1) * 1024]
                            .rearrange("p (a f) -> p a f", f=128),
                        in_=fwkT[:, h * 128:(h + 1) * 128]
                            .rearrange("(a p) f -> p a f", p=128)), 16)
                    b.inc(sCW, sync.dma_start(
                        out=B1[:, 8192 + hh * 1024: 8192 + (hh + 1) * 1024],
                        in_=fwvT[h * 128:(h + 1) * 128, :]), 16)
            block.sync(ldg)
            nldg = b.n(sCW)

            for tb in range(4):
                kvbank = [PA[:, 0:512], PA[:, 512:1024], PB[:, 0:512], PB[:, 512:1024]]
                mmv0_q = []
                for hh in range(8):
                    kbank = PC if (hh % 2 == 0) else PD
                    kkslot = (S0 if hh < 4 else S1)[:, (hh % 4) * 512:(hh % 4 + 1) * 512]

                    def mmk(t, g8=g8, tb=tb, hh=hh, kbank=kbank, nldg=nldg):
                        b.wait(t, sCW, nldg)
                        b.wait(t, sVE, n_xk2)
                        gate(t, ("bank", id(kbank)))
                        wv = B1[:, hh * 1024:(hh + 1) * 1024] \
                            .rearrange("p (a f) -> p a f", f=128)
                        for ci in range(NCC):
                            ins = t.matmul(kbank[:, 0:512], wv[:, ci, :],
                                           sl(B2, ci)[:, tb * 512:(tb + 1) * 512],
                                           start=(ci == 0), stop=(ci == NCC - 1))
                        b.inc(sTE, ins)
                    block.tensor(mmk)
                    ntk = b.n(sTE)

                    def evk(s, kbank=kbank, kkslot=kkslot, ntk=ntk):
                        b.wait(s, sTE, ntk)
                        gate(s, ("kkS",))
                        b.inc(sSC, s.activation(kkslot, kbank[:, 0:512], ACT.Relu))
                    bscal(evk)
                    setg(("bank", id(kbank)), sSC)
                    nkr = b.n(sSC)

                    def evk2(v, kkslot=kkslot, nkr=nkr):
                        b.wait(v, sSC, nkr)
                        b.inc(sVE, v.tensor_mul(kkslot, kkslot, kkslot))
                    bvec(evk2)
                    nkkh = b.n(sVE)

                    def mmv0(t, hh=hh, nkkh=nkkh, g8=g8, tb=tb):
                        b.wait(t, sVE, nkkh)
                        if hh == 0:
                            gate(t, ("kvev",))
                        kks = (S0 if hh < 4 else S1)[:, (hh % 4) * 512:(hh % 4 + 1) * 512]
                        for ci4 in range(4):
                            ci = ci4
                            ins = t.matmul(
                                kvbank[ci4],
                                B1[:, 8192 + hh * 1024 + ci * 128:
                                   8192 + hh * 1024 + (ci + 1) * 128],
                                kks, start=(hh == 0), stop=(hh == 7))
                        b.inc(sTE, ins)
                    mmv0_q.append(mmv0)
                    if hh > 0:
                        block.tensor(mmv0_q[hh - 1])
                block.tensor(mmv0_q[7])
                nkk = b.n(sVE)
                ntv0 = b.n(sTE)

                def evv0(z, g8=g8, tb=tb, ntv0=ntv0):
                    b.wait(z, sTE, ntv0)
                    for ci4 in range(4):
                        src_ = kvbank[ci4]
                        dst = sl(B3, ci4)[:, tb * 512:(tb + 1) * 512]
                        if g8 == 0:
                            ins = z.copy(dst, src_)
                        else:
                            ins = z.tensor_add(dst, dst, src_)
                    if g8 == 0:
                        b.inc(sSC, ins)
                    else:
                        b.inc(sVE, ins)
                if g8 == 0:
                    bscal(evv0)
                    nev0 = (sSC, b.n(sSC))
                else:
                    bvec(evv0)
                    nev0 = (sVE, b.n(sVE))

                def mmv1(t, g8=g8, tb=tb, nkk=nkk, nev0=nev0):
                    b.wait(t, sVE, nkk)
                    b.wait(t, nev0[0], nev0[1])
                    for hh in range(8):
                        kks = (S0 if hh < 4 else S1)[:, (hh % 4) * 512:(hh % 4 + 1) * 512]
                        for ci4 in range(4):
                            ci = 4 + ci4
                            ins = t.matmul(
                                kvbank[ci4],
                                B1[:, 8192 + hh * 1024 + ci * 128:
                                   8192 + hh * 1024 + (ci + 1) * 128],
                                kks, start=(hh == 0), stop=(hh == 7))
                    b.inc(sTE, ins)
                block.tensor(mmv1)
                ntv1 = b.n(sTE)

                def evv1(z, g8=g8, tb=tb, ntv1=ntv1):
                    b.wait(z, sTE, ntv1)
                    for ci4 in range(4):
                        ci = 4 + ci4
                        src_ = kvbank[ci4]
                        dst = sl(B3, ci)[:, tb * 512:(tb + 1) * 512]
                        if g8 == 0:
                            ins = z.copy(dst, src_)
                        else:
                            ins = z.tensor_add(dst, dst, src_)
                    if g8 == 0:
                        b.inc(sSC, ins)
                    else:
                        b.inc(sVE, ins)
                if g8 == 0:
                    bscal(evv1)
                    marks[("kvev",)] = (sSC, b.n(sSC))
                else:
                    bvec(evv1)
                    marks[("kvev",)] = (sVE, b.n(sVE))
                setg(("kkS",), sTE)
            setg(("cmixw",), sTE)
        n_cmix_te = b.n(sTE)
        n_kv_sc = b.n(sSC)
        n_kv_ve = b.n(sVE)

        for ci in range(NCC):
            zsem = sZR0 if (ci % 2 == 0) else sZR1

            def ldc(sync, ci=ci, zsem=zsem):
                b.wait(sync, sSPa, n_sgsp)
                b.wait(sync, sTE, n_cmix_te)
                gate(sync, ("cres", ci % 2))
                b.inc(zsem, sync.dma_start(
                    out=(S0 if ci % 2 == 0 else S1),
                    in_=sgsp[:, ci * 2048:(ci + 1) * 2048]), 16)
            block.sync(ldc)
            nld1 = b.n(zsem)

            def cmb1(v, ci=ci, nld1=nld1, zsem=zsem):
                b.wait(v, zsem, nld1)
                b.wait(v, sVE, n_kv_ve)
                b.wait(v, sSC, n_kv_sc)
                b.inc(sVE, v.tensor_mul(sl(B3, ci), sl(B3, ci),
                                        (S0 if ci % 2 == 0 else S1)))
            bvec(cmb1)
            nve1 = b.n(sVE)

            def ldc2(sync, ci=ci, nve1=nve1, zsem=zsem):
                b.wait(sync, sVE, nve1)
                b.inc(zsem, sync.dma_start(
                    out=(S0 if ci % 2 == 0 else S1),
                    in_=xasp[:, ci * 2048:(ci + 1) * 2048]), 16)
            block.sync(ldc2)
            nld2 = b.n(zsem)

            def cmb2(v, ci=ci, nld2=nld2, zsem=zsem):
                b.wait(v, zsem, nld2)
                b.inc(sVE, v.tensor_add(sl(B3, ci), sl(B3, ci),
                                        (S0 if ci % 2 == 0 else S1)))
            bvec(cmb2)
            marks[("cres", ci % 2)] = (sVE, b.n(sVE))
        n_bo = b.n(sVE)

        if "bo" in dbg:
            tap = mktap("bo", (128, 16384))

            def dt3(s, tap=tap):
                b.wait(s, sVE, n_bo)
                b.inc(sSPt, s.dma_start(out=tap, in_=B3), 16)
            bscal(dt3)
            setg(("tap", id(B3)), sSPt)

        for tt in range(NT):
            for bg in range(2):
                srcs = [B3[:, cb2 * 2048 + tt * 128: cb2 * 2048 + (tt + 1) * 128]
                        for cb2 in range(bg * 4, bg * 4 + 4)]
                dsts = [B2[:, tt * 1024 + bg * 512: tt * 1024 + (bg + 1) * 512]]
                emit_tp(srcs, dsts, wait_list=[(sVE, n_bo)])
        n_botm = b.n(sSC)

        emit_ln(lambda i: tmt(B2, i), NT, wait_list=[(sSC, n_botm)])
        n_lnb = b.n(sVE)

        for tt in range(NT):
            zsem = sZR0 if (tt % 2 == 0) else sZR1

            def ldx(sync, tt=tt, zsem=zsem):
                b.wait(sync, sSC, n_botm)
                gate(sync, ("xres", tt % 2))
                b.inc(zsem, sync.dma_start(
                    out=(S0 if tt % 2 == 0 else S1)[:, 0:1024],
                    in_=x_sl[1 + tt * 128:1 + (tt + 1) * 128, :]), 16)
            block.sync(ldx)
            nldx = b.n(zsem)

            def x2t(v, tt=tt, nldx=nldx, zsem=zsem):
                b.wait(v, zsem, nldx)
                b.wait(v, sVE, n_lnb)
                tapwait(v)
                b.inc(sVE, v.tensor_add(tmt(B3, tt), tmt(B2, tt),
                                        (S0 if tt % 2 == 0 else S1)[:, 0:1024]))
            bvec(x2t)
            marks[("xres", tt % 2)] = (sVE, b.n(sVE))
        n_x2 = b.n(sVE)

        def spx2(s):
            b.wait(s, sVE, n_x2)
            tapwait(s)
            b.inc(sSPc3, s.dma_start(out=x2sp, in_=B3), 16)
        bscal(spx2)
        n_x2sp = b.n(sSPc3)

        emit_ln(lambda i: tmt(B3, i), NT, out_tile=lambda i: tmt(B2, i))
        n_y = b.n(sVE)

        if "y" in dbg:
            tap = mktap("y", (128, 16384))

            def dt4(s, tap=tap):
                b.wait(s, sVE, n_y)
                b.inc(sSPt, s.dma_start(out=tap, in_=B2), 16)
            bscal(dt4)
            setg(("tap", id(B2)), sSPt)

        def ldphi(sync):
            b.wait(sync, sSPc3, n_x2sp)
            b.wait(sync, sVE, n_y)
            gate(sync, ("cmixw",))
            b.inc(sLD, sync.dma_start(
                out=B1.rearrange("p (a f) -> p a f", f=2048),
                in_=phiT_d.rearrange("(a p) f -> p a f", p=128)), 16)
            b.inc(sLD, sync.dma_start(out=PHIB, in_=phib_d), 16)
            b.inc(sLD, sync.dma_start(out=MDs, in_=md_d), 16)
            b.inc(sLD, sync.dma_start(out=MCs, in_=mc_d), 16)
        block.sync(ldphi)
        n_ldphi = b.n(sLD)

        E_ODD = YT
        DC_ODD = XT
        CSTG2 = XTt

        def phi_banks(tt):
            if tt % 2 == 0:
                return [PA[:, 0:512], PA[:, 512:1024], PB[:, 0:512], PB[:, 512:1024]]
            return [PC[:, 0:512], PD[:, 0:512], PE_[:, 0:512], PF[:, 0:512]]

        def phi_bufs(tt):
            if tt % 2 == 0:
                return (S0, S1, CSTG)
            return (E_ODD, DC_ODD, CSTG2)

        def phi_stageA(tt):
            bks = phi_banks(tt)
            for bb_ in range(2):
                def ty(t, tt=tt, bb_=bb_, bks=bks):
                    b.wait(t, sVE, n_y)
                    gate(t, ("pbank", tt % 2))
                    for jj in range(4):
                        dd = bb_ * 4 + jj
                        ins = t.transpose(bks[bb_][:, jj * 128:(jj + 1) * 128],
                                          tmt(B2, tt)[:, dd * 128:(dd + 1) * 128],
                                          IDN)
                    b.inc(sTE, ins)
                block.tensor(ty)
                nty = b.n(sTE)

                def evy_(s, bb_=bb_, nty=nty, bks=bks, tt=tt):
                    b.wait(s, sTE, nty)
                    if bb_ == 0:
                        gate(s, ("w", id(W0)))
                    b.inc(sSC, s.copy(W0[:, bb_ * 512:(bb_ + 1) * 512], bks[bb_]))
                bscal(evy_)
            setg(("w", id(W0)), sSC)
            n_yfm = b.n(sSC)
            for q4 in range(4):
                def mm(t, q4=q4, n_yfm=n_yfm, tt=tt, bks=bks):
                    b.wait(t, sLD, n_ldphi)
                    b.wait(t, sSC, n_yfm)
                    wv = W0.rearrange("p (a f) -> p a f", f=128)
                    for dd in range(NCC):
                        ins = t.matmul(bks[q4], wv[:, dd, :],
                                       B1[:, dd * 2048 + q4 * 512:
                                          dd * 2048 + (q4 + 1) * 512],
                                       start=(dd == 0), stop=(dd == NCC - 1))
                    b.inc(sTE, ins)
                block.tensor(mm)
            setg(("w", id(W0)), sTE)
            marks[("philg", tt)] = (sTE, b.n(sTE))

        def phi_stageB1(tt):
            bks = phi_banks(tt)
            Ebuf, Dbuf, Cstg = phi_bufs(tt)
            n_lg = marks[("philg", tt)][1]

            def sm1(v, n_lg=n_lg, bks=bks, Ebuf=Ebuf, tt=tt):
                b.wait(v, sTE, n_lg)
                gate(v, ("Ebuf", tt % 2))
                for q4 in range(4):
                    v.tensor_add(Ebuf[:, q4 * 512:(q4 + 1) * 512], bks[q4],
                                 PHIB[:, q4 * 512:(q4 + 1) * 512])
                v.reduce_max(f["f5"][:, 0:1], Ebuf, AX.X)
                b.inc(sVE, v.tensor_scalar_mul(f["f5"][:, 0:1], f["f5"][:, 0:1], -1.0))
            bvec(sm1)
            nsm1 = b.n(sVE)

            def sm2(s, nsm1=nsm1, Ebuf=Ebuf):
                b.wait(s, sVE, nsm1)
                b.inc(sSC, s.activation(Ebuf, Ebuf, ACT.Exp, bias=f["f5"][:, 0:1]))
            bscal(sm2)

        def phi_stageB2(tt):
            bks = phi_banks(tt)
            Ebuf, Dbuf, Cstg = phi_bufs(tt)
            nsm2 = b.n(sSC)

            def sm3(v, nsm2=nsm2, Ebuf=Ebuf):
                b.wait(v, sSC, nsm2)
                v.tensor_reduce(sD, Ebuf.rearrange("p (n s) -> p n s", s=32),
                                AX.X, ALU.add)
                v.tensor_reduce(sCt, Ebuf.rearrange("p (n s) -> p s n", s=32),
                                AX.X, ALU.add)
                v.reciprocal(sD, sD)
                b.inc(sVE, v.reciprocal(sCt, sCt))
            bvec(sm3)
            nsm3 = b.n(sVE)

            def sm4(t, nsm3=nsm3, bks=bks):
                b.wait(t, sVE, nsm3)
                t.transpose(bks[0][0:64, 0:128], sD, IDN)
                b.inc(sTE, t.transpose(bks[0][0:32, 128:256], sCt, IDN))
            block.tensor(sm4)
            nsm4 = b.n(sTE)

            def sm5(s, nsm4=nsm4, bks=bks):
                b.wait(s, sTE, nsm4)
                s.copy(rDT, bks[0][0:64, 0:128])
                b.inc(sSC, s.copy(rCT, bks[0][0:32, 128:256]))
            bscal(sm5)
            nsm5 = b.n(sSC)

            for (which, rT, Ms) in (("D", rDT, MDs), ("C", rCT, MCs)):
                for q4 in range(4):
                    def mmb(t, rT=rT, Ms=Ms, q4=q4, nsm5=nsm5, which=which, tt=tt,
                            bks=bks):
                        b.wait(t, sSC, nsm5)
                        if which == "C" and q4 == 0:
                            gate(t, ("dcmul", tt % 2))
                        b.inc(sTE, t.matmul(bks[q4], rT,
                                            Ms[:, q4 * 512:(q4 + 1) * 512],
                                            start=True, stop=True))
                    block.tensor(mmb)
                nmb = b.n(sTE)

                def mlt(v, nmb=nmb, which=which, tt=tt, bks=bks, Ebuf=Ebuf,
                        Dbuf=Dbuf):
                    b.wait(v, sTE, nmb)
                    if which == "D":
                        gate(v, ("Dbuf", tt % 2))
                    else:
                        gate(v, ("Ddma", tt % 2))
                    for q4 in range(4):
                        ins = v.tensor_mul(Dbuf[:, q4 * 512:(q4 + 1) * 512],
                                           Ebuf[:, q4 * 512:(q4 + 1) * 512],
                                           bks[q4])
                    b.inc(sVE, ins)
                bvec(mlt)
                nmlt = b.n(sVE)
                if which == "D":
                    marks[("dcmul", tt % 2)] = (sVE, nmlt)

                    def smd(s, tt=tt, nmlt=nmlt, Dbuf=Dbuf):
                        b.wait(s, sVE, nmlt)
                        b.inc(sDsp, s.dma_start(
                            out=dsp[tt * 128:(tt + 1) * 128, :], in_=Dbuf), 16)
                    bscal(smd)
                    marks[("Ddma", tt % 2)] = (sDsp, b.n(sDsp))
                else:
                    marks[("Ebuf", tt % 2)] = (sVE, nmlt)

            nCmul = b.n(sVE)
            for bb_ in range(4):
                bk = bks[bb_ % 2]

                def tc(t, tt=tt, bb_=bb_, bk=bk, nCmul=nCmul, Dbuf=Dbuf):
                    b.wait(t, sVE, nCmul)
                    if bb_ >= 2:
                        gate(t, ("ctev", tt % 2, bb_ % 2))
                    for jj in range(4):
                        np_ = bb_ * 4 + jj
                        ins = t.transpose(bk[:, jj * 128:(jj + 1) * 128],
                                          Dbuf[:, np_ * 128:(np_ + 1) * 128], IDN)
                    b.inc(sTE, ins)
                block.tensor(tc)
                ntc = b.n(sTE)

                def evc(s, tt=tt, bb_=bb_, bk=bk, ntc=ntc, Cstg=Cstg):
                    b.wait(s, sTE, ntc)
                    if bb_ == 0:
                        gate(s, ("cstg", tt % 2))
                    b.inc(sSC, s.copy(Cstg[:, bb_ * 512:(bb_ + 1) * 512], bk))
                bscal(evc)
                marks[("ctev", tt % 2, bb_ % 2)] = (sSC, b.n(sSC))
            marks[("Dbuf", tt % 2)] = (sSC, b.n(sSC))
            marks[("pbank", tt % 2)] = (sSC, b.n(sSC))
            nctp = b.n(sSC)

            csem = sCs0 if (tt % 2 == 0) else sCs1

            def smc(s, tt=tt, nctp=nctp, Cstg=Cstg, csem=csem):
                b.wait(s, sSC, nctp)
                b.inc(csem, s.dma_start(
                    out=csp.rearrange("(a p) t -> p a t", p=128)[:, :, tt * 128:(tt + 1) * 128],
                    in_=Cstg.rearrange("p (a f) -> p a f", f=128)), 16)
            bscal(smc)
            marks[("cstg", tt % 2)] = (csem, b.n(csem))

        phi_stageA(0)
        for tt in range(NT):
            phi_stageB1(tt)
            if tt + 1 < NT:
                phi_stageA(tt + 1)
            phi_stageB2(tt)

        n_phi = b.n(sDsp)
        n_phic = (b.n(sCs0), b.n(sCs1))
        n_phi_ve = b.n(sVE)
        n_phi_te = b.n(sTE)

        for p16 in range(16):
            dslot = S0 if (p16 % 2 == 0) else S1
            zsem = sZR0 if (p16 % 2 == 0) else sZR1

            def ldd(sync, p16=p16, dslot=dslot, zsem=zsem):
                b.wait(sync, sDsp, n_phi)
                b.wait(sync, sVE, n_phi_ve)
                b.wait(sync, sTE, n_phi_te)
                gate(sync, ("dld", p16 % 2))
                b.inc(zsem, sync.dma_start(
                    out=dslot.rearrange("p (a f) -> p a f", f=128),
                    in_=dsp[:, p16 * 128:(p16 + 1) * 128]
                        .rearrange("(a p) f -> p a f", p=128)), 16)
            block.sync(ldd)
            nldd = b.n(zsem)
            pbank = PA if (p16 % 2 == 0) else PB

            def mmx(t, p16=p16, dslot=dslot, pbank=pbank, nldd=nldd, zsem=zsem):
                b.wait(t, zsem, nldd)
                gate(t, ("bank", id(pbank)))
                dv = dslot.rearrange("p (a f) -> p a f", f=128)
                for half in range(2):
                    for tt in range(NT):
                        ins = t.matmul(pbank[:, half * 512:(half + 1) * 512],
                                       dv[:, tt, :],
                                       tmt(B2, tt)[:, half * 512:(half + 1) * 512],
                                       start=(tt == 0), stop=(tt == NT - 1))
                b.inc(sTE, ins)
            block.tensor(mmx)
            nmx = b.n(sTE)

            def evx(s, p16=p16, pbank=pbank, nmx=nmx):
                b.wait(s, sTE, nmx)
                gate(s, ("xstg", p16 % 2))
                gate(s, ("w", id(W0 if p16 % 2 == 0 else W1)))
                b.inc(sSC, s.copy(W0 if p16 % 2 == 0 else W1, pbank))
            bscal(evx)
            setg(("bank", id(pbank)), sSC)
            setg(("w", id(W0 if p16 % 2 == 0 else W1)), sSC)
            nex = b.n(sSC)

            wsem3 = sSPw0 if (p16 % 2 == 0) else sSPw1

            def spx(s, p16=p16, nex=nex, wsem3=wsem3):
                b.wait(s, sSC, nex)
                b.inc(wsem3, s.dma_start(out=rsi[p16 * 128:(p16 + 1) * 128, :],
                                         in_=W0 if p16 % 2 == 0 else W1), 16)
            bscal(spx)
            marks[("xstg", p16 % 2)] = (wsem3, b.n(wsem3))
            marks[("w", id(W0 if p16 % 2 == 0 else W1))] = (wsem3, b.n(wsem3))
            marks[("dld", p16 % 2)] = (sTE, nmx)
        n_xp = (b.n(sSPw0), b.n(sSPw1))

        def rs1(g):
            b.wait(g, sSPw0, n_xp[0])
            b.wait(g, sSPw1, n_xp[1])
            b.inc(sCC, g.collective_compute(
                "ReduceScatter", ALU.add, replica_groups=[list(range(NCORE))],
                ins=[rsi.opt()], outs=[rso.opt()]))
        block.gpsimd(rs1)

        def rs1rd(sync):
            sync.wait_ge(b.cnt[sCC][0], 2)
            b.wait(sync, sVE, n_phi_ve)
            b.wait(sync, sTE, n_phi_te)
            b.inc(sLD, sync.dma_start(
                out=XT.rearrange("p (a f) -> p a f", f=1024),
                in_=rso.rearrange("(a p) f -> p a f", p=128)), 16)
        block.sync(rs1rd)
        n_rs = b.n(sLD)

        def xt1(v):
            b.wait(v, sLD, n_rs)
            v.tensor_scalar_max(XT, XT, -33000.0)
            b.inc(sVE, v.tensor_scalar_min(XT, XT, 65000.0))
        bvec(xt1)
        n_clip = b.n(sVE)
        emit_ln(lambda i: XT[:, i * 1024:(i + 1) * 1024], 2,
                wait_list=[(sLD, n_rs), (sVE, n_clip)])
        nxt = b.n(sVE)

        def xt2(s, nxt=nxt):
            b.wait(s, sVE, nxt)
            b.inc(sSC, s.activation(XT, XT, ACT.Tanh))
        bscal(xt2)
        n_xt = b.n(sSC)

        srcsX = [XT[:, a * 1024 + dd * 128: a * 1024 + (dd + 1) * 128]
                 for dd in range(8) for a in range(2)]
        dstsX = [XTt[:, q * 512:(q + 1) * 512] for q in range(4)]
        emit_tp(srcsX, dstsX, wait_list=[(sSC, n_xt), (sCs1, n_phic[1])])
        n_xtt = b.n(sSC)

        def ldbq(sync):
            b.wait(sync, sSPc3, n_x2sp)
            b.wait(sync, sVE, n_y)
            b.inc(sCW, sync.dma_start(out=BQ, in_=biasq_d), 16)
        block.sync(ldbq)
        n_bq = b.n(sCW)

        qb = {(0, 0): PC, (0, 1): PD, (1, 0): PE_, (1, 1): PF}
        for q in range(2):
            for e in range(4):
                n_loc = q * 4 + e
                for dd in range(NCC):
                    wslot = W0 if ((q * 32 + e * 8 + dd) % 2 == 0) else W1

                    wsem = sW0d if ((q * 32 + e * 8 + dd) % 2 == 0) else sW1d

                    def ldw(sync, n_loc=n_loc, dd=dd, wslot=wslot, wsem=wsem):
                        gate(sync, ("w", id(wslot)))
                        b.inc(wsem, sync.dma_start(
                            out=wslot,
                            in_=expw[n_loc, dd * 128:(dd + 1) * 128, :]), 16)
                    block.sync(ldw)
                    nldw = b.n(wsem)

                    def mme(t, q=q, e=e, dd=dd, wslot=wslot, nldw=nldw, wsem=wsem):
                        b.wait(t, wsem, nldw)
                        b.wait(t, sSC, n_xtt)
                        if q == 0 and e == 0 and dd == 0:
                            for bank in (PC, PD, PE_, PF):
                                gate(t, ("bank", id(bank)))
                        for Hh in range(2):
                            bank = qb[(q, Hh)]
                            ins = t.matmul(
                                bank[32 * e:32 * (e + 1), 0:512],
                                XTt[:, dd * 256 + q * 128 + e * 32:
                                    dd * 256 + q * 128 + (e + 1) * 32],
                                wslot[:, Hh * 512:(Hh + 1) * 512],
                                start=(dd == 0), stop=(dd == NCC - 1),
                                tile_position=(0, 32 * e))
                        b.inc(sTE, ins)
                    block.tensor(mme)
                    setg(("w", id(wslot)), sTE)
        n_exp = b.n(sTE)

        def eve(v):
            b.wait(v, sTE, n_exp)
            b.wait(v, sCW, n_bq)
            for q in range(2):
                for Hh in range(2):
                    ins = v.tensor_add(
                        YT[:, q * 1024 + Hh * 512: q * 1024 + (Hh + 1) * 512],
                        qb[(q, Hh)][:, 0:512],
                        BQ[:, q * 1024 + Hh * 512: q * 1024 + (Hh + 1) * 512])
            b.inc(sVE, ins)
        bvec(eve)
        for bank in (PC, PD, PE_, PF):
            setg(("bank", id(bank)), sVE)
        n_yt = b.n(sVE)

        def ag2st(sync):
            b.wait(sync, sVE, n_yt)
            b.inc(sLD, sync.dma_start(
                out=ag2i.rearrange("(a p) f -> p a f", p=128),
                in_=YT.rearrange("p (a f) -> p a f", f=1024)), 16)
        block.sync(ag2st)
        n_ag2st = b.n(sLD)

        def ag2(g):
            b.wait(g, sLD, n_ag2st)
            b.inc(sCC, g.collective_compute(
                "AllGather", ALU.bypass, replica_groups=[list(range(NCORE))],
                ins=[ag2i.opt()], outs=[ag2o.opt()]))
        block.gpsimd(ag2)

        def ag2rd(sync):
            sync.wait_ge(b.cnt[sCC][0], 3)
            b.inc(sLD, sync.dma_start(
                out=B1.rearrange("p (a f) -> p a f", f=1024),
                in_=ag2o.rearrange("(a p) f -> p a f", p=128)), 16)
        block.sync(ag2rd)
        n_ag2 = b.n(sLD)

        ybanks = [
            [PA[:, 0:512], PA[:, 512:1024], PB[:, 0:512], PB[:, 512:1024]],
            [PC[:, 0:512], PD[:, 0:512], PE_[:, 0:512], PF[:, 0:512]],
        ]
        for pg in range(8):
            bks = ybanks[pg % 2]
            for np_ in range(16):
                cslot = (S0 if (np_ % 2 == 0) else S1)
                zsem = sZR0 if (np_ % 2 == 0) else sZR1

                def ldcf(sync, pg=pg, np_=np_, cslot=cslot, zsem=zsem):
                    b.wait(sync, sLD, n_ag2)
                    b.wait(sync, sCs0, n_phic[0])
                    b.wait(sync, sCs1, n_phic[1])
                    gate(sync, ("cld", np_ % 2))
                    b.inc(zsem, sync.dma_start(
                        out=cslot[:, 0:256],
                        in_=csp[np_ * 128:(np_ + 1) * 128, pg * 256:(pg + 1) * 256]), 16)
                block.sync(ldcf)
                nldc = b.n(zsem)

                def mmy(t, pg=pg, np_=np_, cslot=cslot, nldc=nldc, zsem=zsem,
                        bks=bks):
                    b.wait(t, zsem, nldc)
                    if np_ == 0:
                        gate(t, ("ybank", pg % 2))
                    for tt2 in range(2):
                        for half in range(2):
                            ins = t.matmul(
                                bks[tt2 * 2 + half],
                                cslot[:, tt2 * 128:(tt2 + 1) * 128],
                                B1[:, np_ * 1024 + half * 512: np_ * 1024 + (half + 1) * 512],
                                start=(np_ == 0), stop=(np_ == 15))
                    b.inc(sTE, ins)
                block.tensor(mmy)
                marks[("cld", np_ % 2)] = (sTE, b.n(sTE))
            nym = b.n(sTE)

            for tt2 in range(2):
                tt = pg * 2 + tt2

                def evy(v, pg=pg, tt2=tt2, tt=tt, nym=nym, bks=bks):
                    b.wait(v, sTE, nym)
                    tapwait(v)
                    v.tensor_copy(tmt(B2, tt)[:, 0:512], bks[tt2 * 2])
                    b.inc(sVE, v.tensor_copy(tmt(B2, tt)[:, 512:1024], bks[tt2 * 2 + 1]))
                bvec(evy)
            nevy = b.n(sVE)

            emit_ln(lambda i, pg=pg: tmt(B2, pg * 2 + i), 2,
                    wait_list=[(sVE, nevy)])
            nlny = b.n(sVE)
            marks[("ybank", pg % 2)] = (sVE, nlny)
            for tt2 in range(2):
                tt = pg * 2 + tt2
                zsem2 = sRG0 if (tt % 2 == 0) else sRG1

                def ldx2(sync, tt=tt, zsem2=zsem2):
                    b.wait(sync, sSPc3, n_x2sp)
                    gate(sync, ("x2res", tt % 2))
                    b.inc(zsem2, sync.dma_start(
                        out=(S0 if tt % 2 == 0 else S1)[:, 1024:2048],
                        in_=x2sp[:, tt * 1024:(tt + 1) * 1024]), 16)
                block.sync(ldx2)
                nlx2 = b.n(zsem2)

                def addo(v, tt=tt, nlx2=nlx2, nlny=nlny, zsem2=zsem2):
                    b.wait(v, zsem2, nlx2)
                    b.wait(v, sVE, nlny)
                    b.inc(sVE, v.tensor_add(
                        tmt(B2, tt), tmt(B2, tt),
                        (S0 if tt % 2 == 0 else S1)[:, 1024:2048]))
                bvec(addo)
                marks[("x2res", tt % 2)] = (sVE, b.n(sVE))
                nado = b.n(sVE)

                def sto(s, tt=tt, nado=nado):
                    b.wait(s, sVE, nado)
                    b.inc(sSPo, s.dma_start(
                        out=o_out[tt * 128:(tt + 1) * 128, :], in_=tmt(B2, tt)), 16)
                bscal(sto)
        n_store = b.n(sSPo)
        n_taps = b.n(sSPt)

        def fin(sync):
            b.wait(sync, sSPo, n_store)
            if n_taps:
                b.wait(sync, sSPt, n_taps)
        block.sync(fin)

    return nc, d_dbg


_NC_CACHE = {}


def _get_nc(dbg=()):
    key = tuple(sorted(dbg))
    if key not in _NC_CACHE:
        _NC_CACHE[key] = build_nc(dbg)
    return _NC_CACHE[key]


def make_inmaps(inputs, dbg=()):
    x = np.ascontiguousarray(np.asarray(inputs["L0_attn_x"], dtype=np.float32)[0])

    def t(a):
        return np.ascontiguousarray(np.asarray(a, dtype=np.float32).T)

    def p8(a):
        return np.ascontiguousarray(
            np.asarray(a, dtype=np.float32).reshape(NCC, 128).T)

    base = {
        "wvT": t(inputs["L0_attn_att_Wv"]),
        "wkT": t(inputs["L0_attn_att_Wk"]),
        "wrT": t(inputs["L0_attn_att_Wr"]),
        "woT": t(inputs["L0_attn_att_Wo"]),
        "fwkT": t(inputs["L0_attn_ffn_Wk"]),
        "fwvT": t(inputs["L0_attn_ffn_Wv"]),
        "fwrT": t(inputs["L0_attn_ffn_Wr"]),
        "phiT": t(inputs["L0_ffn_phi_w"]),
        "phib": np.ascontiguousarray(np.broadcast_to(
            np.asarray(inputs["L0_ffn_phi_b"], dtype=np.float32), (128, E2))),
        "md": np.ascontiguousarray(
            np.repeat(np.eye(64, dtype=np.float32), 32, axis=1)),
        "mc": np.ascontiguousarray(
            np.tile(np.eye(32, dtype=np.float32), (1, 64))),
        "iden": np.eye(128, dtype=np.float32),
        "ln1g": p8(inputs["L0_attn_ln1_g"]),
        "ln1b": p8(inputs["L0_attn_ln1_b"]),
        "ln2g": p8(inputs["L0_attn_ln2_g"]),
        "ln2b": p8(inputs["L0_attn_ln2_b"]),
        "tmk": p8(inputs["L0_attn_time_mix_k"]),
        "tmv": p8(inputs["L0_attn_time_mix_v"]),
        "tmr": p8(inputs["L0_attn_time_mix_r"]),
        "cmk": p8(inputs["L0_attn_cm_k"]),
        "cmr": p8(inputs["L0_attn_cm_r"]),
        "decay": p8(inputs["L0_attn_time_decay"]),
        "u8": p8(inputs["L0_attn_time_first"]),
    }
    expw_f = np.asarray(inputs["L0_ffn_expert_w"], dtype=np.float32)
    expb_f = np.asarray(inputs["L0_ffn_expert_b"], dtype=np.float32)

    in_maps = []
    for m in range(NCORE):
        im = dict(base)
        xs = np.zeros((TC + 1, C), dtype=np.float32)
        xs[1:] = x[m * TC:(m + 1) * TC]
        if m > 0:
            xs[0] = x[m * TC - 1]
        im["x_sl"] = xs
        im["expw"] = np.ascontiguousarray(expw_f[m * NEC:(m + 1) * NEC])
        bq = np.zeros((128, 2 * C), dtype=np.float32)
        for q in range(2):
            for e in range(4):
                n_loc = q * 4 + e
                bq[e * 32:(e + 1) * 32, q * C:(q + 1) * C] = \
                    expb_f[m * NEC + n_loc][None, :]
        im["biasq"] = bq
        ms = np.zeros((128, NCORE), dtype=np.float32)
        mp = np.full((128, NCORE), NEG, dtype=np.float32)
        ms[:, :m] = 1.0
        mp[:, :m] = 0.0
        im["maskS"], im["maskP"] = ms, mp
        m2s = np.zeros((128, NCORE), dtype=np.float32)
        m2p = np.full((128, NCORE), NEG, dtype=np.float32)
        m2s[:, :max(m - 1, 0)] = 1.0
        m2p[:, :max(m - 1, 0)] = 0.0
        im["m2S"], im["m2P"] = m2s, m2p
        m3s = np.zeros((128, NCORE), dtype=np.float32)
        m3p = np.full((128, NCORE), NEG, dtype=np.float32)
        if m > 0:
            m3s[:, m - 1] = 1.0
            m3p[:, m - 1] = 0.0
        im["m3S"], im["m3P"] = m3s, m3p
        im["c0m"] = np.full((128, 1), 0.0 if m == 0 else 1.0, dtype=np.float32)
        in_maps.append(im)
    return in_maps


def kernel(**inputs):
    nc, _ = _get_nc()
    in_maps = make_inmaps(inputs)
    res = run_bass_kernel_spmd(nc, in_maps, core_ids=list(range(NCORE)))
    out = np.concatenate([res.results[m]["o_out"] for m in range(NCORE)], axis=0)
    return out[None, :, :].astype(np.float32)


# version 14
# speedup vs baseline: 1.0113x; 1.0025x over previous
import numpy as np
import concourse.bass as bass
import concourse.mybir as mybir
from concourse.bass_utils import run_bass_kernel_spmd

F32 = mybir.dt.float32
ALU = mybir.AluOpType
ACT = mybir.ActivationFunctionType
AX = mybir.AxisListType

NCORE = 8
T = 16384
TC = 2048
C = 1024
NCC = 8
NT = 16
H4 = 4096
NEC = 8
E2 = 2048
EPS = 1e-5
CLAMP = 80.0
NEG = -1e38


class Bld:
    def __init__(self, nc, block):
        self.nc = nc
        self.block = block
        self.cnt = {}

    def sem(self, name):
        self.cnt[name] = [self.nc.alloc_semaphore(name), 0]
        return name

    def inc(self, name, ins, k=1):
        ins.then_inc(self.cnt[name][0], k)
        self.cnt[name][1] += k
        return ins

    def n(self, name):
        return self.cnt[name][1]

    def wait(self, eng, name, val=None):
        h, v = self.cnt[name]
        eng.wait_ge(h, v if val is None else val)


class AutoDrain:
    _PASS = {"wait_ge", "drain", "engine_nop", "sem_inc", "nop", "wait_op"}

    def __init__(self, eng, state):
        self._e = eng
        self._st = state

    @staticmethod
    def _span(ap):
        try:
            name = ap.name
            name = name() if callable(name) else name
        except Exception:
            return None
        try:
            off = int(ap.offset)
            reach = 1
            for st, ct in ap.ap:
                reach += abs(int(st)) * (int(ct) - 1)
            return (str(name), off, off + reach)
        except Exception:
            return (str(name), 0, 1 << 60)

    @staticmethod
    def _hits(spans, lst):
        return any(s[0] == wn and s[1] < whi and wlo < s[2]
                   for s in spans for (wn, wlo, whi) in lst)

    def __getattr__(self, nm):
        fn = getattr(self._e, nm)
        if nm in self._PASS or not callable(fn):
            return fn

        def wrap(*a, **k):
            aps = [x for x in a if isinstance(x, bass.AP)]
            aps += [x for x in k.values() if isinstance(x, bass.AP)]
            outs, ins = [], []
            if aps and nm != "dma_start":
                outs = [self._span(aps[0])]
                ac = k.get("accum_out")
                if isinstance(ac, bass.AP):
                    outs.append(self._span(ac))
                ins = [self._span(x) for x in aps[1:] if not isinstance(x, type(None))]
            else:
                ins = [self._span(x) for x in aps]
            outs = [s for s in outs if s is not None]
            ins = [s for s in ins if s is not None]
            need = (self._hits(outs, self._st["r"]) or
                    self._hits(outs, self._st["w"]) or
                    self._hits(ins, self._st["w"]))
            if need:
                self._e.drain()
                self._st["w"] = []
                self._st["r"] = []
            res = fn(*a, **k)
            self._st["w"].extend(outs)
            self._st["r"].extend(ins)
            return res
        return wrap


def build_nc(dbg=()):
    dbg = set(dbg)
    nc = bass.Bass("TRN2", target_bir_lowering=False,
                   dynamic_dma_scratch_size=4096)

    def di(name, shape):
        return nc.dram_tensor(name, list(shape), F32, kind="ExternalInput").ap()

    x_sl = di("x_sl", (TC + 1, C))
    wvT = di("wvT", (C, C))
    wkT = di("wkT", (C, C))
    wrT = di("wrT", (C, C))
    woT = di("woT", (C, C))
    fwkT = di("fwkT", (C, H4))
    fwvT = di("fwvT", (H4, C))
    fwrT = di("fwrT", (C, C))
    phiT_d = di("phiT", (C, E2))
    phib_d = di("phib", (128, E2))
    expw = di("expw", (NEC, C, C))
    biasq_d = di("biasq", (128, 2 * C))
    md_d = di("md", (64, E2))
    mc_d = di("mc", (32, E2))
    iden = di("iden", (128, 128))
    pn = ["ln1g", "ln1b", "ln2g", "ln2b", "tmk", "tmv", "tmr", "cmk", "cmr",
          "decay", "u8"]
    p_in = {nm: di(nm, (128, NCC)) for nm in pn}
    mk_nm = ["maskS", "maskP", "m2S", "m2P", "m3S", "m3P"]
    mk_in = {nm: di(nm, (128, NCORE)) for nm in mk_nm}
    c0m = di("c0m", (128, 1))

    o_out = nc.dram_tensor("o_out", [TC, C], F32, kind="ExternalOutput").ap()

    z0sp = nc.dram_tensor("z0sp", [128, NCC * TC], F32).ap()
    ksp = nc.dram_tensor("ksp", [128, NCC * TC], F32).ap()
    vsp = nc.dram_tensor("vsp", [128, NCC * TC], F32).ap()
    rsp = nc.dram_tensor("rsp", [128, NCC * TC], F32).ap()
    xasp = nc.dram_tensor("xasp", [128, NCC * TC], F32).ap()
    sgsp = nc.dram_tensor("sgsp", [128, NCC * TC], F32).ap()
    x2sp = nc.dram_tensor("x2sp", [128, NT * C], F32).ap()
    dsp = nc.dram_tensor("dsp", [TC, E2], F32).ap()
    csp = nc.dram_tensor("csp", [E2, TC], F32).ap()
    ag1i = nc.dram_tensor("ag1i", [128, 72], F32).ap()
    ag1o = nc.dram_tensor("ag1o", [128 * NCORE, 72], F32).ap()
    rsi = nc.dram_tensor("rsi", [E2, C], F32).ap()
    rso = nc.dram_tensor("rso", [E2 // NCORE, C], F32).ap()
    ag2i = nc.dram_tensor("ag2i", [E2 // NCORE, C], F32).ap()
    ag2o = nc.dram_tensor("ag2o", [E2, C], F32).ap()

    B1 = nc.alloc_sbuf_tensor("B1", [128, 16384], F32).ap()
    B2 = nc.alloc_sbuf_tensor("B2", [128, 16384], F32).ap()
    B3 = nc.alloc_sbuf_tensor("B3", [128, 16384], F32).ap()
    S0 = nc.alloc_sbuf_tensor("S0", [128, 2048], F32).ap()
    S1 = nc.alloc_sbuf_tensor("S1", [128, 2048], F32).ap()
    W0 = nc.alloc_sbuf_tensor("W0", [128, 1024], F32).ap()
    W1 = nc.alloc_sbuf_tensor("W1", [128, 1024], F32).ap()

    sm = {}

    def asm(name, shape):
        sm[name] = nc.alloc_sbuf_tensor("s_" + name, list(shape), F32).ap()
        return sm[name]

    IDN = asm("IDN", (128, 128))
    prm = {nm: asm("p_" + nm, (128, NCC)) for nm in pn}
    w8 = asm("w8", (128, NCC))
    tw8 = asm("tw8", (128, NCC))
    tw47 = asm("tw47", (128, NCC))
    MK = {nm: asm("m_" + nm, (128, NCORE)) for nm in mk_nm}
    C0M = asm("C0M", (128, 1))
    ONESC = asm("ONESC", (128, 1))
    z0b = asm("z0b", (128, NCC))
    xab = asm("xab", (128, NCC))
    xcb = asm("xcb", (128, NCC))
    stat = asm("stat", (128, 96))
    stat1 = asm("stat1", (1, 24))
    sS = {nm: asm(nm, (128, NCC)) for nm in
          ["PLe", "ALe", "BLe", "PLp", "ALp", "BLp", "KL", "VL", "RL"]}
    ppin = asm("ppin", (128, NCC))
    aain = asm("aain", (128, NCC))
    bbin = asm("bbin", (128, NCC))
    f = {nm: asm(nm, (128, NCC)) for nm in
         ["f1", "f2", "f3", "f4", "f5", "f6", "f7", "fA", "fB", "fP",
          "bk8", "bv8", "br8", "bm2x"]}

    PA = nc.alloc_psum_tensor("PA", [128, 1024], F32).ap()
    PB = nc.alloc_psum_tensor("PB", [128, 1024], F32).ap()
    PC = nc.alloc_psum_tensor("PC", [128, 512], F32).ap()
    PD = nc.alloc_psum_tensor("PD", [128, 512], F32).ap()
    PE_ = nc.alloc_psum_tensor("PE_", [128, 512], F32).ap()
    PF = nc.alloc_psum_tensor("PF", [128, 512], F32).ap()

    def sl(buf, i, w=2048):
        return buf[:, i * w:(i + 1) * w]

    def tmt(buf, tt):
        return buf[:, tt * 1024:(tt + 1) * 1024]

    VBUF = B1[:, 0:2048]
    R_W = B1[:, 2048:4096]
    R_PL = B1[:, 4096:6145]
    R_AL = B1[:, 6145:8194]
    R_BL = B1[:, 8194:10243]
    R_T2 = B1[:, 10243:12291]
    R_T3 = B1[:, 12291:14339]
    ROWB = W1[0:1, 0:1024]
    PHIB = B3[:, 0:2048]
    MDs = B3[0:64, 2048:4096]
    MCs = B3[0:32, 4096:6144]
    CSTG = B3[:, 6144:8192]
    XT = B3[:, 8192:10240]
    XTt = B3[:, 10240:12288]
    YT = B3[:, 12288:14336]
    BQ = B3[:, 14336:16384]
    agbuf = B3[:, 8192:8768].rearrange("p (r f) -> p r f", f=72)
    rDT = asm("rDT", (64, 128))
    rCT = asm("rCT", (32, 128))
    sD = stat[:, 0:64]
    sCt = stat[:, 64:96]
    ONESR = S0[0:1, 1024:1152]

    d_dbg = {}

    def mktap(name, shape):
        d_dbg[name] = nc.dram_tensor("dbg_" + name, list(shape), F32,
                                     kind="ExternalOutput").ap()
        return d_dbg[name]

    with nc.Block() as block:
        b = Bld(nc, block)

        _wve = {"w": [], "r": []}
        _wsc = {"w": [], "r": []}

        def bvec(fn):
            block.vector(lambda e: fn(AutoDrain(e, _wve)))

        def bscal(fn):
            block.scalar(lambda e: fn(AutoDrain(e, _wsc)))
        sLD = b.sem("sLD")
        sSP = b.sem("sSP")
        sTE = b.sem("sTE")
        sVE = b.sem("sVE")
        sSC = b.sem("sSC")
        sGP = b.sem("sGP")
        sCC = b.sem("sCC")
        sW0d = b.sem("sW0d")
        sW1d = b.sem("sW1d")
        sRG0 = b.sem("sRG0")
        sRG1 = b.sem("sRG1")
        sZR0 = b.sem("sZR0")
        sZR1 = b.sem("sZR1")
        sZR2 = b.sem("sZR2")
        sZR3 = b.sem("sZR3")
        sCW = b.sem("sCW")
        sSPb = b.sem("sSPb")
        sSP0 = b.sem("sSP0")
        sSP1 = b.sem("sSP1")
        sSPa = b.sem("sSPa")
        sSPc3 = b.sem("sSPc3")
        sSPw0 = b.sem("sSPw0")
        sSPw1 = b.sem("sSPw1")
        sSPo = b.sem("sSPo")
        sVs0 = b.sem("sVs0"); sVs1 = b.sem("sVs1")
        sKs0 = b.sem("sKs0"); sKs1 = b.sem("sKs1")
        sRs0 = b.sem("sRs0"); sRs1 = b.sem("sRs1")
        sDsp = b.sem("sDsp")
        sCs0 = b.sem("sCs0"); sCs1 = b.sem("sCs1")
        sSPt = b.sem("sSPt")
        marks = {}

        def gate(eng, key):
            if key in marks:
                nm, v = marks[key]
                b.wait(eng, nm, v)

        def setg(key, nm):
            marks[key] = (nm, b.n(nm))

        def tapwait(eng):
            if b.n(sSPt):
                b.wait(eng, sSPt, b.n(sSPt))

        def emit_ln(get_tile, ntile, out_tile=None, wait_list=()):
            P0 = get_tile(0).shape[0]
            if P0 != 128:
                _emit_ln_one(get_tile(0), get_tile(0) if out_tile is None
                             else out_tile(0), wait_list)
                return
            for i in range(ntile):
                tile = get_tile(i)
                F = tile.shape[-1]
                c = i * 6

                def v1(v, tile=tile, F=F, c=c, i=i, wl=tuple(wait_list)):
                    if i == 0:
                        for (nm, val) in wl:
                            b.wait(v, nm, val)
                        gate(v, ("w", id(W0)))
                    v.reduce_sum(stat[:, c:c + 1], tile, AX.X)
                    v.scalar_tensor_tensor(W0[:, 0:F], tile, 1.0, tile,
                                           ALU.mult, ALU.mult,
                                           accum_out=stat[:, c + 1:c + 2])
                    v.tensor_scalar_mul(stat[:, c + 2:c + 3], stat[:, c:c + 1], 1.0 / F)
                    v.tensor_scalar_mul(stat[:, c + 3:c + 4], stat[:, c + 1:c + 2], 1.0 / F)
                    v.scalar_tensor_tensor(stat[:, c + 4:c + 5], stat[:, c + 2:c + 3],
                                           stat[:, c + 2:c + 3], stat[:, c + 3:c + 4],
                                           ALU.mult, ALU.subtract)
                    ins = v.tensor_scalar(stat[:, c + 4:c + 5], stat[:, c + 4:c + 5],
                                          -1.0, EPS, ALU.mult, ALU.add)
                    if i == ntile - 1:
                        b.inc(sVE, ins)
                bvec(v1)
            setg(("w", id(W0)), sVE)
            nve = b.n(sVE)

            def s2(s, nve=nve, ntile=ntile):
                b.wait(s, sVE, nve)
                for i in range(ntile):
                    c = i * 6
                    ins = s.activation(stat[:, c + 5:c + 6], stat[:, c + 4:c + 5],
                                       ACT.Sqrt)
                b.inc(sSC, ins)
            bscal(s2)
            nsc = b.n(sSC)

            for i in range(ntile):
                tile = get_tile(i)
                dst = tile if out_tile is None else out_tile(i)
                c = i * 6

                def v3(v, tile=tile, dst=dst, c=c, i=i, nsc=nsc):
                    if i == 0:
                        b.wait(v, sSC, nsc)
                    v.reciprocal(stat[:, c:c + 1], stat[:, c + 5:c + 6])
                    ins = v.tensor_scalar(dst, tile, stat[:, c + 2:c + 3],
                                          stat[:, c:c + 1], ALU.subtract, ALU.mult)
                    if i == ntile - 1:
                        b.inc(sVE, ins)
                bvec(v3)

        def _emit_ln_one(tile, dst, wait_list):
            F = tile.shape[-1]
            P = tile.shape[0]
            st = stat1

            def v1(v, wl=tuple(wait_list)):
                for (nm, val) in wl:
                    b.wait(v, nm, val)
                gate(v, ("w", id(W0)))
                v.reduce_sum(st[:, 0:1], tile, AX.X)
                v.scalar_tensor_tensor(W0[0:P, 0:F], tile, 1.0, tile,
                                       ALU.mult, ALU.mult, accum_out=st[:, 1:2])
                v.tensor_scalar_mul(st[:, 2:3], st[:, 0:1], 1.0 / F)
                v.tensor_scalar_mul(st[:, 3:4], st[:, 1:2], 1.0 / F)
                v.scalar_tensor_tensor(st[:, 4:5], st[:, 2:3], st[:, 2:3],
                                       st[:, 3:4], ALU.mult, ALU.subtract)
                b.inc(sVE, v.tensor_scalar(st[:, 4:5], st[:, 4:5], -1.0, EPS,
                                           ALU.mult, ALU.add))
            bvec(v1)
            setg(("w", id(W0)), sVE)
            nve1 = b.n(sVE)

            def s2(s, nve1=nve1):
                b.wait(s, sVE, nve1)
                b.inc(sSC, s.activation(st[:, 5:6], st[:, 4:5], ACT.Sqrt))
            bscal(s2)
            nsc2 = b.n(sSC)

            def v3(v, nsc2=nsc2):
                b.wait(v, sSC, nsc2)
                v.reciprocal(st[:, 6:7], st[:, 5:6])
                b.inc(sVE, v.tensor_scalar(dst, tile, st[:, 2:3],
                                           st[:, 6:7], ALU.subtract, ALU.mult))
            bvec(v3)

        tps = [0]

        def emit_tp(srcs, dsts512, wait_list=()):
            nb = (len(srcs) + 3) // 4
            assert len(dsts512) == nb
            for bb_ in range(nb):
                bank = PE_ if (tps[0] % 2 == 0) else PF
                tps[0] += 1
                lo, hi = bb_ * 4, min(bb_ * 4 + 4, len(srcs))

                def t1(t, lo=lo, hi=hi, bank=bank, wl=tuple(wait_list)):
                    for (nm, v) in wl:
                        b.wait(t, nm, v)
                    gate(t, ("bank", id(bank)))
                    for j in range(lo, hi):
                        ins = t.transpose(bank[:, (j - lo) * 128:(j - lo + 1) * 128],
                                          srcs[j], IDN)
                    b.inc(sTE, ins)
                block.tensor(t1)
                nte = b.n(sTE)

                def s1(s, bank=bank, bb_=bb_, nte=nte, lo=lo, hi=hi):
                    b.wait(s, sTE, nte)
                    b.inc(sSC, s.copy(dsts512[bb_][:, 0:(hi - lo) * 128],
                                      bank[:, 0:(hi - lo) * 128]))
                bscal(s1)
                setg(("bank", id(bank)), sSC)

        def p0_loads(sync):
            for nm in pn:
                b.inc(sLD, sync.dma_start(out=prm[nm], in_=p_in[nm]), 16)
            for nm in mk_nm:
                b.inc(sLD, sync.dma_start(out=MK[nm], in_=mk_in[nm]), 16)
            b.inc(sLD, sync.dma_start(out=IDN, in_=iden), 16)
            b.inc(sLD, sync.dma_start(out=C0M, in_=c0m), 16)
            b.inc(sLD, sync.dma_start(
                out=B1.rearrange("p (a f) -> p a f", f=1024),
                in_=x_sl[1:TC + 1, :].rearrange("(a p) f -> p a f", p=128)), 16)
            b.inc(sLD, sync.dma_start(out=ROWB, in_=x_sl[0:1, :]), 16)
        block.sync(p0_loads)
        n_ld0 = b.n(sLD)

        def p0_ones(v):
            b.wait(v, sLD, n_ld0)
            b.inc(sVE, v.memset(ONESC, 1.0))
        bvec(p0_ones)

        def p0_w(s):
            b.wait(s, sLD, n_ld0)
            b.inc(sSC, s.activation(w8, prm["decay"], ACT.Exp))
        bscal(p0_w)
        n_w = b.n(sSC)

        def p0_w2(v):
            b.wait(v, sSC, n_w)
            v.tensor_scalar_mul(w8, w8, -1.0)
            v.tensor_scalar_mul(tw8, w8, float(TC))
            b.inc(sVE, v.tensor_scalar_mul(tw47, w8, float(TC - 1)))
        bvec(p0_w2)

        emit_ln(lambda i: tmt(B1, i), NT)
        emit_ln(lambda i: ROWB, 1)
        setg(("w", id(W1)), sVE)
        n_z0 = b.n(sVE)

        for i in range(NCC):
            srcs = [tmt(B1, tt)[:, i * 128:(i + 1) * 128] for tt in range(NT)]
            dsts = [B2[:, i * 2048 + q * 512: i * 2048 + (q + 1) * 512]
                    for q in range(4)]
            emit_tp(srcs, dsts, wait_list=[(sVE, n_z0)])
        n_z0fm = b.n(sSC)

        def p0_bt(t):
            b.wait(t, sVE, n_z0)
            gate(t, ("bank", id(PE_)))
            for i in range(NCC):
                ins = t.transpose(PE_[:, i:i + 1], ROWB[:, i * 128:(i + 1) * 128],
                                  IDN[0:1, 0:1])
            b.inc(sTE, ins)
        block.tensor(p0_bt)
        n_bt = b.n(sTE)

        def p0_bt2(s):
            b.wait(s, sTE, n_bt)
            b.inc(sSC, s.copy(z0b, PE_[:, 0:NCC]))
        bscal(p0_bt2)
        setg(("bank", id(PE_)), sSC)
        setg(("w", id(W1)), sTE)
        n_z0b = b.n(sSC)

        def p0_spill(s):
            b.wait(s, sSC, n_z0fm)
            b.inc(sSPb, s.dma_start(out=z0sp, in_=B2), 16)
        bscal(p0_spill)
        n_z0sp = b.n(sSPb)

        if "z0fm" in dbg:
            tap = mktap("z0fm", (128, 16384))

            def dt0(s, tap=tap):
                b.wait(s, sSC, n_z0fm)
                b.inc(sSPt, s.dma_start(out=tap, in_=B2), 16)
            bscal(dt0)
            setg(("tap", id(B2)), sSPt)

        def p0_xa(v):
            b.wait(v, sSC, n_z0fm)
            for i in range(NCC):
                v.tensor_scalar(sl(B3, i), sl(B2, i), prm["ln1g"][:, i:i + 1],
                                prm["ln1b"][:, i:i + 1], ALU.mult, ALU.add)
            b.wait(v, sSC, n_z0b)
            v.tensor_mul(xab, z0b, prm["ln1g"])
            v.tensor_add(xab, xab, prm["ln1b"])
            b.inc(sVE, v.tensor_scalar_mul(xab, xab, C0M[:, 0:1]))
        bvec(p0_xa)
        n_xa = b.n(sVE)

        def pa_mix(v):
            b.wait(v, sVE, n_xa)
            b.wait(v, sSPb, n_z0sp)
            tapwait(v)
            for i in range(NCC):
                xa_i = sl(B3, i)
                v.tensor_sub(S0[:, 1:2048], xa_i[:, 1:2048], xa_i[:, 0:2047])
                v.tensor_sub(S0[:, 0:1], xa_i[:, 0:1], xab[:, i:i + 1])
                for (buf, tnm) in ((B1, "tmv"), (B2, "tmr"), (B3, "tmk")):
                    dst = sl(buf, i)
                    v.scalar_tensor_tensor(dst[:, 1:2048], S0[:, 1:2048],
                                           prm[tnm][:, i:i + 1], xa_i[:, 0:2047],
                                           ALU.mult, ALU.add)
                    ins = v.scalar_tensor_tensor(dst[:, 0:1], S0[:, 0:1],
                                                 prm[tnm][:, i:i + 1],
                                                 xab[:, i:i + 1], ALU.mult, ALU.add)
                b.inc(sVE, ins)
        bvec(pa_mix)
        n_mix = b.n(sVE)

        def proj(pnm, wT, mixbuf, spdst, lastc, ssems):
            for j in range(NCC):
                wslot = W0 if (j % 2 == 0) else W1

                wsem = sW0d if (j % 2 == 0) else sW1d

                def ld(sync, wT=wT, j=j, wslot=wslot, wsem=wsem):
                    gate(sync, ("w", id(wslot)))
                    b.inc(wsem, sync.dma_start(
                        out=wslot.rearrange("p (a f) -> p a f", f=128),
                        in_=wT[:, j * 128:(j + 1) * 128]
                            .rearrange("(a p) f -> p a f", p=128)), 16)
                block.sync(ld)
                nld = b.n(wsem)
                sslot = S0 if (j % 2 == 0) else S1
                for tb in range(4):
                    bank = PC if (tb % 2 == 0) else PD

                    wsem2 = sW0d if (j % 2 == 0) else sW1d

                    def mm(t, j=j, tb=tb, wslot=wslot, mixbuf=mixbuf, bank=bank,
                           nld=nld, sslot=sslot, wsem2=wsem2):
                        b.wait(t, wsem2, nld)
                        b.wait(t, sVE, n_mix)
                        gate(t, ("bank", id(bank)))
                        if tb == 0:
                            gate(t, ("spill", id(sslot)))
                            gate(t, ("lc", id(sslot)))
                        wv = wslot.rearrange("p (a f) -> p a f", f=128)
                        for ci in range(NCC):
                            ins = t.matmul(bank[:, 0:512], wv[:, ci, :],
                                           sl(mixbuf, ci)[:, tb * 512:(tb + 1) * 512],
                                           start=(ci == 0), stop=(ci == NCC - 1))
                        b.inc(sTE, ins)
                    block.tensor(mm)
                    setg(("w", id(wslot)), sTE)
                    nte = b.n(sTE)

                    def ev(s, tb=tb, bank=bank, nte=nte, sslot=sslot):
                        b.wait(s, sTE, nte)
                        b.inc(sSC, s.copy(sslot[:, tb * 512:(tb + 1) * 512],
                                          bank[:, 0:512]))
                    bscal(ev)
                    setg(("bank", id(bank)), sSC)
                nev = b.n(sSC)

                def lc(v, j=j, sslot=sslot, lastc=lastc, nev=nev):
                    b.wait(v, sSC, nev)
                    b.inc(sVE, v.tensor_copy(lastc[:, j:j + 1], sslot[:, 2047:2048]))
                bvec(lc)
                setg(("lc", id(sslot)), sVE)

                ssem = ssems[j % 2]

                def spill(s, j=j, sslot=sslot, spdst=spdst, nev=nev, ssem=ssem):
                    b.wait(s, sSC, nev)
                    b.inc(ssem, s.dma_start(out=spdst[:, j * 2048:(j + 1) * 2048],
                                            in_=sslot), 16)
                bscal(spill)
                setg(("spill", id(sslot)), ssem)

        proj("v", wvT, B1, vsp, sS["VL"], (sVs0, sVs1))
        n_v_sp = (b.n(sVs0), b.n(sVs1))

        proj("r", wrT, B2, rsp, sS["RL"], (sRs0, sRs1))
        n_r_sp = (b.n(sRs0), b.n(sRs1))
        n_r_lc = b.n(sVE)

        ZT = B2[:, 12288:14336]

        def zts(s):
            b.wait(s, sRs0, n_r_sp[0])
            b.wait(s, sRs1, n_r_sp[1])
            b.inc(sSC, s.memzero(ZT))
        bscal(zts)
        nprev1 = n_mix
        for j in range(NCC):
            wslot = W0 if (j % 2 == 0) else W1
            wsem = sW0d if (j % 2 == 0) else sW1d
            sslot = S0 if (j % 2 == 0) else S1
            ksem = sKs0 if (j % 2 == 0) else sKs1

            def ldw(sync, j=j, wslot=wslot, wsem=wsem):
                gate(sync, ("w", id(wslot)))
                b.inc(wsem, sync.dma_start(
                    out=wslot.rearrange("p (a f) -> p a f", f=128),
                    in_=wkT[:, j * 128:(j + 1) * 128]
                        .rearrange("(a p) f -> p a f", p=128)), 16)
            block.sync(ldw)
            nld = b.n(wsem)

            def ldv(sync, j=j):
                b.wait(sync, sVs0, n_v_sp[0])
                b.wait(sync, sVs1, n_v_sp[1])
                gate(sync, ("vbuf",))
                b.inc(sRG0, sync.dma_start(out=VBUF,
                                           in_=vsp[:, j * 2048:(j + 1) * 2048]), 16)
            block.sync(ldv)
            nldv = b.n(sRG0)

            for tb in range(4):
                bank = PC if (tb % 2 == 0) else PD

                def mm(t, j=j, tb=tb, wslot=wslot, bank=bank, nld=nld, wsem=wsem,
                       sslot=sslot):
                    b.wait(t, wsem, nld)
                    b.wait(t, sVE, n_mix)
                    gate(t, ("bank", id(bank)))
                    if tb == 0:
                        gate(t, ("spill", id(sslot)))
                        gate(t, ("lc", id(sslot)))
                        gate(t, ("pscan", id(sslot)))
                    wv = wslot.rearrange("p (a f) -> p a f", f=128)
                    for ci in range(NCC):
                        ins = t.matmul(bank[:, 0:512], wv[:, ci, :],
                                       sl(B3, ci)[:, tb * 512:(tb + 1) * 512],
                                       start=(ci == 0), stop=(ci == NCC - 1))
                    b.inc(sTE, ins)
                block.tensor(mm)
                setg(("w", id(wslot)), sTE)
                nte = b.n(sTE)

                def ev(s, tb=tb, bank=bank, nte=nte, sslot=sslot):
                    b.wait(s, sTE, nte)
                    b.inc(sSC, s.copy(sslot[:, tb * 512:(tb + 1) * 512],
                                      bank[:, 0:512]))
                bscal(ev)
                setg(("bank", id(bank)), sSC)
            nev = b.n(sSC)

            def lc(v, j=j, sslot=sslot, nev=nev):
                b.wait(v, sSC, nev)
                b.inc(sVE, v.tensor_copy(sS["KL"][:, j:j + 1], sslot[:, 2047:2048]))
            bvec(lc)
            setg(("lc", id(sslot)), sVE)

            def spill(s, j=j, sslot=sslot, nev=nev, ksem=ksem):
                b.wait(s, sSC, nev)
                b.inc(ksem, s.dma_start(out=ksp[:, j * 2048:(j + 1) * 2048],
                                        in_=sslot), 16)
            bscal(spill)
            setg(("spill", id(sslot)), ksem)

            def s0p(s, j=j, nprev1=nprev1):
                b.wait(s, sVE, nprev1)
                b.inc(sSC, s.add(R_W, ZT, w8[:, j:j + 1]))
            bscal(s0p)
            ns0p = b.n(sSC)

            def v1(v, j=j, sslot=sslot, ns0p=ns0p, nldv=nldv):
                b.wait(v, sSC, ns0p)
                b.wait(v, sRG0, nldv)
                v.memset(R_PL[:, 0:1], NEG)
                v.tensor_tensor_scan(R_PL[:, 1:2049], R_W, sslot, NEG, ALU.add, ALU.max)
                b.inc(sVE, v.tensor_sub(R_T2, R_PL[:, 0:2048], R_PL[:, 1:2049]))
            bvec(v1)
            nv1 = b.n(sVE)
            nprev1 = nv1

            def s1(s, j=j, nv1=nv1):
                b.wait(s, sVE, nv1)
                b.inc(sSC, s.activation(R_T2, R_T2, ACT.Exp, bias=w8[:, j:j + 1]))
            bscal(s1)
            ns1 = b.n(sSC)

            def v2(v, sslot=sslot, ns1=ns1):
                b.wait(v, sSC, ns1)
                b.inc(sVE, v.tensor_sub(R_T3, sslot, R_PL[:, 1:2049]))
            bvec(v2)
            nv2 = b.n(sVE)

            def s2(s, nv2=nv2):
                b.wait(s, sVE, nv2)
                b.inc(sSC, s.activation(R_T3, R_T3, ACT.Exp))
            bscal(s2)
            ns2 = b.n(sSC)

            def v3(v, j=j, ns2=ns2):
                b.wait(v, sSC, ns2)
                v.tensor_tensor_scan(R_BL[:, 1:2049], R_T2, R_T3, 0.0, ALU.mult, ALU.add)
                v.tensor_mul(R_T3, R_T3, VBUF)
                v.tensor_tensor_scan(R_AL[:, 1:2049], R_T2, R_T3, 0.0, ALU.mult, ALU.add)
                v.tensor_copy(sS["PLe"][:, j:j + 1], R_PL[:, 2048:2049])
                v.tensor_copy(sS["ALe"][:, j:j + 1], R_AL[:, 2048:2049])
                v.tensor_copy(sS["BLe"][:, j:j + 1], R_BL[:, 2048:2049])
                v.tensor_copy(sS["PLp"][:, j:j + 1], R_PL[:, 2047:2048])
                v.tensor_copy(sS["ALp"][:, j:j + 1], R_AL[:, 2047:2048])
                b.inc(sVE, v.tensor_copy(sS["BLp"][:, j:j + 1], R_BL[:, 2047:2048]))
            bvec(v3)
            setg(("pscan", id(sslot)), sVE)
            setg(("vbuf",), sVE)
        n_p1 = b.n(sVE)
        n_k_sp = (b.n(sKs0), b.n(sKs1))

        def ag1st(sync):
            b.wait(sync, sVE, max(n_p1, n_r_lc))
            cols = [("PLe", 0), ("ALe", 8), ("BLe", 16), ("PLp", 24), ("ALp", 32),
                    ("BLp", 40), ("KL", 48), ("VL", 56), ("RL", 64)]
            for nm, lo in cols:
                b.inc(sLD, sync.dma_start(out=ag1i[:, lo:lo + 8], in_=sS[nm]), 16)
        block.sync(ag1st)
        n_ag1st = b.n(sLD)

        def ag1(g):
            b.wait(g, sLD, n_ag1st)
            b.inc(sCC, g.collective_compute(
                "AllGather", ALU.bypass, replica_groups=[list(range(NCORE))],
                ins=[ag1i.opt()], outs=[ag1o.opt()]))
        block.gpsimd(ag1)

        def ag1rd(sync):
            sync.wait_ge(b.cnt[sCC][0], 1)
            b.inc(sLD, sync.dma_start(out=agbuf,
                                      in_=ag1o.rearrange("(r p) f -> p r f", p=128)), 16)
        block.sync(ag1rd)
        n_ag1 = b.n(sLD)

        def emit_fold(dstP, dstA, dstB, chain2):
            def fv0(v):
                b.wait(v, sLD, n_ag1)
                v.memset(f["fP"], NEG)
                v.memset(f["fA"], 0.0)
                b.inc(sVE, v.memset(f["fB"], 0.0))
            bvec(fv0)
            for j in range(NCORE - 1):
                def fv1(v, j=j):
                    g = lambda lo: agbuf[:, j, lo:lo + 8]
                    if not chain2:
                        v.tensor_scalar_mul(f["f1"], tw8, MK["maskS"][:, j:j + 1])
                        v.tensor_scalar_add(f["f3"], g(0), MK["maskP"][:, j:j + 1])
                    else:
                        v.tensor_scalar_mul(f["f1"], tw8, MK["m2S"][:, j:j + 1])
                        v.tensor_scalar_mul(f["f2"], tw47, MK["m3S"][:, j:j + 1])
                        v.tensor_add(f["f1"], f["f1"], f["f2"])
                        v.tensor_scalar_add(f["f3"], g(24), MK["m3P"][:, j:j + 1])
                        v.tensor_scalar_add(f["f4"], g(0), MK["m2P"][:, j:j + 1])
                        v.tensor_max(f["f3"], f["f3"], f["f4"])
                    v.tensor_add(f["f1"], f["f1"], f["fP"])
                    v.tensor_max(f["f4"], f["f1"], f["f3"])
                    v.tensor_sub(f["f5"], f["f1"], f["f4"])
                    b.inc(sVE, v.tensor_sub(f["f6"], f["f3"], f["f4"]))
                bvec(fv1)
                nv = b.n(sVE)

                def fs1(s, nv=nv):
                    b.wait(s, sVE, nv)
                    s.activation(f["f5"], f["f5"], ACT.Exp)
                    b.inc(sSC, s.activation(f["f6"], f["f6"], ACT.Exp))
                bscal(fs1)
                ns = b.n(sSC)

                def fv2(v, j=j, ns=ns):
                    g = lambda lo: agbuf[:, j, lo:lo + 8]
                    b.wait(v, sSC, ns)
                    for (acc, loF, loP) in ((f["fA"], 8, 32), (f["fB"], 16, 40)):
                        if not chain2:
                            v.tensor_scalar_mul(f["f7"], g(loF), MK["maskS"][:, j:j + 1])
                        else:
                            v.tensor_scalar_mul(f["f7"], g(loP), MK["m3S"][:, j:j + 1])
                            v.tensor_scalar_mul(f["f2"], g(loF), MK["m2S"][:, j:j + 1])
                            v.tensor_add(f["f7"], f["f7"], f["f2"])
                        v.tensor_mul(acc, acc, f["f5"])
                        v.tensor_mul(f["f7"], f["f7"], f["f6"])
                        v.tensor_add(acc, acc, f["f7"])
                    b.inc(sVE, v.tensor_copy(f["fP"], f["f4"]))
                bvec(fv2)

            def fv3(v):
                v.tensor_copy(dstP, f["fP"])
                v.tensor_copy(dstA, f["fA"])
                b.inc(sVE, v.tensor_copy(dstB, f["fB"]))
            bvec(fv3)

        emit_fold(ppin, aain, bbin, False)
        emit_fold(sS["PLp"], sS["ALp"], sS["BLp"], True)
        n_folds = b.n(sVE)

        def bnd1(v):
            b.wait(v, sVE, n_folds)
            v.memset(f["bk8"], 0.0)
            v.memset(f["bv8"], 0.0)
            v.memset(f["br8"], 0.0)
            for j in range(NCORE - 1):
                for (acc, lo) in ((f["bk8"], 48), (f["bv8"], 56), (f["br8"], 64)):
                    v.tensor_scalar_mul(f["f4"], agbuf[:, j, lo:lo + 8],
                                        MK["m3S"][:, j:j + 1])
                    v.tensor_add(acc, acc, f["f4"])
            v.tensor_sub(f["f4"], f["bk8"], sS["PLp"])
            v.tensor_add(f["f4"], f["f4"], prm["u8"])
            b.inc(sVE, v.tensor_scalar_min(f["f4"], f["f4"], CLAMP))
        bvec(bnd1)
        nbv = b.n(sVE)

        def bnd2(s):
            b.wait(s, sVE, nbv)
            s.activation(f["f4"], f["f4"], ACT.Exp)
            b.inc(sSC, s.activation(f["br8"], f["br8"], ACT.Exp, scale=-1.0))
        bscal(bnd2)
        nbs = b.n(sSC)

        def bnd3(v):
            b.wait(v, sSC, nbs)
            v.tensor_mul(f["f5"], f["f4"], f["bv8"])
            v.tensor_add(f["f5"], sS["ALp"], f["f5"])
            v.tensor_add(f["f6"], sS["BLp"], f["f4"])
            v.reciprocal(f["f6"], f["f6"])
            v.tensor_mul(f["f5"], f["f5"], f["f6"])
            v.tensor_scalar_add(f["br8"], f["br8"], 1.0)
            v.reciprocal(f["br8"], f["br8"])
            b.inc(sVE, v.tensor_mul(f["f5"], f["f5"], f["br8"]))
        bvec(bnd3)
        n_rwkvl = b.n(sVE)

        nprev = n_folds
        for i in range(NCC):
            base = (i % 2) * 3
            rk = sl(B2, base)
            rv = sl(B2, base + 1)
            rr = sl(B2, base + 2)

            rsem = sRG0 if (i % 2 == 0) else sRG1

            def s0(s, i=i, nprev=nprev):
                b.wait(s, sVE, nprev)
                b.inc(sSC, s.add(R_W, ZT, w8[:, i:i + 1]))
            bscal(s0)
            ns0 = b.n(sSC)

            def ld(sync, i=i, rk=rk, rv=rv, rr=rr, rsem=rsem):
                b.wait(sync, sRs0, n_r_sp[0])
                b.wait(sync, sRs1, n_r_sp[1])
                b.wait(sync, sKs0, n_k_sp[0])
                b.wait(sync, sKs1, n_k_sp[1])
                b.wait(sync, sLD, n_ag1)
                gate(sync, ("ring2", i % 2))
                b.inc(rsem, sync.dma_start(out=rk, in_=ksp[:, i * 2048:(i + 1) * 2048]), 16)
                b.inc(rsem, sync.dma_start(out=rv, in_=vsp[:, i * 2048:(i + 1) * 2048]), 16)
                b.inc(rsem, sync.dma_start(out=rr, in_=rsp[:, i * 2048:(i + 1) * 2048]), 16)
            block.sync(ld)
            nld = b.n(rsem)

            def v1(v, i=i, rk=rk, nld=nld, rsem=rsem, ns0=ns0):
                b.wait(v, rsem, nld)
                b.wait(v, sVE, n_folds)
                b.wait(v, sSC, ns0)
                v.tensor_copy(R_PL[:, 0:1], ppin[:, i:i + 1])
                v.tensor_tensor_scan(R_PL[:, 1:2049], R_W, rk, ppin[:, i:i + 1],
                                     ALU.add, ALU.max)
                b.inc(sVE, v.tensor_sub(R_T2, R_PL[:, 0:2048], R_PL[:, 1:2049]))
            bvec(v1)
            nv1 = b.n(sVE)
            nprev = nv1

            def s1(s, i=i, nv1=nv1):
                b.wait(s, sVE, nv1)
                b.inc(sSC, s.activation(R_T2, R_T2, ACT.Exp, bias=w8[:, i:i + 1]))
            bscal(s1)
            ns1 = b.n(sSC)

            def v2(v, rk=rk, ns1=ns1):
                b.wait(v, sSC, ns1)
                b.inc(sVE, v.tensor_sub(R_T3, rk, R_PL[:, 1:2049]))
            bvec(v2)
            nv2 = b.n(sVE)

            def s2(s, nv2=nv2):
                b.wait(s, sVE, nv2)
                b.inc(sSC, s.activation(R_T3, R_T3, ACT.Exp))
            bscal(s2)
            ns2 = b.n(sSC)

            def v3(v, i=i, rv=rv, rk=rk, ns2=ns2):
                b.wait(v, sSC, ns2)
                v.tensor_copy(R_BL[:, 0:1], bbin[:, i:i + 1])
                v.tensor_tensor_scan(R_BL[:, 1:2049], R_T2, R_T3, bbin[:, i:i + 1],
                                     ALU.mult, ALU.add)
                v.tensor_mul(R_T3, R_T3, rv)
                v.tensor_copy(R_AL[:, 0:1], aain[:, i:i + 1])
                v.tensor_tensor_scan(R_AL[:, 1:2049], R_T2, R_T3, aain[:, i:i + 1],
                                     ALU.mult, ALU.add)
                v.tensor_sub(R_T2, rk, R_PL[:, 0:2048])
                b.inc(sVE, v.tensor_scalar(R_T2, R_T2, prm["u8"][:, i:i + 1],
                                           CLAMP, ALU.add, ALU.min))
            bvec(v3)
            nv3 = b.n(sVE)

            def s3(s, rr=rr, nv3=nv3):
                b.wait(s, sVE, nv3)
                s.activation(R_T2, R_T2, ACT.Exp)
                b.inc(sSC, s.activation(rr, rr, ACT.Exp, scale=-1.0))
            bscal(s3)
            ns3 = b.n(sSC)

            def v4(v, i=i, rv=rv, rr=rr, ns3=ns3):
                b.wait(v, sSC, ns3)
                v.tensor_mul(R_T3, R_T2, rv)
                v.tensor_add(R_T3, R_AL[:, 0:2048], R_T3)
                v.tensor_add(R_T2, R_BL[:, 0:2048], R_T2)
                v.reciprocal(R_T2, R_T2)
                v.tensor_mul(R_T3, R_T3, R_T2)
                v.tensor_scalar_add(rr, rr, 1.0)
                v.reciprocal(rr, rr)
                b.inc(sVE, v.tensor_mul(sl(B3, i), R_T3, rr))
            bvec(v4)
            marks[("ring2", i % 2)] = (sVE, b.n(sVE))
        n_p2 = b.n(sVE)

        if "rwkv" in dbg:
            tap = mktap("rwkv", (128, 16384))

            def dt1(s, tap=tap):
                b.wait(s, sVE, n_p2)
                b.inc(sSPt, s.dma_start(out=tap, in_=B3), 16)
            bscal(dt1)
            setg(("tap", id(B3)), sSPt)

        PAlo = PA[:, 0:512]
        PAhi = PA[:, 512:1024]
        dbanks = [PC, PD, PAlo, PAhi]
        zs4 = [S0[:, 0:512], S0[:, 512:1024], S1[:, 0:512], S1[:, 512:1024]]
        for cb in range(NCC):
            wslot = W0 if (cb % 2 == 0) else W1
            wsem = sW0d if (cb % 2 == 0) else sW1d

            def ld(sync, cb=cb, wslot=wslot, wsem=wsem):
                gate(sync, ("w", id(wslot)))
                b.inc(wsem, sync.dma_start(
                    out=wslot.rearrange("p (a f) -> p a f", f=128),
                    in_=woT[:, cb * 128:(cb + 1) * 128]
                        .rearrange("(a p) f -> p a f", p=128)), 16)
            block.sync(ld)
            nld = b.n(wsem)

            for tb in range(4):
                bank = dbanks[tb]
                zslot = zs4[tb]

                zsem = [sZR0, sZR1, sZR2, sZR3][tb]

                def zld(sync, cb=cb, tb=tb, zslot=zslot, zsem=zsem):
                    b.wait(sync, sRs0, n_r_sp[0])
                    b.wait(sync, sRs1, n_r_sp[1])
                    b.wait(sync, sSPb, n_z0sp)
                    gate(sync, ("zres", id(zslot)))
                    b.inc(zsem, sync.dma_start(
                        out=zslot,
                        in_=z0sp[:, cb * 2048 + tb * 512: cb * 2048 + (tb + 1) * 512]), 16)
                block.sync(zld)
                nzld = b.n(zsem)

                def mm(t, cb=cb, tb=tb, wslot=wslot, bank=bank, nld=nld, wsem=wsem):
                    if cb == 0 and tb == 0:
                        b.wait(t, sVE, n_p2)
                    b.wait(t, wsem, nld)
                    gate(t, ("bank", id(bank)))
                    wv = wslot.rearrange("p (a f) -> p a f", f=128)
                    for a in range(NCC):
                        ins = t.matmul(bank, wv[:, a, :],
                                       sl(B3, a)[:, tb * 512:(tb + 1) * 512],
                                       start=(a == 0), stop=(a == NCC - 1))
                    b.inc(sTE, ins)
                block.tensor(mm)
                nte = b.n(sTE)

                def ev(v, cb=cb, tb=tb, bank=bank, nte=nte, nzld=nzld, zslot=zslot,
                       zsem=zsem):
                    b.wait(v, sTE, nte)
                    b.wait(v, zsem, nzld)
                    b.inc(sVE, v.tensor_add(
                        sl(B1, cb)[:, tb * 512:(tb + 1) * 512],
                        bank, zslot))
                bvec(ev)
                setg(("bank", id(bank)), sVE)
                setg(("zres", id(zslot)), sVE)

            def bmm(t, cb=cb, wslot=wslot):
                b.wait(t, sVE, n_rwkvl)
                if cb == 0:
                    gate(t, ("bank", id(PE_)))
                wv = wslot.rearrange("p (a f) -> p a f", f=128)
                for a in range(NCC):
                    ins = t.matmul(PE_[:, cb:cb + 1], wv[:, a, :],
                                   f["f5"][:, a:a + 1],
                                   start=(a == 0), stop=(a == NCC - 1))
                b.inc(sTE, ins)
            block.tensor(bmm)
            setg(("w", id(wslot)), sTE)
        n_delta = b.n(sVE)
        n_dl = b.n(sTE)

        def dl1(s):
            b.wait(s, sTE, n_dl)
            b.inc(sSC, s.copy(f["f7"], PE_[:, 0:NCC]))
        bscal(dl1)
        setg(("bank", id(PE_)), sSC)
        ndl1 = b.n(sSC)

        def dl2(v):
            b.wait(v, sSC, ndl1)
            b.inc(sVE, v.tensor_add(f["f7"], f["f7"], z0b))
        bvec(dl2)
        ndl2 = b.n(sVE)

        def dl3(s):
            b.wait(s, sVE, ndl2)
            b.inc(sSC, s.activation(f["f6"], f["f7"], ACT.Square))
        bscal(dl3)
        ndl3 = b.n(sSC)

        def dl4(t):
            b.wait(t, sSC, ndl3)
            gate(t, ("bank", id(PF)))
            t.matmul(PF[0:1, 0:8], ONESC, f["f7"], start=True, stop=True)
            b.inc(sTE, t.matmul(PF[0:1, 8:16], ONESC, f["f6"], start=True, stop=True))
        block.tensor(dl4)
        ndl4 = b.n(sTE)

        def dl5(v):
            b.wait(v, sTE, ndl4)
            v.reduce_sum(stat1[:, 0:1], PF[0:1, 0:8], AX.X)
            v.reduce_sum(stat1[:, 1:2], PF[0:1, 8:16], AX.X)
            v.tensor_scalar_mul(stat1[:, 2:3], stat1[:, 0:1], 1.0 / C)
            v.tensor_scalar_mul(stat1[:, 3:4], stat1[:, 1:2], 1.0 / C)
            v.scalar_tensor_tensor(stat1[:, 4:5], stat1[:, 2:3], stat1[:, 2:3],
                                   stat1[:, 3:4], ALU.mult, ALU.subtract)
            b.inc(sVE, v.tensor_scalar(stat1[:, 4:5], stat1[:, 4:5], -1.0, EPS,
                                       ALU.mult, ALU.add))
        bvec(dl5)
        ndl5 = b.n(sVE)

        def dl6(s):
            b.wait(s, sVE, ndl5)
            b.inc(sSC, s.activation(stat1[:, 5:6], stat1[:, 4:5], ACT.Sqrt))
        bscal(dl6)
        ndl6 = b.n(sSC)

        def dl7(v):
            b.wait(v, sSC, ndl6)
            v.memset(ONESR, 1.0)
            v.reciprocal(stat1[:, 6:7], stat1[:, 5:6])
            b.inc(sVE, v.tensor_copy(stat1[:, 7:8], stat1[:, 2:3]))
        bvec(dl7)
        ndl7 = b.n(sVE)

        def dl8(t):
            b.wait(t, sVE, ndl7)
            b.inc(sTE, t.matmul(PF[:, 16:18], ONESR, stat1[:, 6:8],
                                start=True, stop=True))
        block.tensor(dl8)
        ndl8 = b.n(sTE)

        def dl9(v):
            b.wait(v, sTE, ndl8)
            v.tensor_copy(f["bm2x"][:, 0:1], PF[:, 17:18])
            v.tensor_copy(f["bm2x"][:, 1:2], PF[:, 16:17])
            v.tensor_scalar(xcb, f["f7"], f["bm2x"][:, 0:1], f["bm2x"][:, 1:2],
                            ALU.subtract, ALU.mult)
            v.tensor_mul(xcb, xcb, prm["ln2g"])
            v.tensor_add(xcb, xcb, prm["ln2b"])
            b.inc(sVE, v.tensor_scalar_mul(xcb, xcb, C0M[:, 0:1]))
        bvec(dl9)
        setg(("bank", id(PF)), sVE)
        n_xcb = b.n(sVE)

        if "xatt" in dbg:
            tap = mktap("xatt", (128, 16384))

            def dt2(s, tap=tap):
                b.wait(s, sVE, n_delta)
                b.inc(sSPt, s.dma_start(out=tap, in_=B1), 16)
            bscal(dt2)
            setg(("tap", id(B1)), sSPt)

        for tt in range(NT):
            for bg in range(2):
                srcs = [B1[:, cb2 * 2048 + tt * 128: cb2 * 2048 + (tt + 1) * 128]
                        for cb2 in range(bg * 4, bg * 4 + 4)]
                dsts = [B2[:, tt * 1024 + bg * 512: tt * 1024 + (bg + 1) * 512]]
                emit_tp(srcs, dsts, wait_list=[(sVE, n_delta)])
        n_xatm = b.n(sSC)

        emit_ln(lambda i: tmt(B2, i), NT, wait_list=[(sSC, n_xatm)])
        n_ln2 = b.n(sVE)

        for i in range(NCC):
            srcs = [tmt(B2, tt)[:, i * 128:(i + 1) * 128] for tt in range(NT)]
            dsts = [B3[:, i * 2048 + q * 512: i * 2048 + (q + 1) * 512]
                    for q in range(4)]
            wl = [(sVE, max(n_ln2, n_p2))]
            if b.n(sSPt):
                wl.append((sSPt, b.n(sSPt)))
            emit_tp(srcs, dsts, wait_list=wl)
        n_z2 = b.n(sSC)

        def cm_xc(v):
            b.wait(v, sSC, n_z2)
            for i in range(NCC):
                ins = v.tensor_scalar(sl(B3, i), sl(B3, i), prm["ln2g"][:, i:i + 1],
                                      prm["ln2b"][:, i:i + 1], ALU.mult, ALU.add)
            b.inc(sVE, ins)
        bvec(cm_xc)
        n_xc = b.n(sVE)

        def cm_spx(s):
            b.wait(s, sSC, n_xatm)
            tapwait(s)
            b.inc(sSPa, s.dma_start(out=xasp, in_=B1), 16)
        bscal(cm_spx)
        n_xasp = b.n(sSPa)

        def mix2(dstbuf, cname, wve, tag):
            def fn(v):
                b.wait(v, sVE, wve)
                for i in range(NCC):
                    xc_i = sl(B3, i)
                    v.tensor_sub(S0[:, 1:2048], xc_i[:, 1:2048], xc_i[:, 0:2047])
                    v.tensor_sub(S0[:, 0:1], xc_i[:, 0:1], xcb[:, i:i + 1])
                    v.scalar_tensor_tensor(sl(dstbuf, i)[:, 1:2048], S0[:, 1:2048],
                                           prm[cname][:, i:i + 1], xc_i[:, 0:2047],
                                           ALU.mult, ALU.add)
                    ins = v.scalar_tensor_tensor(sl(dstbuf, i)[:, 0:1], S0[:, 0:1],
                                                 prm[cname][:, i:i + 1],
                                                 xcb[:, i:i + 1], ALU.mult, ALU.add)
                b.inc(sVE, ins)
            bvec(fn)

        mix2(B2, "cmr", max(n_xc, n_xcb), "xr2")
        n_xr2 = b.n(sVE)

        for j in range(NCC):
            wslot = W0 if (j % 2 == 0) else W1
            wsem = sW0d if (j % 2 == 0) else sW1d

            def ld(sync, j=j, wslot=wslot, wsem=wsem):
                gate(sync, ("w", id(wslot)))
                b.inc(wsem, sync.dma_start(
                    out=wslot.rearrange("p (a f) -> p a f", f=128),
                    in_=fwrT[:, j * 128:(j + 1) * 128]
                        .rearrange("(a p) f -> p a f", p=128)), 16)
            block.sync(ld)
            nld = b.n(wsem)
            for tb in range(4):
                bank = PC if (tb % 2 == 0) else PD

                def mm(t, tb=tb, wslot=wslot, bank=bank, nld=nld, wsem=wsem):
                    b.wait(t, wsem, nld)
                    b.wait(t, sVE, n_xr2)
                    b.wait(t, sSPa, n_xasp)
                    gate(t, ("bank", id(bank)))
                    wv = wslot.rearrange("p (a f) -> p a f", f=128)
                    for ci in range(NCC):
                        ins = t.matmul(bank[:, 0:512], wv[:, ci, :],
                                       sl(B2, ci)[:, tb * 512:(tb + 1) * 512],
                                       start=(ci == 0), stop=(ci == NCC - 1))
                    b.inc(sTE, ins)
                block.tensor(mm)
                setg(("w", id(wslot)), sTE)
                nte = b.n(sTE)

                def ev(s, j=j, tb=tb, bank=bank, nte=nte):
                    b.wait(s, sTE, nte)
                    tapwait(s)
                    b.inc(sSC, s.copy(sl(B1, j)[:, tb * 512:(tb + 1) * 512],
                                      bank[:, 0:512]))
                bscal(ev)
                setg(("bank", id(bank)), sSC)
        n_r2 = b.n(sSC)

        for i in range(NCC):
            def sg1(s, i=i):
                b.wait(s, sSC, n_r2)
                b.inc(sSC, s.activation(sl(B1, i), sl(B1, i), ACT.Exp, scale=-1.0))
            bscal(sg1)
            nsg = b.n(sSC)

            def sg2(v, i=i, nsg=nsg):
                b.wait(v, sSC, nsg)
                v.tensor_scalar_add(sl(B1, i), sl(B1, i), 1.0)
                b.inc(sVE, v.reciprocal(sl(B1, i), sl(B1, i)))
            bvec(sg2)
        n_sig2 = b.n(sVE)

        def cm_spsg(s):
            b.wait(s, sVE, n_sig2)
            b.inc(sSPa, s.dma_start(out=sgsp, in_=B1), 16)
        bscal(cm_spsg)
        n_sgsp = b.n(sSPa)

        def cm_gate_xk2(v):
            b.wait(v, sSC, n_r2)
            b.inc(sVE, v.engine_nop())
        bvec(cm_gate_xk2)
        mix2(B2, "cmk", b.n(sVE), "xk2")
        n_xk2 = b.n(sVE)

        for g8 in range(4):
            def ldg(sync, g8=g8):
                b.wait(sync, sSPa, n_sgsp)
                gate(sync, ("cmixw",))
                for hh in range(8):
                    h = g8 * 8 + hh
                    b.inc(sCW, sync.dma_start(
                        out=B1[:, hh * 1024:(hh + 1) * 1024]
                            .rearrange("p (a f) -> p a f", f=128),
                        in_=fwkT[:, h * 128:(h + 1) * 128]
                            .rearrange("(a p) f -> p a f", p=128)), 16)
                    b.inc(sCW, sync.dma_start(
                        out=B1[:, 8192 + hh * 1024: 8192 + (hh + 1) * 1024],
                        in_=fwvT[h * 128:(h + 1) * 128, :]), 16)
            block.sync(ldg)
            nldg = b.n(sCW)

            for tb in range(4):
                kvbank = [PA[:, 0:512], PA[:, 512:1024], PB[:, 0:512], PB[:, 512:1024]]
                mmv0_q = []
                for hh in range(8):
                    kbank = PC if (hh % 2 == 0) else PD
                    kkslot = (S0 if hh < 4 else S1)[:, (hh % 4) * 512:(hh % 4 + 1) * 512]

                    def mmk(t, g8=g8, tb=tb, hh=hh, kbank=kbank, nldg=nldg):
                        b.wait(t, sCW, nldg)
                        b.wait(t, sVE, n_xk2)
                        gate(t, ("bank", id(kbank)))
                        wv = B1[:, hh * 1024:(hh + 1) * 1024] \
                            .rearrange("p (a f) -> p a f", f=128)
                        for ci in range(NCC):
                            ins = t.matmul(kbank[:, 0:512], wv[:, ci, :],
                                           sl(B2, ci)[:, tb * 512:(tb + 1) * 512],
                                           start=(ci == 0), stop=(ci == NCC - 1))
                        b.inc(sTE, ins)
                    block.tensor(mmk)
                    ntk = b.n(sTE)

                    def evk(s, kbank=kbank, kkslot=kkslot, ntk=ntk):
                        b.wait(s, sTE, ntk)
                        gate(s, ("kkS",))
                        b.inc(sSC, s.activation(kkslot, kbank[:, 0:512], ACT.Relu))
                    bscal(evk)
                    setg(("bank", id(kbank)), sSC)
                    nkr = b.n(sSC)

                    def evk2(v, kkslot=kkslot, nkr=nkr):
                        b.wait(v, sSC, nkr)
                        b.inc(sVE, v.tensor_mul(kkslot, kkslot, kkslot))
                    bvec(evk2)
                    nkkh = b.n(sVE)

                    def mmv0(t, hh=hh, nkkh=nkkh, g8=g8, tb=tb):
                        b.wait(t, sVE, nkkh)
                        if hh == 0:
                            gate(t, ("kvev",))
                        kks = (S0 if hh < 4 else S1)[:, (hh % 4) * 512:(hh % 4 + 1) * 512]
                        for ci4 in range(4):
                            ci = ci4
                            ins = t.matmul(
                                kvbank[ci4],
                                B1[:, 8192 + hh * 1024 + ci * 128:
                                   8192 + hh * 1024 + (ci + 1) * 128],
                                kks, start=(hh == 0), stop=(hh == 7))
                        b.inc(sTE, ins)
                    mmv0_q.append(mmv0)
                    if hh > 0:
                        block.tensor(mmv0_q[hh - 1])
                block.tensor(mmv0_q[7])
                nkk = b.n(sVE)
                ntv0 = b.n(sTE)

                def evv0(z, g8=g8, tb=tb, ntv0=ntv0):
                    b.wait(z, sTE, ntv0)
                    for ci4 in range(4):
                        src_ = kvbank[ci4]
                        dst = sl(B3, ci4)[:, tb * 512:(tb + 1) * 512]
                        if g8 == 0:
                            ins = z.copy(dst, src_)
                        else:
                            ins = z.tensor_add(dst, dst, src_)
                    if g8 == 0:
                        b.inc(sSC, ins)
                    else:
                        b.inc(sVE, ins)
                if g8 == 0:
                    bscal(evv0)
                    nev0 = (sSC, b.n(sSC))
                else:
                    bvec(evv0)
                    nev0 = (sVE, b.n(sVE))

                def mmv1(t, g8=g8, tb=tb, nkk=nkk, nev0=nev0):
                    b.wait(t, sVE, nkk)
                    b.wait(t, nev0[0], nev0[1])
                    for hh in range(8):
                        kks = (S0 if hh < 4 else S1)[:, (hh % 4) * 512:(hh % 4 + 1) * 512]
                        for ci4 in range(4):
                            ci = 4 + ci4
                            ins = t.matmul(
                                kvbank[ci4],
                                B1[:, 8192 + hh * 1024 + ci * 128:
                                   8192 + hh * 1024 + (ci + 1) * 128],
                                kks, start=(hh == 0), stop=(hh == 7))
                    b.inc(sTE, ins)
                block.tensor(mmv1)
                ntv1 = b.n(sTE)

                def evv1(z, g8=g8, tb=tb, ntv1=ntv1):
                    b.wait(z, sTE, ntv1)
                    for ci4 in range(4):
                        ci = 4 + ci4
                        src_ = kvbank[ci4]
                        dst = sl(B3, ci)[:, tb * 512:(tb + 1) * 512]
                        if g8 == 0:
                            ins = z.copy(dst, src_)
                        else:
                            ins = z.tensor_add(dst, dst, src_)
                    if g8 == 0:
                        b.inc(sSC, ins)
                    else:
                        b.inc(sVE, ins)
                if g8 == 0:
                    bscal(evv1)
                    marks[("kvev",)] = (sSC, b.n(sSC))
                else:
                    bvec(evv1)
                    marks[("kvev",)] = (sVE, b.n(sVE))
                setg(("kkS",), sTE)
            setg(("cmixw",), sTE)
        n_cmix_te = b.n(sTE)
        n_kv_sc = b.n(sSC)
        n_kv_ve = b.n(sVE)

        for ci in range(NCC):
            zsem = sZR0 if (ci % 2 == 0) else sZR1

            def ldc(sync, ci=ci, zsem=zsem):
                b.wait(sync, sSPa, n_sgsp)
                b.wait(sync, sTE, n_cmix_te)
                gate(sync, ("cres", ci % 2))
                b.inc(zsem, sync.dma_start(
                    out=(S0 if ci % 2 == 0 else S1),
                    in_=sgsp[:, ci * 2048:(ci + 1) * 2048]), 16)
            block.sync(ldc)
            nld1 = b.n(zsem)

            def cmb1(v, ci=ci, nld1=nld1, zsem=zsem):
                b.wait(v, zsem, nld1)
                b.wait(v, sVE, n_kv_ve)
                b.wait(v, sSC, n_kv_sc)
                b.inc(sVE, v.tensor_mul(sl(B3, ci), sl(B3, ci),
                                        (S0 if ci % 2 == 0 else S1)))
            bvec(cmb1)
            nve1 = b.n(sVE)

            def ldc2(sync, ci=ci, nve1=nve1, zsem=zsem):
                b.wait(sync, sVE, nve1)
                b.inc(zsem, sync.dma_start(
                    out=(S0 if ci % 2 == 0 else S1),
                    in_=xasp[:, ci * 2048:(ci + 1) * 2048]), 16)
            block.sync(ldc2)
            nld2 = b.n(zsem)

            def cmb2(v, ci=ci, nld2=nld2, zsem=zsem):
                b.wait(v, zsem, nld2)
                b.inc(sVE, v.tensor_add(sl(B3, ci), sl(B3, ci),
                                        (S0 if ci % 2 == 0 else S1)))
            bvec(cmb2)
            marks[("cres", ci % 2)] = (sVE, b.n(sVE))
        n_bo = b.n(sVE)

        if "bo" in dbg:
            tap = mktap("bo", (128, 16384))

            def dt3(s, tap=tap):
                b.wait(s, sVE, n_bo)
                b.inc(sSPt, s.dma_start(out=tap, in_=B3), 16)
            bscal(dt3)
            setg(("tap", id(B3)), sSPt)

        for tt in range(NT):
            for bg in range(2):
                srcs = [B3[:, cb2 * 2048 + tt * 128: cb2 * 2048 + (tt + 1) * 128]
                        for cb2 in range(bg * 4, bg * 4 + 4)]
                dsts = [B2[:, tt * 1024 + bg * 512: tt * 1024 + (bg + 1) * 512]]
                emit_tp(srcs, dsts, wait_list=[(sVE, n_bo)])
        n_botm = b.n(sSC)

        emit_ln(lambda i: tmt(B2, i), NT, wait_list=[(sSC, n_botm)])
        n_lnb = b.n(sVE)

        for tt in range(NT):
            zsem = sZR0 if (tt % 2 == 0) else sZR1

            def ldx(sync, tt=tt, zsem=zsem):
                b.wait(sync, sSC, n_botm)
                gate(sync, ("xres", tt % 2))
                b.inc(zsem, sync.dma_start(
                    out=(S0 if tt % 2 == 0 else S1)[:, 0:1024],
                    in_=x_sl[1 + tt * 128:1 + (tt + 1) * 128, :]), 16)
            block.sync(ldx)
            nldx = b.n(zsem)

            def x2t(v, tt=tt, nldx=nldx, zsem=zsem):
                b.wait(v, zsem, nldx)
                b.wait(v, sVE, n_lnb)
                tapwait(v)
                b.inc(sVE, v.tensor_add(tmt(B3, tt), tmt(B2, tt),
                                        (S0 if tt % 2 == 0 else S1)[:, 0:1024]))
            bvec(x2t)
            marks[("xres", tt % 2)] = (sVE, b.n(sVE))
        n_x2 = b.n(sVE)

        def spx2(s):
            b.wait(s, sVE, n_x2)
            tapwait(s)
            b.inc(sSPc3, s.dma_start(out=x2sp, in_=B3), 16)
        bscal(spx2)
        n_x2sp = b.n(sSPc3)

        emit_ln(lambda i: tmt(B3, i), NT, out_tile=lambda i: tmt(B2, i))
        n_y = b.n(sVE)

        if "y" in dbg:
            tap = mktap("y", (128, 16384))

            def dt4(s, tap=tap):
                b.wait(s, sVE, n_y)
                b.inc(sSPt, s.dma_start(out=tap, in_=B2), 16)
            bscal(dt4)
            setg(("tap", id(B2)), sSPt)

        def ldphi(sync):
            b.wait(sync, sSPc3, n_x2sp)
            b.wait(sync, sVE, n_y)
            gate(sync, ("cmixw",))
            b.inc(sLD, sync.dma_start(
                out=B1.rearrange("p (a f) -> p a f", f=2048),
                in_=phiT_d.rearrange("(a p) f -> p a f", p=128)), 16)
            b.inc(sLD, sync.dma_start(out=PHIB, in_=phib_d), 16)
            b.inc(sLD, sync.dma_start(out=MDs, in_=md_d), 16)
            b.inc(sLD, sync.dma_start(out=MCs, in_=mc_d), 16)
        block.sync(ldphi)
        n_ldphi = b.n(sLD)

        E_ODD = YT
        DC_ODD = XT
        CSTG2 = XTt

        def phi_banks(tt):
            if tt % 2 == 0:
                return [PA[:, 0:512], PA[:, 512:1024], PB[:, 0:512], PB[:, 512:1024]]
            return [PC[:, 0:512], PD[:, 0:512], PE_[:, 0:512], PF[:, 0:512]]

        def phi_bufs(tt):
            if tt % 2 == 0:
                return (S0, S1, CSTG)
            return (E_ODD, DC_ODD, CSTG2)

        def phi_stageA(tt):
            bks = phi_banks(tt)
            for bb_ in range(2):
                def ty(t, tt=tt, bb_=bb_, bks=bks):
                    b.wait(t, sVE, n_y)
                    gate(t, ("pbank", tt % 2))
                    for jj in range(4):
                        dd = bb_ * 4 + jj
                        ins = t.transpose(bks[bb_][:, jj * 128:(jj + 1) * 128],
                                          tmt(B2, tt)[:, dd * 128:(dd + 1) * 128],
                                          IDN)
                    b.inc(sTE, ins)
                block.tensor(ty)
                nty = b.n(sTE)

                def evy_(s, bb_=bb_, nty=nty, bks=bks, tt=tt):
                    b.wait(s, sTE, nty)
                    if bb_ == 0:
                        gate(s, ("w", id(W0)))
                    b.inc(sSC, s.copy(W0[:, bb_ * 512:(bb_ + 1) * 512], bks[bb_]))
                bscal(evy_)
            setg(("w", id(W0)), sSC)
            n_yfm = b.n(sSC)
            for q4 in range(4):
                def mm(t, q4=q4, n_yfm=n_yfm, tt=tt, bks=bks):
                    b.wait(t, sLD, n_ldphi)
                    b.wait(t, sSC, n_yfm)
                    wv = W0.rearrange("p (a f) -> p a f", f=128)
                    for dd in range(NCC):
                        ins = t.matmul(bks[q4], wv[:, dd, :],
                                       B1[:, dd * 2048 + q4 * 512:
                                          dd * 2048 + (q4 + 1) * 512],
                                       start=(dd == 0), stop=(dd == NCC - 1))
                    b.inc(sTE, ins)
                block.tensor(mm)
            setg(("w", id(W0)), sTE)
            marks[("philg", tt)] = (sTE, b.n(sTE))

        def phi_stageB1(tt):
            bks = phi_banks(tt)
            Ebuf, Dbuf, Cstg = phi_bufs(tt)
            n_lg = marks[("philg", tt)][1]

            def sm1(v, n_lg=n_lg, bks=bks, Ebuf=Ebuf, tt=tt):
                b.wait(v, sTE, n_lg)
                gate(v, ("Ebuf", tt % 2))
                for q4 in range(4):
                    v.tensor_add(Ebuf[:, q4 * 512:(q4 + 1) * 512], bks[q4],
                                 PHIB[:, q4 * 512:(q4 + 1) * 512])
                v.reduce_max(f["f5"][:, 0:1], Ebuf, AX.X)
                b.inc(sVE, v.tensor_scalar_mul(f["f5"][:, 0:1], f["f5"][:, 0:1], -1.0))
            bvec(sm1)
            nsm1 = b.n(sVE)

            def sm2(s, nsm1=nsm1, Ebuf=Ebuf):
                b.wait(s, sVE, nsm1)
                b.inc(sSC, s.activation(Ebuf, Ebuf, ACT.Exp, bias=f["f5"][:, 0:1]))
            bscal(sm2)

        def phi_stageB2(tt):
            bks = phi_banks(tt)
            Ebuf, Dbuf, Cstg = phi_bufs(tt)
            nsm2 = b.n(sSC)

            def sm3(v, nsm2=nsm2, Ebuf=Ebuf):
                b.wait(v, sSC, nsm2)
                v.tensor_reduce(sD, Ebuf.rearrange("p (n s) -> p n s", s=32),
                                AX.X, ALU.add)
                v.tensor_reduce(sCt, Ebuf.rearrange("p (n s) -> p s n", s=32),
                                AX.X, ALU.add)
                v.reciprocal(sD, sD)
                b.inc(sVE, v.reciprocal(sCt, sCt))
            bvec(sm3)
            nsm3 = b.n(sVE)

            def sm4(t, nsm3=nsm3, bks=bks):
                b.wait(t, sVE, nsm3)
                t.transpose(bks[0][0:64, 0:128], sD, IDN)
                b.inc(sTE, t.transpose(bks[0][0:32, 128:256], sCt, IDN))
            block.tensor(sm4)
            nsm4 = b.n(sTE)

            def sm5(s, nsm4=nsm4, bks=bks):
                b.wait(s, sTE, nsm4)
                s.copy(rDT, bks[0][0:64, 0:128])
                b.inc(sSC, s.copy(rCT, bks[0][0:32, 128:256]))
            bscal(sm5)
            nsm5 = b.n(sSC)

            for (which, rT, Ms) in (("D", rDT, MDs), ("C", rCT, MCs)):
                for q4 in range(4):
                    def mmb(t, rT=rT, Ms=Ms, q4=q4, nsm5=nsm5, which=which, tt=tt,
                            bks=bks):
                        b.wait(t, sSC, nsm5)
                        if which == "C" and q4 == 0:
                            gate(t, ("dcmul", tt % 2))
                        b.inc(sTE, t.matmul(bks[q4], rT,
                                            Ms[:, q4 * 512:(q4 + 1) * 512],
                                            start=True, stop=True))
                    block.tensor(mmb)
                nmb = b.n(sTE)

                def mlt(v, nmb=nmb, which=which, tt=tt, bks=bks, Ebuf=Ebuf,
                        Dbuf=Dbuf):
                    b.wait(v, sTE, nmb)
                    if which == "D":
                        gate(v, ("Dbuf", tt % 2))
                    else:
                        gate(v, ("Ddma", tt % 2))
                    for q4 in range(4):
                        ins = v.tensor_mul(Dbuf[:, q4 * 512:(q4 + 1) * 512],
                                           Ebuf[:, q4 * 512:(q4 + 1) * 512],
                                           bks[q4])
                    b.inc(sVE, ins)
                bvec(mlt)
                nmlt = b.n(sVE)
                if which == "D":
                    marks[("dcmul", tt % 2)] = (sVE, nmlt)

                    def smd(s, tt=tt, nmlt=nmlt, Dbuf=Dbuf):
                        b.wait(s, sVE, nmlt)
                        b.inc(sDsp, s.dma_start(
                            out=dsp[tt * 128:(tt + 1) * 128, :], in_=Dbuf), 16)
                    bscal(smd)
                    marks[("Ddma", tt % 2)] = (sDsp, b.n(sDsp))
                else:
                    marks[("Ebuf", tt % 2)] = (sVE, nmlt)

            nCmul = b.n(sVE)
            for bb_ in range(4):
                bk = bks[bb_ % 2]

                def tc(t, tt=tt, bb_=bb_, bk=bk, nCmul=nCmul, Dbuf=Dbuf):
                    b.wait(t, sVE, nCmul)
                    if bb_ >= 2:
                        gate(t, ("ctev", tt % 2, bb_ % 2))
                    for jj in range(4):
                        np_ = bb_ * 4 + jj
                        ins = t.transpose(bk[:, jj * 128:(jj + 1) * 128],
                                          Dbuf[:, np_ * 128:(np_ + 1) * 128], IDN)
                    b.inc(sTE, ins)
                block.tensor(tc)
                ntc = b.n(sTE)

                def evc(s, tt=tt, bb_=bb_, bk=bk, ntc=ntc, Cstg=Cstg):
                    b.wait(s, sTE, ntc)
                    if bb_ == 0:
                        gate(s, ("cstg", tt % 2))
                    b.inc(sSC, s.copy(Cstg[:, bb_ * 512:(bb_ + 1) * 512], bk))
                bscal(evc)
                marks[("ctev", tt % 2, bb_ % 2)] = (sSC, b.n(sSC))
            marks[("Dbuf", tt % 2)] = (sSC, b.n(sSC))
            marks[("pbank", tt % 2)] = (sSC, b.n(sSC))
            nctp = b.n(sSC)

            csem = sCs0 if (tt % 2 == 0) else sCs1

            def smc(s, tt=tt, nctp=nctp, Cstg=Cstg, csem=csem):
                b.wait(s, sSC, nctp)
                b.inc(csem, s.dma_start(
                    out=csp.rearrange("(a p) t -> p a t", p=128)[:, :, tt * 128:(tt + 1) * 128],
                    in_=Cstg.rearrange("p (a f) -> p a f", f=128)), 16)
            bscal(smc)
            marks[("cstg", tt % 2)] = (csem, b.n(csem))

        phi_stageA(0)
        for tt in range(NT):
            phi_stageB1(tt)
            if tt + 1 < NT:
                phi_stageA(tt + 1)
            phi_stageB2(tt)

        n_phi = b.n(sDsp)
        n_phic = (b.n(sCs0), b.n(sCs1))
        n_phi_ve = b.n(sVE)
        n_phi_te = b.n(sTE)

        for p16 in range(16):
            dslot = S0 if (p16 % 2 == 0) else S1
            zsem = sZR0 if (p16 % 2 == 0) else sZR1

            def ldd(sync, p16=p16, dslot=dslot, zsem=zsem):
                b.wait(sync, sDsp, n_phi)
                b.wait(sync, sVE, n_phi_ve)
                b.wait(sync, sTE, n_phi_te)
                gate(sync, ("dld", p16 % 2))
                b.inc(zsem, sync.dma_start(
                    out=dslot.rearrange("p (a f) -> p a f", f=128),
                    in_=dsp[:, p16 * 128:(p16 + 1) * 128]
                        .rearrange("(a p) f -> p a f", p=128)), 16)
            block.sync(ldd)
            nldd = b.n(zsem)
            pbank = PA if (p16 % 2 == 0) else PB

            def mmx(t, p16=p16, dslot=dslot, pbank=pbank, nldd=nldd, zsem=zsem):
                b.wait(t, zsem, nldd)
                gate(t, ("bank", id(pbank)))
                dv = dslot.rearrange("p (a f) -> p a f", f=128)
                for half in range(2):
                    for tt in range(NT):
                        ins = t.matmul(pbank[:, half * 512:(half + 1) * 512],
                                       dv[:, tt, :],
                                       tmt(B2, tt)[:, half * 512:(half + 1) * 512],
                                       start=(tt == 0), stop=(tt == NT - 1))
                b.inc(sTE, ins)
            block.tensor(mmx)
            nmx = b.n(sTE)

            def evx(s, p16=p16, pbank=pbank, nmx=nmx):
                b.wait(s, sTE, nmx)
                gate(s, ("xstg", p16 % 2))
                gate(s, ("w", id(W0 if p16 % 2 == 0 else W1)))
                b.inc(sSC, s.copy(W0 if p16 % 2 == 0 else W1, pbank))
            bscal(evx)
            setg(("bank", id(pbank)), sSC)
            setg(("w", id(W0 if p16 % 2 == 0 else W1)), sSC)
            nex = b.n(sSC)

            wsem3 = sSPw0 if (p16 % 2 == 0) else sSPw1

            def spx(s, p16=p16, nex=nex, wsem3=wsem3):
                b.wait(s, sSC, nex)
                b.inc(wsem3, s.dma_start(out=rsi[p16 * 128:(p16 + 1) * 128, :],
                                         in_=W0 if p16 % 2 == 0 else W1), 16)
            bscal(spx)
            marks[("xstg", p16 % 2)] = (wsem3, b.n(wsem3))
            marks[("w", id(W0 if p16 % 2 == 0 else W1))] = (wsem3, b.n(wsem3))
            marks[("dld", p16 % 2)] = (sTE, nmx)
        n_xp = (b.n(sSPw0), b.n(sSPw1))

        def rs1(g):
            b.wait(g, sSPw0, n_xp[0])
            b.wait(g, sSPw1, n_xp[1])
            b.inc(sCC, g.collective_compute(
                "ReduceScatter", ALU.add, replica_groups=[list(range(NCORE))],
                ins=[rsi.opt()], outs=[rso.opt()]))
        block.gpsimd(rs1)

        def rs1rd(sync):
            sync.wait_ge(b.cnt[sCC][0], 2)
            b.wait(sync, sVE, n_phi_ve)
            b.wait(sync, sTE, n_phi_te)
            b.inc(sLD, sync.dma_start(
                out=XT.rearrange("p (a f) -> p a f", f=1024),
                in_=rso.rearrange("(a p) f -> p a f", p=128)), 16)
        block.sync(rs1rd)
        n_rs = b.n(sLD)

        def xt1(v):
            b.wait(v, sLD, n_rs)
            v.tensor_scalar_max(XT, XT, -33000.0)
            b.inc(sVE, v.tensor_scalar_min(XT, XT, 65000.0))
        bvec(xt1)
        n_clip = b.n(sVE)
        emit_ln(lambda i: XT[:, i * 1024:(i + 1) * 1024], 2,
                wait_list=[(sLD, n_rs), (sVE, n_clip)])
        nxt = b.n(sVE)

        def xt2(s, nxt=nxt):
            b.wait(s, sVE, nxt)
            b.inc(sSC, s.activation(XT, XT, ACT.Tanh))
        bscal(xt2)
        n_xt = b.n(sSC)

        srcsX = [XT[:, a * 1024 + dd * 128: a * 1024 + (dd + 1) * 128]
                 for dd in range(8) for a in range(2)]
        dstsX = [XTt[:, q * 512:(q + 1) * 512] for q in range(4)]
        emit_tp(srcsX, dstsX, wait_list=[(sSC, n_xt), (sCs1, n_phic[1])])
        n_xtt = b.n(sSC)

        def ldbq(sync):
            b.wait(sync, sSPc3, n_x2sp)
            b.wait(sync, sVE, n_y)
            b.inc(sCW, sync.dma_start(out=BQ, in_=biasq_d), 16)
        block.sync(ldbq)
        n_bq = b.n(sCW)

        qb = {(0, 0): PC, (0, 1): PD, (1, 0): PE_, (1, 1): PF}
        for q in range(2):
            for e in range(4):
                n_loc = q * 4 + e
                for dd in range(NCC):
                    wslot = W0 if ((q * 32 + e * 8 + dd) % 2 == 0) else W1

                    wsem = sW0d if ((q * 32 + e * 8 + dd) % 2 == 0) else sW1d

                    def ldw(sync, n_loc=n_loc, dd=dd, wslot=wslot, wsem=wsem):
                        gate(sync, ("w", id(wslot)))
                        b.inc(wsem, sync.dma_start(
                            out=wslot,
                            in_=expw[n_loc, dd * 128:(dd + 1) * 128, :]), 16)
                    block.sync(ldw)
                    nldw = b.n(wsem)

                    def mme(t, q=q, e=e, dd=dd, wslot=wslot, nldw=nldw, wsem=wsem):
                        b.wait(t, wsem, nldw)
                        b.wait(t, sSC, n_xtt)
                        if q == 0 and e == 0 and dd == 0:
                            for bank in (PC, PD, PE_, PF):
                                gate(t, ("bank", id(bank)))
                        for Hh in range(2):
                            bank = qb[(q, Hh)]
                            ins = t.matmul(
                                bank[32 * e:32 * (e + 1), 0:512],
                                XTt[:, dd * 256 + q * 128 + e * 32:
                                    dd * 256 + q * 128 + (e + 1) * 32],
                                wslot[:, Hh * 512:(Hh + 1) * 512],
                                start=(dd == 0), stop=(dd == NCC - 1),
                                tile_position=(0, 32 * e))
                        b.inc(sTE, ins)
                    block.tensor(mme)
                    setg(("w", id(wslot)), sTE)
        n_exp = b.n(sTE)

        def eve(v):
            b.wait(v, sTE, n_exp)
            b.wait(v, sCW, n_bq)
            for q in range(2):
                for Hh in range(2):
                    ins = v.tensor_add(
                        YT[:, q * 1024 + Hh * 512: q * 1024 + (Hh + 1) * 512],
                        qb[(q, Hh)][:, 0:512],
                        BQ[:, q * 1024 + Hh * 512: q * 1024 + (Hh + 1) * 512])
            b.inc(sVE, ins)
        bvec(eve)
        for bank in (PC, PD, PE_, PF):
            setg(("bank", id(bank)), sVE)
        n_yt = b.n(sVE)

        def ag2st(sync):
            b.wait(sync, sVE, n_yt)
            b.inc(sLD, sync.dma_start(
                out=ag2i.rearrange("(a p) f -> p a f", p=128),
                in_=YT.rearrange("p (a f) -> p a f", f=1024)), 16)
        block.sync(ag2st)
        n_ag2st = b.n(sLD)

        def ag2(g):
            b.wait(g, sLD, n_ag2st)
            b.inc(sCC, g.collective_compute(
                "AllGather", ALU.bypass, replica_groups=[list(range(NCORE))],
                ins=[ag2i.opt()], outs=[ag2o.opt()]))
        block.gpsimd(ag2)

        def ag2rd(sync):
            sync.wait_ge(b.cnt[sCC][0], 3)
            b.inc(sLD, sync.dma_start(
                out=B1.rearrange("p (a f) -> p a f", f=1024),
                in_=ag2o.rearrange("(a p) f -> p a f", p=128)), 16)
        block.sync(ag2rd)
        n_ag2 = b.n(sLD)

        ybanks = [
            [PA[:, 0:512], PA[:, 512:1024], PB[:, 0:512], PB[:, 512:1024]],
            [PC[:, 0:512], PD[:, 0:512], PE_[:, 0:512], PF[:, 0:512]],
        ]
        for pg in range(8):
            bks = ybanks[pg % 2]
            for np_ in range(16):
                cslot = (S0 if (np_ % 2 == 0) else S1)
                zsem = sZR0 if (np_ % 2 == 0) else sZR1

                def ldcf(sync, pg=pg, np_=np_, cslot=cslot, zsem=zsem):
                    b.wait(sync, sLD, n_ag2)
                    b.wait(sync, sCs0, n_phic[0])
                    b.wait(sync, sCs1, n_phic[1])
                    gate(sync, ("cld", np_ % 2))
                    b.inc(zsem, sync.dma_start(
                        out=cslot[:, 0:256],
                        in_=csp[np_ * 128:(np_ + 1) * 128, pg * 256:(pg + 1) * 256]), 16)
                block.sync(ldcf)
                nldc = b.n(zsem)

                def mmy(t, pg=pg, np_=np_, cslot=cslot, nldc=nldc, zsem=zsem,
                        bks=bks):
                    b.wait(t, zsem, nldc)
                    if np_ == 0:
                        gate(t, ("ybank", pg % 2))
                    for tt2 in range(2):
                        for half in range(2):
                            ins = t.matmul(
                                bks[tt2 * 2 + half],
                                cslot[:, tt2 * 128:(tt2 + 1) * 128],
                                B1[:, np_ * 1024 + half * 512: np_ * 1024 + (half + 1) * 512],
                                start=(np_ == 0), stop=(np_ == 15))
                    b.inc(sTE, ins)
                block.tensor(mmy)
                marks[("cld", np_ % 2)] = (sTE, b.n(sTE))
            nym = b.n(sTE)

            for tt2 in range(2):
                tt = pg * 2 + tt2

                def evy(v, pg=pg, tt2=tt2, tt=tt, nym=nym, bks=bks):
                    b.wait(v, sTE, nym)
                    tapwait(v)
                    v.tensor_copy(tmt(B2, tt)[:, 0:512], bks[tt2 * 2])
                    b.inc(sVE, v.tensor_copy(tmt(B2, tt)[:, 512:1024], bks[tt2 * 2 + 1]))
                bvec(evy)
            nevy = b.n(sVE)

            emit_ln(lambda i, pg=pg: tmt(B2, pg * 2 + i), 2,
                    wait_list=[(sVE, nevy)])
            nlny = b.n(sVE)
            marks[("ybank", pg % 2)] = (sVE, nlny)
            for tt2 in range(2):
                tt = pg * 2 + tt2
                zsem2 = sRG0 if (tt % 2 == 0) else sRG1

                def ldx2(sync, tt=tt, zsem2=zsem2):
                    b.wait(sync, sSPc3, n_x2sp)
                    gate(sync, ("x2res", tt % 2))
                    b.inc(zsem2, sync.dma_start(
                        out=(S0 if tt % 2 == 0 else S1)[:, 1024:2048],
                        in_=x2sp[:, tt * 1024:(tt + 1) * 1024]), 16)
                block.sync(ldx2)
                nlx2 = b.n(zsem2)

                def addo(v, tt=tt, nlx2=nlx2, nlny=nlny, zsem2=zsem2):
                    b.wait(v, zsem2, nlx2)
                    b.wait(v, sVE, nlny)
                    b.inc(sVE, v.tensor_add(
                        tmt(B2, tt), tmt(B2, tt),
                        (S0 if tt % 2 == 0 else S1)[:, 1024:2048]))
                bvec(addo)
                marks[("x2res", tt % 2)] = (sVE, b.n(sVE))
                nado = b.n(sVE)

                def sto(s, tt=tt, nado=nado):
                    b.wait(s, sVE, nado)
                    b.inc(sSPo, s.dma_start(
                        out=o_out[tt * 128:(tt + 1) * 128, :], in_=tmt(B2, tt)), 16)
                bscal(sto)
        n_store = b.n(sSPo)
        n_taps = b.n(sSPt)

        def fin(sync):
            b.wait(sync, sSPo, n_store)
            if n_taps:
                b.wait(sync, sSPt, n_taps)
        block.sync(fin)

    return nc, d_dbg


_NC_CACHE = {}


def _get_nc(dbg=()):
    key = tuple(sorted(dbg))
    if key not in _NC_CACHE:
        _NC_CACHE[key] = build_nc(dbg)
    return _NC_CACHE[key]


def make_inmaps(inputs, dbg=()):
    x = np.ascontiguousarray(np.asarray(inputs["L0_attn_x"], dtype=np.float32)[0])

    def t(a):
        return np.ascontiguousarray(np.asarray(a, dtype=np.float32).T)

    def p8(a):
        return np.ascontiguousarray(
            np.asarray(a, dtype=np.float32).reshape(NCC, 128).T)

    base = {
        "wvT": t(inputs["L0_attn_att_Wv"]),
        "wkT": t(inputs["L0_attn_att_Wk"]),
        "wrT": t(inputs["L0_attn_att_Wr"]),
        "woT": t(inputs["L0_attn_att_Wo"]),
        "fwkT": t(inputs["L0_attn_ffn_Wk"]),
        "fwvT": t(inputs["L0_attn_ffn_Wv"]),
        "fwrT": t(inputs["L0_attn_ffn_Wr"]),
        "phiT": t(inputs["L0_ffn_phi_w"]),
        "phib": np.ascontiguousarray(np.broadcast_to(
            np.asarray(inputs["L0_ffn_phi_b"], dtype=np.float32), (128, E2))),
        "md": np.ascontiguousarray(
            np.repeat(np.eye(64, dtype=np.float32), 32, axis=1)),
        "mc": np.ascontiguousarray(
            np.tile(np.eye(32, dtype=np.float32), (1, 64))),
        "iden": np.eye(128, dtype=np.float32),
        "ln1g": p8(inputs["L0_attn_ln1_g"]),
        "ln1b": p8(inputs["L0_attn_ln1_b"]),
        "ln2g": p8(inputs["L0_attn_ln2_g"]),
        "ln2b": p8(inputs["L0_attn_ln2_b"]),
        "tmk": p8(inputs["L0_attn_time_mix_k"]),
        "tmv": p8(inputs["L0_attn_time_mix_v"]),
        "tmr": p8(inputs["L0_attn_time_mix_r"]),
        "cmk": p8(inputs["L0_attn_cm_k"]),
        "cmr": p8(inputs["L0_attn_cm_r"]),
        "decay": p8(inputs["L0_attn_time_decay"]),
        "u8": p8(inputs["L0_attn_time_first"]),
    }
    expw_f = np.asarray(inputs["L0_ffn_expert_w"], dtype=np.float32)
    expb_f = np.asarray(inputs["L0_ffn_expert_b"], dtype=np.float32)

    in_maps = []
    for m in range(NCORE):
        im = dict(base)
        xs = np.zeros((TC + 1, C), dtype=np.float32)
        xs[1:] = x[m * TC:(m + 1) * TC]
        if m > 0:
            xs[0] = x[m * TC - 1]
        im["x_sl"] = xs
        im["expw"] = np.ascontiguousarray(expw_f[m * NEC:(m + 1) * NEC])
        bq = np.zeros((128, 2 * C), dtype=np.float32)
        for q in range(2):
            for e in range(4):
                n_loc = q * 4 + e
                bq[e * 32:(e + 1) * 32, q * C:(q + 1) * C] = \
                    expb_f[m * NEC + n_loc][None, :]
        im["biasq"] = bq
        ms = np.zeros((128, NCORE), dtype=np.float32)
        mp = np.full((128, NCORE), NEG, dtype=np.float32)
        ms[:, :m] = 1.0
        mp[:, :m] = 0.0
        im["maskS"], im["maskP"] = ms, mp
        m2s = np.zeros((128, NCORE), dtype=np.float32)
        m2p = np.full((128, NCORE), NEG, dtype=np.float32)
        m2s[:, :max(m - 1, 0)] = 1.0
        m2p[:, :max(m - 1, 0)] = 0.0
        im["m2S"], im["m2P"] = m2s, m2p
        m3s = np.zeros((128, NCORE), dtype=np.float32)
        m3p = np.full((128, NCORE), NEG, dtype=np.float32)
        if m > 0:
            m3s[:, m - 1] = 1.0
            m3p[:, m - 1] = 0.0
        im["m3S"], im["m3P"] = m3s, m3p
        im["c0m"] = np.full((128, 1), 0.0 if m == 0 else 1.0, dtype=np.float32)
        in_maps.append(im)
    return in_maps


def kernel(**inputs):
    nc, _ = _get_nc()
    in_maps = make_inmaps(inputs)
    res = run_bass_kernel_spmd(nc, in_maps, core_ids=list(range(NCORE)))
    out = np.concatenate([res.results[m]["o_out"] for m in range(NCORE)], axis=0)
    return out[None, :, :].astype(np.float32)
